# version 14
# baseline (speedup 1.0000x reference)
import numpy as np
import ml_dtypes
from contextlib import ExitStack

import os
import concourse.bass as bass
import concourse.mybir as mybir

MKSKIP = set(os.environ.get("MKSKIP", "").split(","))

F32 = mybir.dt.float32
BF16 = mybir.dt.bfloat16
F16 = mybir.dt.float16
I16 = mybir.dt.int16
I32 = mybir.dt.int32
AF = mybir.ActivationFunctionType
ALU = mybir.AluOpType

H = 1024
T = 2048
NF = 8
NTC = 4
TCW = 512
F1 = 4096
NF1 = 32
IG = 2816
NIG = 22
NHP = 8
NB = 32
EPS = 1e-5

NCORES = 8


class Stream:

    def __init__(self, bld, queue, name, depth):
        self.b = bld
        self.q = queue
        self.depth = depth
        self.names = []
        for i in range(depth):
            nm = "%s%d" % (name, i)
            self.names.append(nm)
            bld.sems[nm] = bld.stack.enter_context(bld.nc.semaphore(nm))
            bld.cnt[nm] = 0
        self.i = 0

    def dma(self, out, in_):
        b = self.b
        k = self.i % self.depth
        nm = self.names[k]
        rounds = self.i // self.depth
        if rounds > 0:
            b.wait(self.q, (nm, 16 * rounds))
        b.ninstr += 1
        h = b.sems[nm]
        b.ops[self.q].append(
            lambda e, o=out, i_=in_, h=h: e.dma_start(out=o, in_=i_).then_inc(h, 16))
        b.cnt[nm] += 16
        self.i += 1
        return (nm, 16 * (rounds + 1))


class Bld:
    ENGS = ("sync", "gpsimd", "vector", "scalar", "tensor")
    ENG2SEM = {"vector": "pv", "scalar": "psc", "tensor": "pt", "gpsimd": "pg"}

    def __init__(self, nc, stack):
        self.nc = nc
        self.stack = stack
        self.ops = {e: [] for e in self.ENGS}
        self.sems = {}
        for s in ("pv", "psc", "pt", "pg"):
            self.sems[s] = stack.enter_context(nc.semaphore(s))
        self.cnt = {s: 0 for s in self.sems}
        self.lastw = {}
        self.ninstr = 0
        self.streams = {}

    def stream(self, queue, name, depth=2):
        if name not in self.streams:
            self.streams[name] = Stream(self, queue, name, depth)
        return self.streams[name]

    def do(self, eng, fn, inc=False):
        self.ninstr += 1
        if inc:
            s = self.ENG2SEM[eng]
            self.cnt[s] += 1
            tok = (s, self.cnt[s])
            h = self.sems[s]
            self.ops[eng].append(lambda e, fn=fn, h=h: fn(e).then_inc(h, 1))
            return tok
        self.ops[eng].append(fn)
        return None

    def dma(self, q, out, in_, st=None, depth=2):
        return self.stream(q, st or ("_d_" + q), depth).dma(out, in_)

    def wait(self, eng, tok):
        if tok is None:
            return
        s, v = tok
        key = (eng, s)
        if self.lastw.get(key, 0) >= v:
            return
        self.lastw[key] = v
        self.ninstr += 1
        h = self.sems[s]
        self.ops[eng].append(lambda e, h=h, v=v: e.wait_ge(h, v))

    def waits(self, eng, toks):
        for t in toks:
            self.wait(eng, t)

    def mark(self, eng):
        s = self.ENG2SEM[eng]
        self.cnt[s] += 1
        tok = (s, self.cnt[s])
        h = self.sems[s]
        self.ninstr += 1
        self.ops[eng].append(lambda e, h=h: e.sem_inc(h, 1))
        return tok

    def finalize(self):
        with self.nc.Block() as block:
            for eng in self.ENGS:
                lst = self.ops[eng]

                def make(lst):
                    def _(e):
                        for f in lst:
                            f(e)
                    return _
                getattr(block, eng)(make(lst))


def stage_inputs(inputs):
    f32 = np.float32
    x = np.asarray(inputs["L0_attn_hidden_states"], f32)

    def T_(a):
        return np.ascontiguousarray(np.asarray(a, f32).T)

    def col8(a):
        a = np.asarray(a, f32)
        return np.ascontiguousarray(a.reshape(-1, 128).T)

    shared = {
        "wq_t": T_(inputs["L0_attn_Wq"]),
        "wk_t": T_(inputs["L0_attn_Wk"]),
        "wv_t": T_(inputs["L0_attn_Wv"]),
        "bq": col8(inputs["L0_attn_bq"]),
        "bk": col8(inputs["L0_attn_bk"]),
        "bv": col8(inputs["L0_attn_bv"]),
        "bvrow": np.asarray(inputs["L0_attn_bv"], f32).reshape(1, H),
        "lng": col8(inputs["L0_ffn_ln_gamma"]),
        "lnb": col8(inputs["L0_ffn_ln_beta"]),
        "fc1q": np.ascontiguousarray(np.asarray(inputs["L0_ffn_fc1_q"], np.int32)),
        "fc1s": np.ascontiguousarray(np.repeat(np.asarray(inputs["L0_ffn_fc1_scales"], f32), 128, axis=0)),
        "fc1b": col8(inputs["L0_ffn_fc1_bias"]),
        "fc2q": np.ascontiguousarray(np.asarray(inputs["L0_ffn_fc2_q"], np.int32)),
        "fc2s": np.ascontiguousarray(np.repeat(np.asarray(inputs["L0_ffn_fc2_scales"], f32), 128, axis=0)),
        "fc2b": col8(inputs["L0_ffn_fc2_bias"]),
        "nw": col8(inputs["L1_attn_norm_w"]),
        "nb": col8(inputs["L1_attn_norm_b"]),
        "qkvw_t": T_(inputs["L1_attn_qkv_w"]),
        "bq1": col8(np.asarray(inputs["L1_attn_qkv_b"], f32)[0:H]),
        "bkrow": np.asarray(inputs["L1_attn_qkv_b"], f32)[H:2 * H].reshape(1, H),
        "bvrow1": np.asarray(inputs["L1_attn_qkv_b"], f32)[2 * H:3 * H].reshape(1, H),
        "projw_t": T_(inputs["L1_attn_proj_w"]),
        "pb": col8(inputs["L1_attn_proj_b"]),
        "clipx": np.asarray(inputs["L1_ffn_clip_x"], f32).reshape(1, 1),
        "clipd": np.asarray(inputs["L1_ffn_clip_down"], f32).reshape(1, 1),
        "wg": np.ascontiguousarray(np.asarray(inputs["L1_ffn_w_gate"], f32)),
        "wu": np.ascontiguousarray(np.asarray(inputs["L1_ffn_w_up"], f32)),
        "wd": np.ascontiguousarray(np.asarray(inputs["L1_ffn_w_down"], f32)),
        "eye": np.eye(128, dtype=ml_dtypes.bfloat16),
        "ones_c_bf": np.ones((128, 1), ml_dtypes.bfloat16),
        "ones_r_f": np.ones((1, 128), f32),
        "ones_r_bf": np.ones((1, 128), ml_dtypes.bfloat16),
    }
    in_maps = []
    for c in range(NCORES):
        d = dict(shared)
        d["x_t"] = np.ascontiguousarray(x[c].T)
        in_maps.append(d)
    return in_maps


def declare_dram(nc):
    t = {}

    def inp(name, shape, dtype=F32):
        t[name] = nc.dram_tensor(name, shape, dtype, kind="ExternalInput")

    inp("x_t", [H, T])
    for n in ("wq_t", "wk_t", "wv_t"):
        inp(n, [H, H])
    for n in ("bq", "bk", "bv", "lng", "lnb", "fc2b", "nw", "nb", "bq1", "pb"):
        inp(n, [128, 8])
    inp("bvrow", [1, H]); inp("bkrow", [1, H]); inp("bvrow1", [1, H])
    inp("fc1q", [H, F1], I32); inp("fc1s", [H, F1]); inp("fc1b", [128, 32])
    inp("fc2q", [F1, H], I32); inp("fc2s", [F1, H])
    inp("qkvw_t", [H, 3 * H]); inp("projw_t", [H, H])
    inp("clipx", [1, 1]); inp("clipd", [1, 1])
    inp("wg", [IG, H]); inp("wu", [IG, H]); inp("wd", [H, IG])
    inp("eye", [128, 128], BF16)
    inp("ones_c_bf", [128, 1], BF16)
    inp("ones_r_f", [1, 128])
    inp("ones_r_bf", [1, 128], BF16)

    t["xn_park"] = nc.dram_tensor("xn_park", [128, NF * T], BF16)
    t["x1_park"] = nc.dram_tensor("x1_park", [128, NF * T], BF16)
    t["xn2_park"] = nc.dram_tensor("xn2_park", [128, NF * T], BF16)
    t["s2_park"] = nc.dram_tensor("s2_park", [128, 4 * T], BF16)
    t["w2u_hbm"] = nc.dram_tensor("w2u_hbm", [128, NF1 * H], BF16)
    t["wgq_hbm"] = nc.dram_tensor("wgq_hbm", [128, NF * IG], BF16)
    t["wuq_hbm"] = nc.dram_tensor("wuq_hbm", [128, NF * IG], BF16)
    t["wdq_hbm"] = nc.dram_tensor("wdq_hbm", [128, NIG * H], BF16)
    t["h_park"] = nc.dram_tensor("h_park", [128, 2 * T], BF16)
    t["ccx_in"] = nc.dram_tensor("ccx_in", [1, 1], F32)
    t["ccx_out"] = nc.dram_tensor("ccx_out", [1, 1], F32)
    t["cch_in"] = nc.dram_tensor("cch_in", [1, 1], F32)
    t["cch_out"] = nc.dram_tensor("cch_out", [1, 1], F32)

    t["y_t"] = nc.dram_tensor("y_t", [H, T], F32, kind="ExternalOutput")
    return t


class MK:
    def __init__(self, nc, debug=False, upto="full"):
        self.ncores = NCORES
        self.attn_js = list(range(NB))
        self.attn_hps = NHP
        self.nc = nc
        self.debug = debug
        self.upto = upto
        self.t = declare_dram(nc)
        self.taps = {}
        if debug:
            for name, shape, dt in [
                ("tap_xn", [128, NF * T], BF16), ("tap_h1", [128, NF * T], BF16),
                ("tap_g", [128, NF * T], BF16), ("tap_x1", [128, NF * T], BF16),
                ("tap_xn2", [128, NF * T], BF16), ("tap_a", [128, NF * H], BF16),
                ("tap_s2", [128, 4 * T], BF16), ("tap_xq", [128, NF * T], BF16),
                ("tap_h", [128, NIG * T], BF16), ("tap_g2", [128, NF * T], BF16),
                ("tap_y1", [128, NF1 * TCW], BF16), ("tap_w1", [128, NF * F1], BF16),
                ("tap_qt", [128, T], BF16), ("tap_kt", [128, T], BF16),
                ("tap_vt", [128, 16 * 128], BF16), ("tap_st", [128, T], F32),
                ("tap_wgq", [128, NF * IG], BF16),
                ("tap_bias", [128, 112], F32), ("tap_sc", [128, 64], F32),
            ]:
                self.taps[name] = nc.dram_tensor(name, shape, dt, kind="ExternalOutput")

    def tap(self, name, src_ap, c0=0, cols=None, barrier=True):
        if not self.debug or name not in self.taps:
            return None
        for s in ("pv", "psc", "pt", "pg"):
            if self.b.cnt[s] > 0:
                self.b.wait("sync", (s, self.b.cnt[s]))
        dst = self.taps[name]
        ap = dst[:, c0:c0 + cols] if cols is not None else dst[:, :]
        tok = self.b.dma("sync", ap, src_ap, st="tap", depth=2)
        self.st_toks.append(tok)
        if barrier:
            for e in ("vector", "scalar", "tensor", "gpsimd"):
                self.b.wait(e, tok)
        return tok

    def rg(self, region, c0, c1):
        return region[:, c0:c1]

    def chunk(self, region, i, width=T):
        return region[:, i * width:(i + 1) * width]

    def build(self):
        nc = self.nc
        with ExitStack() as stack:
            b = Bld(nc, stack)
            self.b = b
            sb = lambda n, s, d=BF16: stack.enter_context(nc.sbuf_tensor(n, s, d))
            self.RA = sb("RA", [128, 16384])
            self.RB = sb("RB", [128, 16384])
            self.RC = sb("RC", [128, 16384])
            self.RD = sb("RD", [128, 16384])
            self.RE = sb("RE", [128, 16384])
            self.RS = sb("RS", [128, 6144])
            self.SQ = sb("SQ", [128, 2048])
            self.QS = sb("QS", [128, 2048], F32)
            self.ROWS = sb("ROWS", [128, 3072], F32)
            self.BIAS = sb("BIAS", [128, 112], F32)
            self.SWQ = sb("SWQ", [128, 256], F32)
            self.SC = sb("SC", [128, 64], F32)
            self.HMX = sb("HMX", [128, 96], F32)
            self.BQC = sb("BQC", [128, 8])
            self.EYE = sb("EYE", [128, 128])
            self.ONESC = sb("ONESC", [128, 1])
            self.ONESRF = sb("ONESRF", [1, 128], F32)
            self.ONESRB = sb("ONESRB", [1, 128])
            self.BVROW = sb("BVROW", [1, H])
            self.SCR = sb("SCR", [128, 32], F32)
            self.ps = [stack.enter_context(nc.psum_tensor("ps%d" % i, [128, 512], F32))
                       for i in range(8)]
            self.st_toks = []
            self.wq_sched = []
            self.ln_war = {}
            self.sq_war = None
            self.sq_sl_war = {}
            self.sq_mid_war = None
            self.rows_war = None
            self.bc_war = {}

            self.setup_consts()
            self.phase_A()
            if self.upto != "A":
                self.phase_ffn()
                if self.upto != "ffn":
                    self.phase_attn()
                    if self.upto not in ("l0",):
                        self.phase_l1()
            self.finish()
            b.finalize()
        return self

    def quant_piece(self, src_ap, dst_ap, recip_ap, out_signed, dep_toks=(), width=1024):
        b = self.b
        QS = self.QS
        assert width <= 1024
        v = QS[:, 0:512].bitcast(F16)[:, 0:width]
        qi = QS[:, 512:1024].bitcast(I16)[:, 0:width]
        d = QS[:, 1024:1536].bitcast(F16)[:, 0:width]
        q2 = QS[:, 1536:2048].bitcast(F16)[:, 0:width]
        q3 = QS[:, 0:512].bitcast(F16)[:, 0:width]
        for t in dep_toks:
            b.wait("vector", t)
        t1 = b.do("vector", lambda e: e.tensor_scalar(
            out=v, in0=src_ap, scalar1=recip_ap, scalar2=128.0,
            op0=ALU.mult, op1=ALU.add), inc=True)
        b.wait("vector", t1)
        t2 = b.do("vector", lambda e: e.tensor_copy(out=qi, in_=v), inc=True)
        b.wait("vector", t2)
        t3 = b.do("vector", lambda e: e.tensor_tensor(
            out=d, in0=v, in1=qi, op=ALU.subtract), inc=True)
        b.wait("vector", t3)
        t4 = b.do("vector", lambda e: e.scalar_tensor_tensor(
            out=q2, in0=d, scalar=0.5, in1=qi, op0=ALU.is_gt, op1=ALU.add), inc=True)
        b.wait("vector", t4)
        if out_signed:
            t5 = b.do("vector", lambda e: e.tensor_scalar(
                out=q3, in0=q2, scalar1=0.0, scalar2=255.0,
                op0=ALU.max, op1=ALU.min), inc=True)
            b.wait("scalar", t5)
            t6 = b.do("scalar", lambda e: e.activation(
                out=dst_ap, in_=q3, func=AF.Copy, bias=-128.0), inc=True)
            return t6
        t5 = b.do("vector", lambda e: e.tensor_scalar(
            out=dst_ap, in0=q2, scalar1=0.0, scalar2=255.0,
            op0=ALU.max, op1=ALU.min), inc=True)
        return t5

    def setup_consts(self):
        b, t = self.b, self.t
        self.ld_toks = {}
        for name, dst in [("eye", self.EYE[:, :]), ("ones_c_bf", self.ONESC[:, :]),
                          ("ones_r_bf", self.ONESRB[:, :])]:
            self.ld_toks[name] = b.dma("sync", dst, t[name][:, :])
        self.ld_toks["ones_r_f"] = b.dma("sync", self.ONESRF[:, :], t["ones_r_f"][:, :])
        BIAS = self.BIAS
        tok_bq = b.dma("sync", self.SCR[:, 0:8], t["bq"][:, :])
        tok_f2 = b.dma("sync", self.SCR[:, 8:16], t["fc2b"][:, :])
        for name, c in [("bk", 16), ("lng", 24), ("lnb", 32), ("pb", 48),
                        ("nw", 56), ("nb", 64), ("bv", 104)]:
            self.ld_toks[name] = b.dma("sync", BIAS[:, c:c + 8], t[name][:, :])
        self.ld_toks["fc1b"] = b.dma("sync", BIAS[:, 72:104], t["fc1b"][:, :])
        b.wait("vector", tok_bq)
        tq = b.do("vector", lambda e: e.tensor_scalar(
            out=BIAS[:, 8:16], in0=self.SCR[:, 0:8], scalar1=0.125, scalar2=None,
            op0=ALU.mult), inc=True)
        b.wait("vector", tok_f2)
        tf = b.do("vector", lambda e: e.tensor_scalar(
            out=BIAS[:, 40:48], in0=self.SCR[:, 8:16], scalar1=0.5, scalar2=None,
            op0=ALU.mult), inc=True)
        self.ld_toks["bq8"] = tq
        self.ld_toks["fc2bh"] = tf
        rb = self.ROWS[:, :].bitcast(BF16)
        self.r_bk = rb[32:33, 0:1024]
        self.r_bv1 = rb[32:33, 1024:2048]
        self.r_sk = rb[32:33, 2048:3072]
        self.r_sv = rb[32:33, 3072:4096]
        self.r_svp = rb[32:33, 4096:5120]
        self.ld_toks["bkrow"] = b.dma("gpsimd", self.r_bk, t["bkrow"][:, :])
        self.ld_toks["bvrow1"] = b.dma("gpsimd", self.r_bv1, t["bvrow1"][:, :])
        self.ld_toks["bvrow"] = b.dma("gpsimd", self.BVROW[:, :], t["bvrow"][:, :])
        self.ld_toks["bq1c"] = b.dma("gpsimd", self.BQC[:, :], t["bq1"][:, :])
        tms = b.do("vector", lambda e: e.memset(self.SC[:, :], 0.0), inc=True)
        b.wait("sync", tms)
        self.ld_toks["clip"] = b.dma("sync", self.SC[0:1, 0:1], t["clipx"][:, :])
        self.ld_toks["clipd"] = b.dma("sync", self.SC[0:1, 1:2], t["clipd"][:, :])

    def emit_ln(self, src_region, dst_region, gamma_col=None, beta_col=None,
                src_tok=None, dst_name="", interleave=None):
        b = self.b
        ps = self.ps
        R1 = self.ROWS
        last = None
        pending = None
        if self.sq_war is not None:
            b.wait("scalar", self.sq_war)
        if src_tok is not None:
            b.wait("scalar", src_tok)
        for tc in range(NTC):
            ts_ = slice(tc * TCW, (tc + 1) * TCW)
            sbank = ps[tc % 2]
            bbank_r = ps[4 + (tc % 2)]
            bbank_m = ps[6 + (tc % 2)]
            if self.ln_war.get(id(sbank)) is not None:
                b.wait("tensor", self.ln_war[id(sbank)])
            for f in range(NF):
                src = src_region[:, f * T + tc * TCW: f * T + (tc + 1) * TCW]
                if src_tok is not None:
                    b.wait("tensor", src_tok)
                b.do("tensor", lambda e, s=src, ps_=sbank, st=(f == 0), sp=(f == NF - 1): e.matmul(
                    ps_[0:1, :], self.ONESC[:, :], s, start=st, stop=sp))
            stats_done = None
            for f in range(NF):
                src = src_region[:, f * T + tc * TCW: f * T + (tc + 1) * TCW]
                so = 512 + (f % 2) * 1024
                if self.sq_sl_war.get(f % 2) is not None:
                    b.wait("scalar", self.sq_sl_war[f % 2])
                tsq = b.do("scalar", lambda e, s=src, so=so: e.activation(
                    out=self.SQ[:, so:so + TCW], in_=s, func=AF.Square), inc=True)
                b.wait("tensor", tsq)
                tmm = b.do("tensor", lambda e, ps_=sbank, so=so,
                           st=(f == 0), sp=(f == NF - 1): e.matmul(
                    ps_[32:33, :], self.ONESC[:, :], self.SQ[:, so:so + TCW],
                    start=st, stop=sp), inc=True)
                self.sq_sl_war[f % 2] = tmm
                stats_done = tmm
            b.wait("vector", stats_done)
            rstd, mr = R1[0:1, 0:512], R1[0:1, 512:1024]
            mu, msq = R1[0:1, 1024:1536], R1[0:1, 1536:2048]
            mu2, vep = R1[0:1, 2048:2560], R1[0:1, 2048:2560]
            var, rec = R1[0:1, 2560:3072], R1[0:1, 2560:3072]
            if self.rows_war is not None:
                b.wait("vector", self.rows_war)
            t_mu = b.do("vector", lambda e, ps_=sbank: e.tensor_scalar(
                out=mu, in0=ps_[0:1, :], scalar1=1.0 / H, scalar2=None, op0=ALU.mult), inc=True)
            t_ms = b.do("vector", lambda e, ps_=sbank: e.tensor_scalar(
                out=msq, in0=ps_[32:33, :], scalar1=1.0 / H, scalar2=None, op0=ALU.mult), inc=True)
            self.ln_war[id(sbank)] = t_ms
            b.wait("vector", t_mu)
            tv = b.do("vector", lambda e: e.tensor_tensor(out=mu2, in0=mu, in1=mu, op=ALU.mult), inc=True)
            b.wait("vector", tv)
            tv = b.do("vector", lambda e: e.tensor_tensor(out=var, in0=msq, in1=mu2, op=ALU.subtract), inc=True)
            b.wait("vector", tv)
            tv = b.do("vector", lambda e: e.tensor_scalar(
                out=vep, in0=var, scalar1=EPS, scalar2=None, op0=ALU.add), inc=True)
            b.wait("vector", tv)
            tv = b.do("vector", lambda e: e.reciprocal(out=rec, in_=vep), inc=True)
            b.wait("scalar", tv)
            tsc = b.do("scalar", lambda e: e.activation(out=rstd, in_=rec, func=AF.Sqrt), inc=True)
            b.wait("vector", tsc)
            trows = b.do("vector", lambda e: e.tensor_tensor(out=mr, in0=mu, in1=rstd, op=ALU.mult), inc=True)
            b.wait("tensor", tsc)
            b.wait("tensor", trows)
            if self.bc_war.get(tc % 2) is not None:
                b.wait("tensor", self.bc_war[tc % 2])
            b.do("tensor", lambda e, p_=bbank_r: e.matmul(
                p_[:, :], self.ONESRF[:, :], rstd, start=True, stop=True))
            tb = b.do("tensor", lambda e, p_=bbank_m: e.matmul(
                p_[:, :], self.ONESRF[:, :], mr, start=True, stop=True), inc=True)
            self.rows_war = tb

            def emit_apply(tc, tb, bbank_r, bbank_m):
                lastv = None
                for f in range(NF):
                    src = src_region[:, f * T + tc * TCW: f * T + (tc + 1) * TCW]
                    dst = dst_region[:, f * T + tc * TCW: f * T + (tc + 1) * TCW]
                    b.wait("vector", tb)
                    if self.sq_war is not None:
                        b.wait("vector", self.sq_war)
                    tv = b.do("vector", lambda e, s=src, p_=bbank_r: e.tensor_tensor(
                        out=self.SQ[:, 0:TCW], in0=s, in1=p_[:, :], op=ALU.mult), inc=True)
                    b.wait("vector", tv)
                    if gamma_col is None:
                        tv = b.do("vector", lambda e, d=dst, p_=bbank_m: e.tensor_tensor(
                            out=d, in0=self.SQ[:, 0:TCW], in1=p_[:, :], op=ALU.subtract), inc=True)
                        self.sq_war = tv
                    else:
                        if self.sq_mid_war is not None:
                            b.wait("vector", self.sq_mid_war)
                        tv = b.do("vector", lambda e, p_=bbank_m: e.tensor_tensor(
                            out=self.SQ[:, 1024:1024 + TCW], in0=self.SQ[:, 0:TCW],
                            in1=p_[:, :], op=ALU.subtract), inc=True)
                        self.sq_war = tv
                        b.wait("scalar", tv)
                        g = self.BIAS[:, gamma_col + f:gamma_col + f + 1]
                        bb = self.BIAS[:, beta_col + f:beta_col + f + 1]
                        tv = b.do("scalar", lambda e, d=dst, g=g, bb=bb: e.activation(
                            out=d, in_=self.SQ[:, 1024:1024 + TCW], func=AF.Identity,
                            scale=g, bias=bb), inc=True)
                        self.sq_mid_war = tv
                    lastv = tv
                return lastv

            if pending is not None:
                last = emit_apply(*pending)
                self.bc_war[pending[0] % 2] = last
            pending = (tc, tb, bbank_r, bbank_m)
            if interleave is not None:
                interleave()
        last = emit_apply(*pending)
        self.bc_war[pending[0] % 2] = last
        if gamma_col is not None:
            self.sq_war = last
        return last

    def phase_A(self):
        b, t = self.b, self.t
        xin = t["x_t"].ap().rearrange("(c p) n -> p c n", p=128)
        xa = self.RA[:, :].rearrange("p (c n) -> p c n", n=T)
        tok_x = b.dma("gpsimd", xa, xin, st="gx", depth=1)
        for k in ("ones_c_bf", "ones_r_f"):
            b.wait("tensor", self.ld_toks[k])
        b.wait("scalar", tok_x)
        b.wait("tensor", tok_x)
        b.wait("vector", tok_x)
        self.ffn_prelude_init()
        cb = lambda: self.ffn_prelude_step(12)
        self.tok_ln1 = self.emit_ln(self.RA, self.RB, interleave=cb)
        self.tap("tap_xn", self.RB[:, :])
        b.wait("vector", self.ld_toks["lng"])
        b.wait("vector", self.ld_toks["lnb"])
        self.tok_h1 = self.emit_ln(self.RB, self.RC, gamma_col=24, beta_col=32,
                                   interleave=cb)
        self.tap("tap_h1", self.RC[:, :])
        b.wait("sync", self.tok_ln1)
        self.tok_xnpark = b.dma("sync", t["xn_park"][:, :], self.RB[:, :], st="park", depth=2)

    def w2_bounce_init(self):
        t = self.t
        self.w2q_r = t["fc2q"].ap().rearrange("(c p) n -> c p n", p=128)
        self.w2s_r = t["fc2s"].ap().rearrange("(c p) n -> c p n", p=128)
        self.w2_k = 0
        self.w2_slot_tok = {}
        self.w2_store_toks = []

    def w2_bounce_step(self, n):
        b, t = self.b, self.t
        RD = self.RD
        for _ in range(n):
            k = self.w2_k
            if k >= NF1:
                return
            self.w2_k += 1
            s = k % 4
            base = s * 4096
            qv = RD[:, base:base + 1024]
            sv = RD[:, base + 1024:base + 2048]
            ov = RD[:, base + 2048:base + 3072]
            if s in self.w2_slot_tok:
                b.wait("gpsimd", self.w2_slot_tok[s])
            t1 = b.dma("gpsimd", qv, self.w2q_r[k], st="w2i", depth=8)
            t2 = b.dma("gpsimd", sv, self.w2s_r[k], st="w2i", depth=8)
            b.wait("scalar", t1)
            t3 = b.do("scalar", lambda e, qv=qv, ov=ov: e.activation(
                out=ov, in_=qv, func=AF.Copy, bias=-128.0), inc=True)
            b.wait("vector", t3)
            b.wait("vector", t2)
            t4 = b.do("vector", lambda e, qv=qv, sv=sv, ov=ov: e.tensor_tensor(
                out=qv, in0=ov, in1=sv, op=ALU.mult), inc=True)
            b.wait("sync", t4)
            t5 = b.dma("sync", t["w2u_hbm"][:, k * 1024:(k + 1) * 1024], qv, st="w2o", depth=4)
            self.w2_slot_tok[s] = t5
            self.w2_store_toks.append(t5)

    def wquant_init(self):
        t = self.t
        self.wq_list = []
        wg_r = t["wg"].ap().rearrange("(c p) n -> c p n", p=128)
        wu_r = t["wu"].ap().rearrange("(c p) n -> c p n", p=128)
        for ic in range(NIG):
            self.wq_list.append(("wg", wg_r[ic], t["wgq_hbm"], ic, 0, 64))
        for ic in range(NIG):
            self.wq_list.append(("wu", wu_r[ic], t["wuq_hbm"], ic, 22, 86))
        self.wq_i = 0
        self.wq_prev_qs = None
        self.wq_slot_tok = {}
        self.wq_t_ttr = None
        self.wq_t_tcs = None
        self.wq_t_tse = None
        self.wq_t_sts = None
        self.wq_store_toks = []

    def wquant_step(self, n):
        b, t = self.b, self.t
        RS, SWQ = self.RS, self.SWQ
        for _ in range(n):
            if self.wq_i >= len(self.wq_list):
                return
            name, src, dsth, ic, scol, ccol = self.wq_list[self.wq_i]
            self.wq_i += 1
            sl = self.wq_i % 2
            inb = RS[:, sl * 1024: (sl + 1) * 1024]
            wqs = RS[:, 4096:5120]
            stg = RS[:, 5120:6144]
            if ("in", sl) in self.wq_slot_tok:
                b.wait("gpsimd", self.wq_slot_tok[("in", sl)])
            if getattr(self, "fc2_last", None) is not None:
                b.wait("gpsimd", self.fc2_last)
            tin = b.dma("gpsimd", inb, src, st="wqi", depth=2)
            b.wait("vector", tin)
            rmax = SWQ[:, 240:241]; rrec = SWQ[:, 241:242]; rw127 = SWQ[:, 242:243]
            tv = b.do("vector", lambda e, i=inb, o=rmax: e.reduce_max(
                out=o, in_=i, axis=mybir.AxisListType.X, apply_absolute_value=True), inc=True)
            b.wait("vector", tv)
            tv = b.do("vector", lambda e, o=rrec, i=rmax: e.reciprocal(out=o, in_=i), inc=True)
            b.wait("vector", tv)
            tv = b.do("vector", lambda e, o=rw127, i=rrec: e.tensor_scalar(
                out=o, in0=i, scalar1=127.0, scalar2=None, op0=ALU.mult), inc=True)
            b.do("vector", lambda e, o=SWQ[:, scol + ic:scol + ic + 1], i=rmax: e.tensor_scalar(
                out=o, in0=i, scalar1=1.0 / 127.0, scalar2=None, op0=ALU.mult))
            b.wait("vector", tv)
            deps = []
            if self.wq_prev_qs is not None:
                deps.append(self.wq_prev_qs)
            if self.wq_t_ttr is not None:
                deps.append(self.wq_t_ttr)
            if self.wq_t_tcs is not None:
                deps.append(self.wq_t_tcs)
            t6 = self.quant_piece(inb, wqs, rw127, out_signed=True, dep_toks=deps)
            self.wq_prev_qs = t6
            self.wq_slot_tok[("in", sl)] = t6
            b.wait("vector", t6)
            tcs = b.do("vector", lambda e, i=wqs, o=SWQ[:, ccol + ic:ccol + ic + 1]: e.tensor_reduce(
                out=o, in_=i, axis=mybir.AxisListType.X, op=ALU.add), inc=True)
            self.wq_t_tcs = tcs
            self.wq_v_done = tcs
            ps7 = self.ps[7][:, :].bitcast(BF16)
            if self.wq_t_tse is not None:
                b.wait("tensor", self.wq_t_tse)
            b.wait("tensor", t6)
            ttr = None
            for j in range(8):
                ttr = b.do("tensor", lambda e, j=j, i=wqs, p=ps7: e.transpose(
                    p[:, j * 128:(j + 1) * 128], i[:, j * 128:(j + 1) * 128],
                    self.EYE[:, :]), inc=(j == 7))
            self.wq_t_ttr = ttr
            b.wait("scalar", ttr)
            if self.wq_t_sts is not None:
                for _t in self.wq_t_sts:
                    b.wait("scalar", _t)
            tse = b.do("scalar", lambda e, p=ps7, s=stg: e.copy(
                out=s[:, 0:1024], in_=p[:, 0:1024]), inc=True)
            self.wq_t_tse = tse
            b.wait("sync", tse)
            dsthr = dsth.ap().rearrange("p (j i) -> p j i", i=IG)
            stl = [b.dma("sync", dsthr[:, 0:8, ic * 128:(ic + 1) * 128],
                         stg[:, 0:1024].rearrange("p (j c) -> p j c", c=128),
                         st="wqo", depth=2)]
            self.wq_store_toks += stl
            self.wq_t_sts = stl[-1:]

    def ffn_prelude_init(self):
        b, t = self.b, self.t
        self.wquant_init()
        self.pr_w2q = t["fc2q"].ap().rearrange("(c p) n -> c p n", p=128)
        self.pr_w2s = t["fc2s"].ap().rearrange("(c p) n -> c p n", p=128)
        self.pr_f1q = t["fc1q"].ap().rearrange("(c p) n -> c p n", p=128)
        self.pr_f1s = t["fc1s"].ap().rearrange("(c p) n -> c p n", p=128)
        self.pr_bsl = {}
        self.pr_wsl = {}
        self.pr_qsb = {}
        self.pr_ld = {}
        self.pr_ovw = None
        self.pr_load_i = 0
        self.pr_comp_i = 0
        self.w1ptok = {}
        self.ffn_prelude_step(0)

    def _pr_load(self, p):
        b, t = self.b, self.t
        RD, RS = self.RD, self.RS
        if p < NF1:
            s = p % 4
            qv = RD[:, s * 4096: s * 4096 + 1024]
            sv = RD[:, s * 4096 + 1024: s * 4096 + 2048]
            if self.pr_bsl.get(s) is not None:
                b.wait("gpsimd", self.pr_bsl[s])
            t1 = b.dma("gpsimd", qv, self.pr_w2q[p], st="w2i", depth=8)
            t2 = b.dma("gpsimd", sv, self.pr_w2s[p], st="w2i", depth=8)
        else:
            i = p - NF1
            sl = i % 2
            hh, k = i // 8, i % 8
            inb = RS[:, sl * 2048:(sl + 1) * 2048]
            scb = self.QS[:, :].bitcast(BF16)[:, sl * 2048:(sl + 1) * 2048]
            if self.pr_wsl.get(sl) is not None:
                b.wait("gpsimd", self.pr_wsl[sl])
            t1 = b.dma("gpsimd", inb, self.pr_f1q[k][:, hh * 2048:(hh + 1) * 2048],
                       st="w1i", depth=4)
            if self.pr_qsb.get(sl) is not None:
                b.wait("gpsimd", self.pr_qsb[sl])
            t2 = b.dma("gpsimd", scb, self.pr_f1s[k][:, hh * 2048:(hh + 1) * 2048],
                       st="w1i", depth=4)
            qv, sv = inb, scb
        self.pr_ld[p] = (qv, sv, t1, t2)

    def _pr_comp(self, p):
        b, t = self.b, self.t
        RD, RE, RS = self.RD, self.RE, self.RS
        qv, sv, t1, t2 = self.pr_ld.pop(p)
        if p < NF1:
            s = p % 4
            ov = RD[:, s * 4096 + 2048: s * 4096 + 3072]
            b.wait("scalar", t1)
            t3 = b.do("scalar", lambda e, qv=qv, ov=ov: e.activation(
                out=ov, in_=qv, func=AF.Copy, bias=-128.0), inc=True)
            b.wait("vector", t3)
            b.wait("vector", t2)
            t4 = b.do("vector", lambda e, qv=qv, sv=sv, ov=ov: e.tensor_tensor(
                out=qv, in0=ov, in1=sv, op=ALU.mult), inc=True)
            b.wait("gpsimd", t4)
            t5 = b.dma("gpsimd", t["w2u_hbm"][:, p * 1024:(p + 1) * 1024], qv,
                       st="w2o", depth=4)
            self.pr_bsl[s] = t5
            self.w2_all_done = t5
        else:
            i = p - NF1
            sl = i % 2
            hh, k = i // 8, i % 8
            OV2 = RS[:, 4096:6144]
            reg = RD if k < 4 else RE
            dst = reg[:, (k % 4) * 4096 + hh * 2048: (k % 4) * 4096 + (hh + 1) * 2048]
            if k < 4:
                b.wait("vector", self.pr_bsl[k % 4])
            b.wait("scalar", t1)
            if self.pr_ovw is not None:
                b.wait("scalar", self.pr_ovw)
            t3 = b.do("scalar", lambda e, qv=qv, OV2=OV2: e.activation(
                out=OV2, in_=qv, func=AF.Copy, bias=-128.0), inc=True)
            b.wait("vector", t3)
            b.wait("vector", t2)
            t4 = b.do("vector", lambda e, dst=dst, sv=sv, OV2=OV2: e.tensor_tensor(
                out=dst, in0=OV2, in1=sv, op=ALU.mult), inc=True)
            self.pr_wsl[sl] = t3
            self.pr_qsb[sl] = t4
            self.pr_ovw = t4
            self.w1ptok[(k, hh)] = t4
            self.pr_last = t4

    def ffn_prelude_step(self, n):
        NP = NF1 + 16
        for _ in range(n + (1 if n == 0 else 0)):
            while (self.pr_load_i < NP and
                   self.pr_load_i < self.pr_comp_i + (4 if self.pr_load_i < NF1 else 2)):
                self._pr_load(self.pr_load_i)
                self.pr_load_i += 1
            if n == 0 or self.pr_comp_i >= NP:
                return
            self._pr_comp(self.pr_comp_i)
            self.pr_comp_i += 1

    def phase_ffn(self):
        b, t = self.b, self.t
        RA, RB, RC, RD, RE, RS = self.RA, self.RB, self.RC, self.RD, self.RE, self.RS
        self.ffn_prelude_step(NF1 + 16)
        last_td = self.pr_last
        for sl in (0, 1):
            self.wq_slot_tok[("in", sl)] = self.pr_wsl[1]
        self.wq_prev_qs = last_td
        self.w2fc2_guard = last_td
        self.tap("tap_w1", RD[:, 0:16384], c0=0, cols=16384)
        self.tap("tap_w1", RE[:, 0:16384], c0=16384, cols=16384)

        def w1c(k):
            reg = RD if k < 4 else RE
            return reg[:, (k % 4) * 4096:(k % 4) * 4096 + 4096]
        b.wait("tensor", self.tok_h1)
        fc2_evac_tok = None
        fc1_mm_done = None
        w2sl_tok = {}
        ev = None
        evmg = {}
        for tc in range(NTC):
            for mg in range(16):
                b2 = (mg % 2) * 2
                if evmg.get(mg % 2) is not None:
                    b.wait("tensor", evmg[mg % 2])
                for k in range(8):
                    b.wait("tensor", self.w1ptok[(k, mg // 8)])
                    for mi in range(2):
                        m = mg * 2 + mi
                        if fc2_evac_tok is not None:
                            b.wait("tensor", fc2_evac_tok)
                        fc1_mm_done = b.do("tensor", lambda e, k=k, m=m, mi=mi, tc=tc, b2=b2,
                                           st=(k == 0), sp=(k == 7): e.matmul(
                            self.ps[b2 + mi][:, :], w1c(k)[:, m * 128:(m + 1) * 128],
                            self.RC[:, k * T + tc * TCW: k * T + (tc + 1) * TCW],
                            start=st, stop=sp),
                            inc=(k == 7 and mi == 1))
                b.wait("scalar", self.ld_toks["fc1b"])
                b.wait("scalar", fc1_mm_done)
                for mi in range(2):
                    m = mg * 2 + mi
                    ev = b.do("scalar", lambda e, mi=mi, m=m, b2=b2: e.activation(
                        out=RA[:, m * TCW:(m + 1) * TCW], in_=self.ps[b2 + mi][:, :],
                        func=AF.Silu, bias=self.BIAS[:, 72 + m:73 + m]), inc=(mi == 1))
                evmg[mg % 2] = ev
                if mg % 2 == 0:
                    self.wquant_step(1 if (tc * 16 + mg) % 4 else 2)
            fc2_evac_tok = ev
            if tc == 0:
                self.tap("tap_y1", RA[:, 0:NF1 * TCW])
            fc2_mm_done = None
            if self.wq_t_tse is not None:
                b.wait("tensor", self.wq_t_tse)
            for tok_ in (self.wq_slot_tok.get(("in", 0)), self.wq_slot_tok.get(("in", 1)),
                         self.wq_prev_qs):
                b.wait("sync", tok_)
            b.wait("sync", self.w2_all_done)
            for kk in range(NF1 // 2):
                sl = kk % 2
                wv = RS[:, sl * 2048: (sl + 1) * 2048]
                b.wait("sync", self.w2fc2_guard)
                if sl in w2sl_tok:
                    b.wait("sync", w2sl_tok[sl])
                tdma = b.dma("sync", wv, t["w2u_hbm"][:, kk * 2048:(kk + 1) * 2048],
                             st="w2l", depth=2)
                b.wait("tensor", tdma)
                b.wait("tensor", ev)
                for k2 in range(2):
                    k = kk * 2 + k2
                    for m in range(8):
                        fc2_mm_done = b.do("tensor", lambda e, k=k, k2=k2, m=m, wv=wv,
                                           st=(k == 0), sp=(k == NF1 - 1): e.matmul(
                            self.ps[m][:, :], wv[:, k2 * 1024 + m * 128: k2 * 1024 + (m + 1) * 128],
                            RA[:, k * TCW:(k + 1) * TCW], start=st, stop=sp),
                            inc=(m == 7 and k2 == 1))
                w2sl_tok[sl] = fc2_mm_done
            b.wait("vector", self.tok_xnpark)
            b.wait("vector", fc2_mm_done)
            ev2 = None
            for m in range(8):
                ev2 = b.do("vector", lambda e, m=m, tc=tc: e.scalar_tensor_tensor(
                    out=RB[:, m * T + tc * TCW: m * T + (tc + 1) * TCW],
                    in0=self.ps[m][:, :], scalar=0.5,
                    in1=self.BIAS[:, 40 + m:41 + m].broadcast_to((128, TCW)),
                    op0=ALU.mult, op1=ALU.add), inc=(m == 7))
            fc2_evac_tok = ev2
            self.fc2_last = fc2_mm_done
            self.ffn_ev2 = ev2
        self.wquant_step(len(self.wq_list))
        self.tap("tap_wgq", None) if False else None
        self.fc1_done = fc1_mm_done
        b.wait("sync", fc1_mm_done)
        b.wait("sync", self.tok_xnpark)
        tok_xn = b.dma("sync", RD[:, :], t["xn_park"][:, :], st="park", depth=2)
        self.tok_xn_restore = tok_xn
        gtok = None
        for f in range(NF):
            b.wait("vector", tok_xn)
            b.wait("vector", fc2_evac_tok)
            if self.sq_war is not None:
                b.wait("vector", self.sq_war)
            t1 = b.do("vector", lambda e, f=f: e.tensor_tensor(
                out=self.SQ[:, :], in0=RB[:, f * T:(f + 1) * T],
                in1=RD[:, f * T:(f + 1) * T], op=ALU.add), inc=True)
            b.wait("scalar", t1)
            gtok = b.do("scalar", lambda e, f=f: e.activation(
                out=RB[:, f * T:(f + 1) * T], in_=self.SQ[:, :], func=AF.Silu), inc=True)
            self.sq_war = gtok
        self.tok_g = gtok
        self.tap("tap_g", RB[:, :])

    def phase_attn(self):
        b, t = self.b, self.t
        RA, RB, RC, RD, RE, RS = self.RA, self.RB, self.RC, self.RD, self.RE, self.RS
        SC = self.SC
        ps = self.ps
        xin = t["x_t"].ap().rearrange("(c p) n -> p c n", p=128)
        xe = RE[:, :].rearrange("p (c n) -> p c n", n=T)
        b.wait("gpsimd", self.fc1_done)
        tok_xe = b.dma("gpsimd", xe, xin, st="gx", depth=1)
        tmem = b.do("vector", lambda e: e.memset(RC[:, 6144:10240], 0.0), inc=True)
        QT = RC[:, 0:2048]
        KT = RC[:, 2048:4096]
        VT = RC[:, 4096:6144]
        QBD = RC[:, 6144:10240]
        AWSP = RC[:, 12288:14336]
        AW_E = RA[:, 4096:6144]
        AW_N = RA[:, 6144:8192]
        SCRF = RA[:, 0:4096].bitcast(F32)

        def aws(s):
            return RC[:, 10240 + s * 512: 10240 + (s + 1) * 512]

        def awt(s):
            return RC[:, 14336 + s * 768: 14336 + (s + 1) * 768]

        ps6b = ps[6][:, :].bitcast(BF16)
        ps7b = ps[7][:, :].bitcast(BF16)
        self.ps6_war = {6: self.wq_t_tse, 7: self.wq_t_tse}
        self.mid_aw_war = {}
        self.mid_awn_war = {}
        self.awt_ctx_war = {}
        self.sp_tn_war = None
        self.sp_ttr_war = None
        self.scrf_war = None
        hw_war = {}
        started = {}
        xupd_tok = None
        sp_exp_tok = {}
        mid_exp_tok = {}
        awsp_war = None
        ctx_done = None
        vt_slot_tok = {}
        qbd_tok = None
        b.wait("tensor", self.ld_toks["eye"])
        b.wait("tensor", self.ld_toks["ones_r_bf"])
        b.wait("tensor", self.ld_toks["bvrow"])
        b.wait("tensor", self.tok_xn_restore)
        b.wait("tensor", self.ffn_ev2)
        b.wait("tensor", self.tok_g)
        self.tap("tap_xn2", RD[:, :])
        self.tap("tap_bias", self.BIAS[:, 8:104], cols=96)
        if self.wq_t_tse is not None:
            b.wait("tensor", self.wq_t_tse)
        for tok_ in ((self.wq_t_sts or []) + [self.wq_t_tcs, self.wq_prev_qs,
                                              self.wq_t_ttr, self.fc2_last]):
            b.wait("gpsimd", tok_)

        def ctx_mm(bank, cols, kb, h, sc_i, awt_ap, st, sp, inc=False):
            vsrc = VT if (kb % 2) == 0 else RA[:, 8192:10240]
            lhsT = vsrc[0:64, (kb // 2) * 128 + h * 64: (kb // 2) * 128 + (h + 1) * 64]
            rhs = awt_ap[0:64, sc_i * 128 + h * 64: sc_i * 128 + (h + 1) * 64]
            return b.do("tensor", lambda e, o=ps[bank][h * 64:(h + 1) * 64, cols:cols + 64],
                        l=lhsT, r=rhs, st=st, sp=sp: e.matmul(
                o, l, r, start=st, stop=sp), inc=inc)

        for hp in range(self.attn_hps):
            sl3 = (hp % 2) * 3072
            wsl = {}
            for wi, wname in enumerate(("wq_t", "wk_t", "wv_t")):
                dst = RS[:, sl3 + wi * 1024: sl3 + (wi + 1) * 1024]
                if (hp % 2, wi) in hw_war:
                    b.wait("gpsimd", hw_war[(hp % 2, wi)])
                srcw = t[wname].ap()[:, hp * 128:(hp + 1) * 128].rearrange(
                    "(c p) n -> p c n", p=128)
                wsl[wname] = (dst, b.dma(
                    "gpsimd", dst.rearrange("p (c n) -> p c n", n=128), srcw,
                    st="hw", depth=3))
            qev = kev = vfev = None
            VFM = RA[:, 4096:6144]
            for which, (wname, bank, biascol, scale, dreg) in enumerate(
                    [("wq_t", 4, 8, 0.125, QT), ("wk_t", 5, 16, 1.0, KT),
                     ("wv_t", 4, 104, 1.0, None)]):
                wdst, wtok = wsl[wname]
                b.wait("tensor", wtok)
                ev_prev = None
                for tc in range(NTC):
                    if ctx_done is not None:
                        b.wait("tensor", ctx_done)
                    if which == 2 and qev is not None:
                        b.wait("tensor", qev)
                    if which == 2 and self.sp_tn_war is not None:
                        b.wait("vector", self.sp_tn_war)
                    if ev_prev is not None:
                        b.wait("tensor", ev_prev)
                    mm = None
                    for k in range(NF):
                        mm = b.do("tensor", lambda e, k=k, tc=tc, w=wdst, bank=bank,
                                  st=(k == 0), sp=(k == NF - 1): e.matmul(
                            ps[bank][:, :], w[:, k * 128:(k + 1) * 128],
                            RD[:, k * T + tc * TCW: k * T + (tc + 1) * TCW],
                            start=st, stop=sp), inc=(k == NF - 1))
                    b.wait("vector", mm)
                    dst = (dreg if dreg is not None else VFM)
                    ev_prev = b.do("vector", lambda e, tc=tc, bank=bank, bc=biascol, sc=scale, hp=hp,
                                   d=dst[:, tc * TCW:(tc + 1) * TCW]:
                                   e.scalar_tensor_tensor(
                        out=d, in0=ps[bank][:, :], scalar=sc,
                        in1=self.BIAS[:, bc + hp:bc + hp + 1].broadcast_to((128, TCW)),
                        op0=ALU.mult, op1=ALU.add), inc=True)
                if which == 0:
                    qev = ev_prev
                elif which == 1:
                    kev = ev_prev
                else:
                    vfev = ev_prev
            vev = None
            for half in range(2):
                tb_ = 6 + half
                psb_ = ps6b if half == 0 else ps7b
                if self.ps6_war.get(tb_) is not None:
                    b.wait("tensor", self.ps6_war[tb_])
                b.wait("tensor", vfev)
                ttrv = None
                for cc in range(8):
                    ci = half * 8 + cc
                    ttrv = b.do("tensor", lambda e, ci=ci, cc=cc, psb_=psb_: e.transpose(
                        psb_[:, cc * 128:(cc + 1) * 128],
                        VFM[:, ci * 128:(ci + 1) * 128], self.EYE[:, :]),
                        inc=(cc == 7))
                b.wait("scalar", ttrv)
                if ctx_done is not None:
                    b.wait("scalar", ctx_done)
                vev = b.do("scalar", lambda e, half=half, psb_=psb_: e.copy(
                    out=VT[:, half * 1024:(half + 1) * 1024],
                    in_=psb_[:, 0:1024]), inc=True)
                self.ps6_war[tb_] = vev
            self.sp_tn_war = ttrv
            hw_war[(hp % 2, 0)] = qev
            hw_war[(hp % 2, 1)] = kev
            hw_war[(hp % 2, 2)] = vev
            V2 = RA[0:64, 8192:10240]
            if ctx_done is not None:
                b.wait("sync", ctx_done)
            b.wait("sync", vev)
            tok_v2 = b.dma("sync", V2, VT[64:128, :], st="v2", depth=1)
            b.wait("scalar", tmem)
            b.wait("scalar", qev)
            if ctx_done is not None:
                b.wait("scalar", ctx_done)
            qbdo = QBD.rearrange("p (j s) -> p j s", s=128)
            qti = QT.rearrange("p (j s) -> p j s", s=64)
            b.do("scalar", lambda e: e.copy(out=qbdo[0:64, :, 0:64], in_=qti[0:64, :, :]))
            qbd_tok = b.do("scalar", lambda e: e.copy(
                out=qbdo[64:128, :, 64:128], in_=qti[64:128, :, :]), inc=True)
            for bank in range(4):
                started[bank] = False
            if xupd_tok is not None:
                b.wait("tensor", xupd_tok)
            for j in [jj for jj in self.attn_js if jj in (0, NB - 1)]:
                bank = j // 8
                cols = (j % 8) * 64
                if j in (0, NB - 1):
                    cp_tok = None
                    for c4 in range(4):
                        sbank = 4 + (c4 % 2)
                        if sp_exp_tok.get(sbank) is not None:
                            b.wait("tensor", sp_exp_tok[sbank])
                        if mid_exp_tok.get(sbank) is not None:
                            b.wait("tensor", mid_exp_tok[sbank])
                        b.wait("tensor", qbd_tok)
                        b.wait("tensor", kev)
                        mm = b.do("tensor", lambda e, c4=c4, sbank=sbank, j=j: e.matmul(
                            ps[sbank][:, :], QBD[:, j * 128:(j + 1) * 128],
                            KT[:, c4 * TCW:(c4 + 1) * TCW], start=True, stop=True), inc=True)
                        b.wait("scalar", mm)
                        if self.scrf_war is not None:
                            b.wait("scalar", self.scrf_war)
                        cp_tok = b.do("scalar", lambda e, c4=c4, sbank=sbank: e.copy(
                            out=SCRF[:, c4 * TCW:(c4 + 1) * TCW], in_=ps[sbank][:, :]),
                            inc=True)
                        sp_exp_tok[sbank] = cp_tok
                    b.wait("vector", cp_tok)
                    tv = b.do("vector", lambda e: e.reduce_max(
                        out=SC[:, 14:15], in_=SCRF[:, :], axis=mybir.AxisListType.X,
                        negate=True), inc=True)
                    self.scrf_war = tv
                    b.wait("scalar", tv)
                    if self.sp_tn_war is not None:
                        b.wait("scalar", self.sp_tn_war)
                    te = b.do("scalar", lambda e: e.activation(
                        out=AW_E[:, :], in_=SCRF[:, :], func=AF.Exp,
                        bias=SC[:, 14:15], scale=1.0, accum_out=SC[:, 15:16]), inc=True)
                    b.wait("vector", te)
                    tr = b.do("vector", lambda e: e.reciprocal(
                        out=SC[:, 16:17], in_=SC[:, 15:16]), inc=True)
                    b.wait("vector", tr)
                    if self.sp_ttr_war is not None:
                        b.wait("vector", self.sp_ttr_war)
                    tn = b.do("vector", lambda e: e.tensor_scalar(
                        out=AW_N[:, :], in0=AW_E[:, :], scalar1=SC[:, 16:17],
                        scalar2=None, op0=ALU.mult), inc=True)
                    self.sp_tn_war = tn
                    evtok = {}
                    ttr = None
                    for half in range(2):
                        tbank = 6 + half
                        psb = ps6b if half == 0 else ps7b
                        if self.ps6_war.get(tbank) is not None:
                            b.wait("tensor", self.ps6_war[tbank])
                        b.wait("tensor", tn)
                        for cc in range(8):
                            ci = half * 8 + cc
                            ttr = b.do("tensor", lambda e, ci=ci, cc=cc, psb=psb: e.transpose(
                                psb[:, cc * 128:(cc + 1) * 128],
                                AW_N[:, ci * 128:(ci + 1) * 128], self.EYE[:, :]),
                                inc=(cc == 7))
                        b.wait("scalar", ttr)
                        if awsp_war is not None:
                            b.wait("scalar", awsp_war)
                        tev = b.do("scalar", lambda e, half=half, psb=psb: e.copy(
                            out=AWSP[:, half * 1024:(half + 1) * 1024],
                            in_=psb[:, 0:1024]), inc=True)
                        evtok[half] = tev
                        self.ps6_war[tbank] = tev
                    self.sp_ttr_war = ttr
                    awsp_war = None
                    b.wait("tensor", vev)
                    for sl8 in evtok.values():
                        b.wait("tensor", sl8)
                    mmc = None
                    for h in range(2):
                        for ci in range(16):
                            mmc = b.do("tensor", lambda e, ci=ci, h=h, st=(ci == 0), bank=bank,
                                       cols=cols: e.matmul(
                                ps[bank][h * 64:(h + 1) * 64, cols:cols + 64],
                                VT[:, ci * 128 + h * 64: ci * 128 + (h + 1) * 64],
                                AWSP[:, ci * 128 + h * 64: ci * 128 + (h + 1) * 64],
                                start=st, stop=(ci == 15)), inc=(ci == 15 and h == 1))
                    awsp_war = mmc
                    ctx_done = mmc
                else:
                    pass
            if True:
                def mid_meta(j):
                    if j == 1:
                        return dict(kbs=[0, 1, 2, 31],
                                    spans=[(KT[:, 0:192], 0, 192), (KT[:, 1984:2048], 192, 64)],
                                    width=256, dummy0=None)
                    if j == NB - 2:
                        return dict(kbs=[0, 29, 30, 31],
                                    spans=[(KT[:, 0:64], 0, 64), (KT[:, 1856:2048], 64, 192)],
                                    width=256, dummy0=None)
                    if j % 2 == 0:
                        return dict(kbs=[0, j - 1, j, j + 1, None, 31],
                                    spans=[(KT[:, 0:64], 0, 64),
                                           (KT[:, (j - 1) * 64:(j + 2) * 64], 64, 192),
                                           (KT[:, 1984:2048], 320, 64)],
                                    width=384, dummy0=256)
                    return dict(kbs=[j - 1, j, j + 1, 31, 0, None],
                                spans=[(KT[:, (j - 1) * 64:(j + 2) * 64], 0, 192),
                                       (KT[:, 1984:2048], 192, 64),
                                       (KT[:, 0:64], 256, 64)],
                                width=384, dummy0=320)

                def em_scores(j, meta):
                    sbank = 4 + (j % 2)
                    if mid_exp_tok.get(sbank) is not None:
                        b.wait("tensor", mid_exp_tok[sbank])
                    if sp_exp_tok.get(sbank) is not None:
                        b.wait("tensor", sp_exp_tok[sbank])
                    b.wait("tensor", qbd_tok)
                    b.wait("tensor", kev)
                    msv = None
                    if meta["dummy0"] is not None:
                        if mid_exp_tok.get((sbank, "ms")) is not None:
                            b.wait("vector", mid_exp_tok[(sbank, "ms")])
                        msv = b.do("vector", lambda e, sbank=sbank, d0=meta["dummy0"]: e.memset(
                            ps[sbank][:, d0:d0 + 64], -30000.0), inc=True)
                        b.wait("tensor", msv)
                    mm = None
                    for _i, (rhs, c0, w) in enumerate(meta["spans"]):
                        mm = b.do("tensor", lambda e, rhs=rhs, c0=c0, w=w, j=j,
                                  sbank=sbank: e.matmul(
                            ps[sbank][:, c0:c0 + w], QBD[:, j * 128:(j + 1) * 128],
                            rhs, start=True, stop=True), inc=(_i == len(meta["spans"]) - 1))
                    meta["mm"] = mm
                    meta["msv"] = msv

                def em_max(j, meta):
                    sl = j % 2
                    sbank = 4 + sl
                    b.wait("vector", meta["mm"])
                    tv_ = b.do("vector", lambda e, sbank=sbank, w=meta["width"], sl=sl: e.reduce_max(
                        out=SC[:, 8 + sl:9 + sl], in_=self.ps[sbank][:, 0:w],
                        axis=mybir.AxisListType.X, negate=True), inc=True)
                    meta["max"] = tv_

                def em_exp(j, meta):
                    sl = j % 2
                    sbank = 4 + sl
                    b.wait("scalar", meta["max"])
                    if self.mid_aw_war.get(sl) is not None:
                        b.wait("scalar", self.mid_aw_war[sl])
                    te_ = b.do("scalar", lambda e, sbank=sbank, w=meta["width"], sl=sl: e.activation(
                        out=aws(sl)[:, 0:w], in_=self.ps[sbank][:, 0:w], func=AF.Exp,
                        bias=SC[:, 8 + sl:9 + sl], scale=1.0,
                        accum_out=SC[:, 10 + sl:11 + sl]), inc=True)
                    meta["exp"] = te_
                    mid_exp_tok[sbank] = te_
                    mid_exp_tok[(sbank, "ms")] = te_

                def em_norm(j, meta):
                    sl = j % 2
                    b.wait("vector", meta["exp"])
                    tr_ = b.do("vector", lambda e, sl=sl: e.reciprocal(
                        out=SC[:, 12 + sl:13 + sl], in_=SC[:, 10 + sl:11 + sl]), inc=True)
                    b.wait("vector", tr_)
                    if self.mid_awn_war.get(sl) is not None:
                        b.wait("vector", self.mid_awn_war[sl])
                    tn_ = b.do("vector", lambda e, sl=sl, w=meta["width"]: e.tensor_scalar(
                        out=aws(2 + sl)[:, 0:w], in0=aws(sl)[:, 0:w],
                        scalar1=SC[:, 12 + sl:13 + sl], scalar2=None, op0=ALU.mult), inc=True)
                    self.mid_aw_war[sl] = tn_
                    meta["norm"] = tn_

                def em_transp(j, meta):
                    sl = j % 2
                    bank = 6 + sl
                    psb = ps6b if sl == 0 else ps7b
                    npieces = meta["width"] // 64
                    if self.ps6_war.get(bank) is not None:
                        b.wait("tensor", self.ps6_war[bank])
                    b.wait("tensor", meta["norm"])
                    ttr_ = None
                    for c in range(npieces):
                        ttr_ = b.do("tensor", lambda e, c=c, psb=psb, sl=sl: e.transpose(
                            psb[0:64, c * 128: (c + 1) * 128],
                            aws(2 + sl)[:, c * 64:(c + 1) * 64], self.EYE[:, :]),
                            inc=(c == npieces - 1))
                    b.wait("scalar", ttr_)
                    if self.awt_ctx_war.get(sl) is not None:
                        b.wait("scalar", self.awt_ctx_war[sl])
                    evl_ = b.do("scalar", lambda e, psb=psb, sl=sl,
                                w=npieces * 128: e.copy(
                        out=awt(sl)[0:64, 0:w], in_=psb[0:64, 0:w]), inc=True)
                    self.ps6_war[bank] = evl_
                    self.mid_awn_war[sl] = evl_
                    meta["evl"] = evl_

                def em_ctx(j, meta):
                    bank = j // 8
                    cols = (j % 8) * 64
                    b.wait("tensor", meta["evl"])
                    b.wait("tensor", vev)
                    b.wait("tensor", tok_v2)
                    valid = [(i_, kb) for i_, kb in enumerate(meta["kbs"]) if kb is not None]
                    mmc_ = None
                    for h in range(2):
                        for idx, (sc_i, kb) in enumerate(valid):
                            mmc_ = ctx_mm(bank, cols, kb, h, sc_i, awt(j % 2),
                                          st=(idx == 0), sp=(idx == len(valid) - 1),
                                          inc=(idx == len(valid) - 1 and h == 1))
                    self.awt_ctx_war[j % 2] = mmc_
                    return mmc_

                stages = [em_scores, em_max, em_exp, em_norm, em_transp]
                groups = [[1], [2, 3], [4, 5], [6, 7], [8, 9], [10, 11], [12, 13],
                          [14, 15], [16, 17], [18, 19], [20, 21], [22, 23], [24, 25],
                          [26, 27], [28, 29], [30]]
                groups = [[j for j in g if j in self.attn_js] for g in groups]
                groups = [g for g in groups if g]
                for grp in groups:
                    metas = {j: mid_meta(j) for j in grp}
                    for stg in stages:
                        for j in grp:
                            stg(j, metas[j])
                    for j in grp:
                        ctx_done = em_ctx(j, metas[j])
            if hp == 0:
                self.tap("tap_qt", QT)
                self.tap("tap_kt", KT)
                self.tap("tap_vt", VT)
            b.wait("vector", ctx_done)
            b.wait("vector", tok_xe)
            b.wait("vector", self.tok_g)
            for q4 in range(4):
                if self.sq_war is not None:
                    b.wait("vector", self.sq_war)
                t1 = b.do("vector", lambda e, q4=q4, hp=hp: e.tensor_tensor(
                    out=self.SQ[:, 0:TCW], in0=ps[q4][:, :],
                    in1=RB[:, hp * T + q4 * TCW: hp * T + (q4 + 1) * TCW],
                    op=ALU.mult), inc=True)
                b.wait("vector", t1)
                xupd_tok = b.do("vector", lambda e, q4=q4, hp=hp: e.tensor_tensor(
                    out=RE[:, hp * T + q4 * TCW: hp * T + (q4 + 1) * TCW],
                    in0=RE[:, hp * T + q4 * TCW: hp * T + (q4 + 1) * TCW],
                    in1=self.SQ[:, 0:TCW], op=ALU.add), inc=True)
                self.sq_war = xupd_tok
        self.tok_x1 = xupd_tok
        self.tap("tap_x1", RE[:, :])

    def softmax_transpose(self, sbank, width, j, aws, awt, ps6b, nchunks,
                          extra_dep=None):
        b = self.b
        SC = self.SC
        sl = j % 2
        nm = SC[:, 8 + sl:9 + sl]
        rs = SC[:, 10 + sl:11 + sl]
        rc = SC[:, 12 + sl:13 + sl]
        if extra_dep is not None:
            b.wait("vector", extra_dep)
        tv = b.do("vector", lambda e: e.reduce_max(
            out=nm, in_=self.ps[sbank][:, 0:width],
            axis=mybir.AxisListType.X, negate=True), inc=True)
        b.wait("scalar", tv)
        if self.mid_aw_war.get(sl) is not None:
            b.wait("scalar", self.mid_aw_war[sl])
        te = b.do("scalar", lambda e: e.activation(
            out=aws(sl)[:, 0:width], in_=self.ps[sbank][:, 0:width], func=AF.Exp,
            bias=nm, scale=1.0, accum_out=rs), inc=True)
        b.wait("vector", te)
        tr = b.do("vector", lambda e: e.reciprocal(out=rc, in_=rs), inc=True)
        b.wait("vector", tr)
        if self.mid_awn_war.get(sl) is not None:
            b.wait("vector", self.mid_awn_war[sl])
        tn = b.do("vector", lambda e: e.tensor_scalar(
            out=aws(2 + sl)[:, 0:width], in0=aws(sl)[:, 0:width], scalar1=rc,
            scalar2=None, op0=ALU.mult), inc=True)
        self.mid_aw_war[sl] = tn
        ttr = None
        npieces = width // 64
        bank = 6 + sl
        psb = self.ps[bank][:, :].bitcast(BF16)
        if self.ps6_war.get(bank) is not None:
            b.wait("tensor", self.ps6_war[bank])
        b.wait("tensor", tn)
        for c in range(npieces):
            ttr = b.do("tensor", lambda e, c=c, psb=psb: e.transpose(
                psb[0:64, c * 128: (c + 1) * 128],
                aws(2 + sl)[:, c * 64:(c + 1) * 64], self.EYE[:, :]),
                inc=(c == npieces - 1))
        b.wait("scalar", ttr)
        if self.awt_ctx_war.get(sl) is not None:
            b.wait("scalar", self.awt_ctx_war[sl])
        evl = b.do("scalar", lambda e, psb=psb, w=npieces * 128: e.copy(
            out=awt(sl)[0:64, 0:w], in_=psb[0:64, 0:w]), inc=True)
        self.ps6_war[bank] = evl
        self.mid_awn_war[sl] = evl
        return te, evl

    def phase_l1(self):
        b, t = self.b, self.t
        RA, RB, RC, RD, RE, RS = self.RA, self.RB, self.RC, self.RD, self.RE, self.RS
        SC, SWQ, HMX, ROWS = self.SC, self.SWQ, self.HMX, self.ROWS
        ps = self.ps
        ps6b = ps[6][:, :].bitcast(BF16)

        b.wait("tensor", self.tok_x1)
        b.wait("scalar", self.tok_x1)
        tok_ln2 = self.emit_ln(RE, RD, src_tok=self.tok_x1)
        self.tap("tap_xn2", RD[:, :])

        b.wait("vector", tok_ln2)
        tv = None
        for f in range(NF):
            tv = b.do("vector", lambda e, f=f: e.reduce_max(
                out=HMX[:, f:f + 1], in_=RD[:, f * T:(f + 1) * T],
                axis=mybir.AxisListType.X, apply_absolute_value=True), inc=(f == NF - 1))
        b.wait("vector", tv)
        tv = b.do("vector", lambda e: e.reduce_max(
            out=HMX[:, 8:9], in_=HMX[:, 0:8], axis=mybir.AxisListType.X), inc=True)
        b.wait("vector", tv)
        tv = b.do("vector", lambda e: e.tensor_copy(
            out=self.SQ[:, 0:1], in_=HMX[:, 8:9]), inc=True)
        b.wait("tensor", tv)
        if self.ps6_war.get(6) is not None:
            b.wait("tensor", self.ps6_war[6])
        ttr = b.do("tensor", lambda e: e.transpose(
            ps6b[0:1, 0:128], self.SQ[:, 0:1], self.EYE[:, :]), inc=True)
        b.wait("scalar", ttr)
        tse = b.do("scalar", lambda e: e.copy(
            out=ROWS[96:97, 1024:1152], in_=ps6b[0:1, 0:128]), inc=True)
        self.ps6_war[6] = tse
        b.wait("vector", tse)
        tv = b.do("vector", lambda e: e.reduce_max(
            out=ROWS[96:97, 1152:1153], in_=ROWS[96:97, 1024:1152],
            axis=mybir.AxisListType.X), inc=True)
        b.wait("sync", tv)
        tcc = b.dma("sync", t["ccx_in"][:, :], ROWS[96:97, 1152:1153], st="cc", depth=1)
        b.wait("gpsimd", tcc)
        ar1 = b.do("gpsimd", lambda e: e.collective_compute(
            "AllReduce", ALU.max, replica_groups=[list(range(self.ncores))],
            ins=[t["ccx_in"].ap().opt()], outs=[t["ccx_out"].ap().opt()]), inc=True)
        b.wait("gpsimd", ar1)
        tgx = b.dma("gpsimd", SC[0:1, 2:3], t["ccx_out"][:, :], st="ccb", depth=1)

        b.wait("vector", self.ld_toks["nw"])
        b.wait("vector", self.ld_toks["nb"])
        b.wait("tensor", tse)
        tok_h2 = self.emit_ln(RD, RB, gamma_col=56, beta_col=64, src_tok=tok_ln2)

        b.wait("sync", tok_ln2)
        tok_x1park = b.dma("sync", t["x1_park"][:, :], RE[:, :], st="park", depth=2)
        b.wait("gpsimd", tok_x1park)
        kvw = RE
        tok_kvw = b.dma("gpsimd", RE[:, :].rearrange("p (c n) -> p c n", n=2048),
                        t["qkvw_t"].ap()[:, H:3 * H].rearrange("(c p) n -> p c n", p=128),
                        st="gw", depth=2)

        b.wait("tensor", tok_kvw)
        b.wait("tensor", tok_h2)
        kv_ev = {}
        for st_ in range(16):
            g4 = (st_ % 2) * 4
            evp = kv_ev.get(st_ % 2)
            if evp is not None:
                b.wait("tensor", evp)
            mm = None
            for k in range(NF):
                for part in range(4):
                    mm = b.do("tensor", lambda e, k=k, st_=st_, part=part, g4=g4,
                              st=(k == 0), sp=(k == NF - 1): e.matmul(
                        ps[g4 + part][:, :],
                        RB[:, k * T + st_ * 128: k * T + (st_ + 1) * 128],
                        kvw[:, k * 2048 + part * 512: k * 2048 + (part + 1) * 512],
                        start=st, stop=sp), inc=(k == NF - 1 and part == 3))
            b.wait("scalar", mm)
            ev = None
            for part in range(4):
                dstreg = RA if part < 2 else RC
                ev = b.do("scalar", lambda e, st_=st_, part=part, g4=g4, d=dstreg: e.copy(
                    out=d[:, st_ * 1024 + (part % 2) * 512: st_ * 1024 + (part % 2 + 1) * 512],
                    in_=ps[g4 + part][:, :]), inc=(part == 3))
            kv_ev[st_ % 2] = ev
        kv_done = ev

        b.wait("tensor", kv_done)
        mm = None
        for st_ in range(16):
            for ii, (reg, bank, row) in enumerate([(RA, 0, 0), (RA, 1, 0), (RC, 0, 32), (RC, 1, 32)]):
                lo = (ii % 2)
                mm = b.do("tensor", lambda e, st_=st_, reg=reg, bank=bank, row=row, lo=lo,
                          st=(st_ == 0), sp=(st_ == 15): e.matmul(
                    ps[bank][row:row + 1, :], self.ONESC[:, :],
                    reg[:, st_ * 1024 + lo * 512: st_ * 1024 + (lo + 1) * 512],
                    start=st, stop=sp), inc=(st_ == 15 and ii == 3))
        b.wait("scalar", mm)
        ev = None
        for ii, (bank, row, dst) in enumerate([(0, 0, self.r_sk[0:1, 0:512]),
                                               (1, 0, self.r_sk[0:1, 512:1024]),
                                               (0, 32, self.r_sv[0:1, 0:512]),
                                               (1, 32, self.r_sv[0:1, 512:1024])]):
            ev = b.do("scalar", lambda e, bank=bank, row=row, dst=dst: e.copy(
                out=dst, in_=ps[bank][row:row + 1, :]), inc=(ii == 3))
        b.wait("vector", ev)
        b.wait("vector", self.ld_toks["bvrow1"])
        tsvp = b.do("vector", lambda e: e.scalar_tensor_tensor(
            out=self.r_svp, in0=self.r_bv1, scalar=2048.0, in1=self.r_sv,
            op0=ALU.mult, op1=ALU.add), inc=True)

        b.wait("tensor", tsvp)
        b.wait("tensor", self.ld_toks["bkrow"])
        a_ev = None
        for c1 in range(8):
            if a_ev is not None:
                b.wait("tensor", a_ev)
            mm = None
            for st_ in range(16):
                for lo in range(2):
                    mm = b.do("tensor", lambda e, c1=c1, st_=st_, lo=lo,
                              st=(st_ == 0): e.matmul(
                        ps[2 + lo][:, :],
                        RA[:, st_ * 1024 + c1 * 64 * 2: st_ * 1024 + (c1 + 1) * 128],
                        RC[:, st_ * 1024 + lo * 512: st_ * 1024 + (lo + 1) * 512],
                        start=st, stop=False))
            for lo in range(2):
                b.do("tensor", lambda e, c1=c1, lo=lo: e.matmul(
                    ps[2 + lo][:, :], self.r_bk[0:1, c1 * 128:(c1 + 1) * 128],
                    self.r_svp[0:1, lo * 512:(lo + 1) * 512], start=False, stop=False))
                mm = b.do("tensor", lambda e, c1=c1, lo=lo: e.matmul(
                    ps[2 + lo][:, :], self.r_sk[0:1, c1 * 128:(c1 + 1) * 128],
                    self.r_bv1[0:1, lo * 512:(lo + 1) * 512], start=False, stop=True),
                    inc=(lo == 1))
            b.wait("scalar", mm)
            for lo in range(2):
                a_ev = b.do("scalar", lambda e, c1=c1, lo=lo: e.activation(
                    out=RE[:, c1 * 1024 + lo * 512: c1 * 1024 + (lo + 1) * 512],
                    in_=ps[2 + lo][:, :], func=AF.Copy, scale=1.0 / 32.0, bias=0.0),
                    inc=(lo == 1))
        a_done = a_ev
        self.tap("tap_a", RE[:, 0:8192])

        b.wait("tensor", a_done)
        b.wait("tensor", self.ld_toks["bq1c"])
        mm = None
        for c2c in range(8):
            for c1 in range(8):
                mm = b.do("tensor", lambda e, c2c=c2c, c1=c1, st=(c1 == 0),
                          sp=(c1 == 7): e.matmul(
                    ps[4][:, c2c:c2c + 1],
                    RE[:, c1 * 1024 + c2c * 128: c1 * 1024 + (c2c + 1) * 128],
                    self.BQC[:, c1:c1 + 1], start=st, stop=sp),
                    inc=(c2c == 7 and c1 == 7))
        b.wait("scalar", mm)
        tbqa = b.do("scalar", lambda e: e.copy(out=SC[:, 24:32], in_=ps[4][:, 0:8]), inc=True)

        b.wait("gpsimd", a_done)
        tok_qw = b.dma("gpsimd", RA[:, 0:8192].rearrange("p (c n) -> p c n", n=1024),
                       t["qkvw_t"].ap()[:, 0:H].rearrange("(c p) n -> p c n", p=128),
                       st="gw", depth=2)
        b.wait("tensor", tok_qw)
        q_ev = {}
        for m in range(8):
            for tc in range(NTC):
                bank = (m * NTC + tc) % 8
                evp = q_ev.get(bank)
                if evp is not None:
                    b.wait("tensor", evp)
                if bank == 4:
                    b.wait("tensor", tbqa)
                mm = None
                for k in range(NF):
                    mm = b.do("tensor", lambda e, m=m, tc=tc, k=k, bank=bank,
                              st=(k == 0), sp=(k == NF - 1): e.matmul(
                        ps[bank][:, :], RA[:, k * 1024 + m * 128: k * 1024 + (m + 1) * 128],
                        RB[:, k * T + tc * TCW: k * T + (tc + 1) * TCW],
                        start=st, stop=sp), inc=(k == NF - 1))
                b.wait("scalar", mm)
                ev = b.do("scalar", lambda e, m=m, tc=tc, bank=bank: e.copy(
                    out=RC[:, m * T + tc * TCW: m * T + (tc + 1) * TCW],
                    in_=ps[bank][:, :]), inc=True)
                q_ev[bank] = ev
        q_done = ev

        b.wait("vector", tgx)
        b.wait("vector", self.ld_toks["clip"])
        tv = b.do("vector", lambda e: e.tensor_copy(out=SC[0:1, 3:4], in_=SC[0:1, 0:1]), inc=True)
        b.wait("scalar", tv)
        tsg = b.do("scalar", lambda e: e.activation(
            out=SC[0:1, 4:5], in_=SC[0:1, 3:4], func=AF.Sigmoid), inc=True)
        b.wait("vector", tsg)
        tv = b.do("vector", lambda e: e.tensor_tensor(
            out=SC[0:1, 5:6], in0=SC[0:1, 4:5], in1=SC[0:1, 2:3], op=ALU.mult), inc=True)
        b.wait("vector", tv)
        tv = b.do("vector", lambda e: e.tensor_scalar(
            out=SC[0:1, 6:7], in0=SC[0:1, 5:6], scalar1=1.0 / 127.0, scalar2=None,
            op0=ALU.mult), inc=True)
        b.wait("vector", tv)
        tsx = b.do("vector", lambda e: e.reciprocal(out=SC[0:1, 7:8], in_=SC[0:1, 6:7]), inc=True)
        b.wait("tensor", tsx)
        b.wait("tensor", q_done)
        mmb = b.do("tensor", lambda e: e.matmul(
            ps[5][:, 0:1], self.ONESRF[:, :], SC[0:1, 6:7], start=True, stop=True))
        mmb = b.do("tensor", lambda e: e.matmul(
            ps[5][:, 1:2], self.ONESRF[:, :], SC[0:1, 7:8], start=True, stop=True), inc=True)
        b.wait("scalar", mmb)
        tsxb = b.do("scalar", lambda e: e.copy(out=SC[:, 6:8], in_=ps[5][:, 0:2]), inc=True)
        xq_tok = None
        b.wait("vector", tsxb)
        b.wait("vector", q_done)
        for f in range(NF):
            for half in range(2):
                src_ap = RD[:, f * T + half * 1024: f * T + (half + 1) * 1024]
                dst_ap = RA[:, f * T + half * 1024: f * T + (half + 1) * 1024]
                xq_tok = self.quant_piece(src_ap, dst_ap, SC[:, 7:8], out_signed=False,
                                          dep_toks=[self.wq_prev_qs] if self.wq_prev_qs else [])
                self.wq_prev_qs = xq_tok
        self.tap("tap_xq", RA[:, :])
        b.wait("sync", xq_tok)
        tok_xn2park = b.dma("sync", t["xn2_park"][:, :], RD[:, :], st="park", depth=2)

        b.wait("vector", self.wq_v_done)
        pre = None
        for i in range(NIG):
            for (scol, ccol, svcol, bvcol) in [(0, 64, 128, 180), (22, 86, 150, 202)]:
                tv1 = b.do("vector", lambda e, i=i, scol=scol, svcol=svcol: e.tensor_tensor(
                    out=SWQ[:, svcol + i:svcol + i + 1], in0=SWQ[:, scol + i:scol + i + 1],
                    in1=SC[:, 6:7], op=ALU.mult), inc=True)
                b.wait("vector", tv1)
                tv2 = b.do("vector", lambda e, i=i, ccol=ccol, svcol=svcol: e.tensor_tensor(
                    out=SWQ[:, 240 + 0:241], in0=SWQ[:, ccol + i:ccol + i + 1],
                    in1=SWQ[:, svcol + i:svcol + i + 1], op=ALU.mult), inc=True)
                b.wait("vector", tv2)
                pre = b.do("vector", lambda e, i=i, bvcol=bvcol: e.tensor_scalar(
                    out=SWQ[:, bvcol + i:bvcol + i + 1], in0=SWQ[:, 240:241],
                    scalar1=-128.0, scalar2=None, op0=ALU.mult), inc=True)

        def hdst(i):
            if i < 8:
                return RD[:, i * T:(i + 1) * T]
            if i < 16:
                return RB[:, (i - 8) * T:(i - 7) * T]
            if i < 20:
                return RE[:, 8192 + (i - 16) * T: 8192 + (i - 15) * T]
            return None

        gl_war = {}
        gu_bank_free = {}
        hm_i = 0
        h_ev = None
        hstage_tok = None
        hpark_toks = []
        b.wait("vector", tok_xn2park)
        for i in range(NIG):
            sl = i % 2
            wgsl = RS[:, sl * 2048: sl * 2048 + 1024]
            wusl = RS[:, sl * 2048 + 1024: sl * 2048 + 2048]
            if sl in gl_war:
                b.wait("sync", gl_war[sl])
            tg = b.dma("sync", wgsl[:, :].rearrange("p (j c) -> p j c", c=128),
                       t["wgq_hbm"].ap().rearrange("p (j c) -> p j c", c=IG)[:, :, i * 128:(i + 1) * 128],
                       st="wgl", depth=2)
            tu = b.dma("sync", wusl[:, :].rearrange("p (j c) -> p j c", c=128),
                       t["wuq_hbm"].ap().rearrange("p (j c) -> p j c", c=IG)[:, :, i * 128:(i + 1) * 128],
                       st="wgl", depth=2)
            b.wait("tensor", tg)
            b.wait("tensor", tu)
            b.wait("tensor", xq_tok)
            mm = None
            for tc in range(NTC):
                if gu_bank_free.get(tc) is not None:
                    b.wait("tensor", gu_bank_free[tc])
                if gu_bank_free.get(4 + tc) is not None:
                    b.wait("tensor", gu_bank_free[4 + tc])
                for j in range(NF):
                    mm = b.do("tensor", lambda e, i=i, tc=tc, j=j, w=wgsl,
                              st=(j == 0), sp=(j == NF - 1): e.matmul(
                        ps[tc][:, :], w[:, j * 128:(j + 1) * 128],
                        RA[:, j * T + tc * TCW: j * T + (tc + 1) * TCW],
                        start=st, stop=sp))
                    mm = b.do("tensor", lambda e, i=i, tc=tc, j=j, w=wusl,
                              st=(j == 0), sp=(j == NF - 1): e.matmul(
                        ps[4 + tc][:, :], w[:, j * 128:(j + 1) * 128],
                        RA[:, j * T + tc * TCW: j * T + (tc + 1) * TCW],
                        start=st, stop=sp), inc=(tc == NTC - 1 and j == NF - 1))
            gl_war[sl] = mm
            b.wait("scalar", mm)
            b.wait("vector", mm)
            b.wait("scalar", pre)
            hd = hdst(i)
            for tc in range(NTC):
                if self.sq_war is not None:
                    b.wait("scalar", self.sq_war)
                tsg_ = b.do("scalar", lambda e, i=i, tc=tc: e.activation(
                    out=self.SQ[:, 0:TCW], in_=ps[tc][:, :], func=AF.Silu,
                    bias=SWQ[:, 180 + i:181 + i], scale=SWQ[:, 128 + i:129 + i]), inc=True)
                gu_bank_free[tc] = tsg_
                b.wait("vector", tsg_)
                tup = b.do("vector", lambda e, i=i, tc=tc: e.scalar_tensor_tensor(
                    out=self.SQ[:, 512:1024], in0=ps[4 + tc][:, :],
                    scalar=SWQ[:, 150 + i:151 + i],
                    in1=SWQ[:, 202 + i:203 + i].broadcast_to((128, TCW)),
                    op0=ALU.mult, op1=ALU.add), inc=True)
                gu_bank_free[4 + tc] = tup
                b.wait("vector", tup)
                if hd is not None:
                    hdest = hd[:, tc * TCW:(tc + 1) * TCW]
                else:
                    hdest = RS[:, 4096 + (tc % 2) * 512: 4096 + (tc % 2 + 1) * 512]
                    if hstage_tok is not None:
                        b.wait("vector", hstage_tok)
                th = b.do("vector", lambda e, d=hdest: e.tensor_tensor(
                    out=d, in0=self.SQ[:, 0:TCW], in1=self.SQ[:, 512:1024],
                    op=ALU.mult), inc=True)
                self.sq_war = th
                h_ev = th
                b.wait("vector", th)
                tvm = b.do("vector", lambda e, d=hdest, c=hm_i: e.reduce_max(
                    out=HMX[:, c:c + 1], in_=d, axis=mybir.AxisListType.X,
                    apply_absolute_value=True), inc=True)
                hm_i += 1
                h_ev = tvm
                if hd is None:
                    b.wait("sync", tvm)
                    hstage_tok = b.dma("sync", t["h_park"][:, (i - 20) * T + tc * TCW:
                                                           (i - 20) * T + (tc + 1) * TCW],
                                       hdest, st="hpk", depth=2)
                    hpark_toks.append(hstage_tok)
                    hstage_tok_v = hstage_tok
        assert hm_i == 88, hm_i

        b.wait("vector", h_ev)
        tv = b.do("vector", lambda e: e.reduce_max(
            out=HMX[:, 88:89], in_=HMX[:, 0:88], axis=mybir.AxisListType.X), inc=True)
        b.wait("vector", tv)
        tv = b.do("vector", lambda e: e.tensor_copy(out=self.SQ[:, 0:1], in_=HMX[:, 88:89]), inc=True)
        b.wait("tensor", tv)
        if self.ps6_war.get(6) is not None:
            b.wait("tensor", self.ps6_war[6])
        ttr = b.do("tensor", lambda e: e.transpose(
            ps6b[0:1, 0:128], self.SQ[:, 0:1], self.EYE[:, :]), inc=True)
        b.wait("scalar", ttr)
        b.wait("scalar", tcc)
        tse = b.do("scalar", lambda e: e.copy(
            out=ROWS[96:97, 1024:1152], in_=ps6b[0:1, 0:128]), inc=True)
        self.ps6_war[6] = tse
        b.wait("vector", tse)
        tv = b.do("vector", lambda e: e.reduce_max(
            out=ROWS[96:97, 1153:1154], in_=ROWS[96:97, 1024:1152],
            axis=mybir.AxisListType.X), inc=True)
        b.wait("sync", tv)
        tcc2 = b.dma("sync", t["cch_in"][:, :], ROWS[96:97, 1153:1154], st="cc", depth=1)
        b.wait("gpsimd", tcc2)
        ar2 = b.do("gpsimd", lambda e: e.collective_compute(
            "AllReduce", ALU.max, replica_groups=[list(range(self.ncores))],
            ins=[t["cch_in"].ap().opt()], outs=[t["cch_out"].ap().opt()]), inc=True)
        b.wait("gpsimd", ar2)
        tgx2 = b.dma("gpsimd", SC[0:1, 17:18], t["cch_out"][:, :], st="ccb", depth=1)

        self.ps6_war = {6: h_ev, 7: h_ev}
        self.wdquant()

        o_ev = {}
        ev = None
        for c2 in range(8):
            for tc in range(NTC):
                bank = (c2 * NTC + tc) % 8
                evp = o_ev.get(bank)
                if evp is not None:
                    b.wait("tensor", evp)
                b.wait("tensor", h_ev)
                for bb_ in (6, 7):
                    if self.ps6_war.get(bb_) is not None:
                        b.wait("tensor", self.ps6_war[bb_])
                mm = None
                for c1 in range(8):
                    mm = b.do("tensor", lambda e, c2=c2, tc=tc, c1=c1, bank=bank,
                              st=(c1 == 0), sp=(c1 == 7): e.matmul(
                        ps[bank][:, :],
                        RE[:, c1 * 1024 + c2 * 128: c1 * 1024 + (c2 + 1) * 128],
                        RC[:, c1 * T + tc * TCW: c1 * T + (tc + 1) * TCW],
                        start=st, stop=sp), inc=(c1 == 7))
                b.wait("vector", mm)
                b.wait("vector", tbqa)
                ev = b.do("vector", lambda e, c2=c2, tc=tc, bank=bank: e.scalar_tensor_tensor(
                    out=RA[:, c2 * T + tc * TCW: c2 * T + (tc + 1) * TCW],
                    in0=ps[bank][:, :], scalar=1.0,
                    in1=SC[:, 24 + c2:25 + c2].broadcast_to((128, TCW)),
                    op0=ALU.mult, op1=ALU.add), inc=True)
                o_ev[bank] = ev
        o_done = ev

        b.wait("vector", tgx2)
        b.wait("vector", self.ld_toks["clipd"])
        tv = b.do("vector", lambda e: e.tensor_copy(out=SC[0:1, 18:19], in_=SC[0:1, 1:2]), inc=True)
        b.wait("scalar", tv)
        tsg2 = b.do("scalar", lambda e: e.activation(
            out=SC[0:1, 19:20], in_=SC[0:1, 18:19], func=AF.Sigmoid), inc=True)
        b.wait("vector", tsg2)
        tv = b.do("vector", lambda e: e.tensor_tensor(
            out=SC[0:1, 20:21], in0=SC[0:1, 19:20], in1=SC[0:1, 17:18], op=ALU.mult), inc=True)
        b.wait("vector", tv)
        tv = b.do("vector", lambda e: e.tensor_scalar(
            out=SC[0:1, 21:22], in0=SC[0:1, 20:21], scalar1=1.0 / 127.0, scalar2=None,
            op0=ALU.mult), inc=True)
        b.wait("vector", tv)
        tshr = b.do("vector", lambda e: e.reciprocal(out=SC[0:1, 22:23], in_=SC[0:1, 21:22]), inc=True)
        b.wait("tensor", tshr)
        b.wait("tensor", o_done)
        mmb = b.do("tensor", lambda e: e.matmul(
            ps[5][:, 2:3], self.ONESRF[:, :], SC[0:1, 21:22], start=True, stop=True))
        mmb = b.do("tensor", lambda e: e.matmul(
            ps[5][:, 3:4], self.ONESRF[:, :], SC[0:1, 22:23], start=True, stop=True), inc=True)
        b.wait("scalar", mmb)
        tshb = b.do("scalar", lambda e: e.copy(out=SC[:, 21:23], in_=ps[5][:, 2:4]), inc=True)
        b.wait("vector", tshb)
        b.wait("vector", self.wd_v_done)
        pre2 = None
        for m in range(8):
            tv1 = b.do("vector", lambda e, m=m: e.tensor_tensor(
                out=SWQ[:, 172 + m:173 + m], in0=SWQ[:, 44 + m:45 + m],
                in1=SC[:, 21:22], op=ALU.mult), inc=True)
            b.wait("vector", tv1)
            tv2 = b.do("vector", lambda e, m=m: e.tensor_tensor(
                out=SWQ[:, 240:241], in0=SWQ[:, 108 + m:109 + m],
                in1=SWQ[:, 172 + m:173 + m], op=ALU.mult), inc=True)
            b.wait("vector", tv2)
            pre2 = b.do("vector", lambda e, m=m: e.tensor_scalar(
                out=SWQ[:, 224 + m:225 + m], in0=SWQ[:, 240:241],
                scalar1=-128.0, scalar2=None, op0=ALU.mult), inc=True)
        b.wait("sync", o_done)
        for tok_ in hpark_toks:
            b.wait("sync", tok_)
        tok_hr = b.dma("sync", RC[:, 0:4096], t["h_park"][:, :], st="park", depth=2)

        def hq_src(i):
            if i < 20:
                return hdst(i)
            return RC[:, (i - 20) * T:(i - 19) * T]

        self.tap("tap_h", self.RD[:, :], c0=0, cols=16384)
        self.tap("tap_h", self.RB[:, :], c0=16384, cols=16384)
        self.tap("tap_h", self.RE[:, 8192:16384], c0=32768, cols=8192)
        self.tap("tap_sc", SC[:, 0:32], cols=32)
        self._hq_list = [(i, half) for i in range(NIG) for half in range(2)]
        self._hq_i = 0
        self._hq_tok = None
        _hq_first = [True]

        def hq_step(n):
            for _ in range(n):
                if self._hq_i >= len(self._hq_list):
                    return
                i, half = self._hq_list[self._hq_i]
                self._hq_i += 1
                if _hq_first[0]:
                    b.wait("vector", tok_hr)
                    b.wait("vector", pre2)
                    _hq_first[0] = False
                src_ap = hq_src(i)[:, half * 1024:(half + 1) * 1024]
                tk = self.quant_piece(src_ap, src_ap, SC[:, 22:23], out_signed=False,
                                      dep_toks=[self.wq_prev_qs] if self.wq_prev_qs else [])
                self.wq_prev_qs = tk
                self._hq_tok = tk
        self.hq_step = hq_step

        b.wait("tensor", tshb)
        b.wait("vector", self.ld_toks["pb"])
        for tok_ in self.wdq_done:
            b.wait("gpsimd", tok_)
            b.wait("sync", tok_)
        s2_ev = None
        pj_war = {}
        xr_war = {}
        s2st_war = None
        s2_park_toks = []
        for tc in range(NTC):
            for c2 in range(8):
                sl = c2 % 2
                pj = RS[:, sl * 1024: (sl + 1) * 1024]
                if (tc, c2) != (0, 0):
                    if pj_war.get(sl) is not None:
                        b.wait("gpsimd", pj_war[sl])
                tpj = b.dma("gpsimd", pj,
                            t["projw_t"].ap()[:, :].rearrange("(c p) n -> c p n", p=128)[c2],
                            st="pjl", depth=2)
                b.wait("tensor", tpj)
                b.wait("tensor", o_done)
                if s2_ev is not None and c2 == 0:
                    b.wait("tensor", s2_ev)
                mm = None
                for m in range(8):
                    mm = b.do("tensor", lambda e, tc=tc, c2=c2, m=m, pj=pj,
                              st=(c2 == 0), sp=(c2 == 7): e.matmul(
                        ps[m][:, :], pj[:, m * 128:(m + 1) * 128],
                        RA[:, c2 * T + tc * TCW: c2 * T + (tc + 1) * TCW],
                        start=st, stop=sp), inc=(m == 7))
                pj_war[sl] = mm
                hq_step(2)
            b.wait("vector", mm)
            txrs = {}
            for m in range(8):
                xsl = RS[:, 2048 + (m % 4) * 512: 2048 + (m % 4 + 1) * 512]
                if xr_war.get(m % 4) is not None:
                    b.wait("sync", xr_war[m % 4])
                txrs[m] = (xsl, b.dma("sync", xsl,
                           t["xn2_park"][:, m * T + tc * TCW: m * T + (tc + 1) * TCW],
                           st="xn2r", depth=4))
                if m % 4 != 3 and m != 7:
                    continue
                for m2 in range(m - (m % 4), m + 1):
                    xsl2, txr2 = txrs[m2]
                    b.wait("vector", txr2)
                    if m2 < 4:
                        dst = RE[:, m2 * T + tc * TCW: m2 * T + (tc + 1) * TCW]
                    else:
                        dst = RS[:, 4096:4608]
                        if s2st_war is not None:
                            b.wait("vector", s2st_war)
                    s2_ev = b.do("vector", lambda e, m2=m2, tc=tc, d=dst, x=xsl2: e.scalar_tensor_tensor(
                        out=d, in0=ps[m2][:, :], scalar=self.BIAS[:, 48 + m2:49 + m2],
                        in1=x, op0=ALU.add, op1=ALU.add), inc=True)
                    xr_war[m2 % 4] = s2_ev
                    if m2 >= 4:
                        b.wait("sync", s2_ev)
                        tpk = b.dma("sync", t["s2_park"][:, (m2 - 4) * T + tc * TCW:
                                                         (m2 - 4) * T + (tc + 1) * TCW],
                                    dst, st="s2p", depth=2)
                        s2_park_toks.append(tpk)
                        s2st_war = tpk
        s2_done = s2_ev
        self.tap("tap_s2", RE[:, 0:8192])
        hq_step(len(self._hq_list))
        hq_tok = self._hq_tok

        def g2dst(m):
            if m < 6:
                return RA[:, 4096 + m * T: 4096 + (m + 1) * T]
            return RC[:, 4096 + (m - 6) * T: 4096 + (m - 5) * T]

        dn_war = {}
        g2_ev = None
        ics = [(a, min(a + 2, NIG)) for a in range(0, NIG, 2)]
        b.wait("sync", pj_war[0])
        b.wait("sync", pj_war[1])
        for tok_ in xr_war.values():
            b.wait("sync", tok_)
        for tc in range(NTC):
            for (ic0, ic1) in ics:
                sl = (ic0 // 2) % 2
                wd = RS[:, sl * 2048: (sl + 1) * 2048]
                if dn_war.get(sl) is not None:
                    b.wait("sync", dn_war[sl])
                twd = b.dma("sync", wd[:, 0:(ic1 - ic0) * 1024],
                            t["wdq_hbm"][:, ic0 * H:ic1 * H], st="wdl", depth=2)
                b.wait("tensor", twd)
                b.wait("tensor", hq_tok)
                if g2_ev is not None and ic0 == 0:
                    b.wait("tensor", g2_ev)
                mm = None
                for ic in range(ic0, ic1):
                    for m in range(8):
                        mm = b.do("tensor", lambda e, tc=tc, ic=ic, ic0=ic0, m=m, wd=wd,
                                  st=(ic == 0), sp=(ic == NIG - 1): e.matmul(
                            ps[m][:, :], wd[:, (ic - ic0) * 1024 + m * 128:
                                            (ic - ic0) * 1024 + (m + 1) * 128],
                            hq_src(ic)[:, tc * TCW:(tc + 1) * TCW],
                            start=st, stop=sp), inc=(m == 7 and ic == ic1 - 1))
                dn_war[sl] = mm
            b.wait("scalar", mm)
            b.wait("scalar", pre2)
            for m in range(8):
                g2_ev = b.do("scalar", lambda e, tc=tc, m=m: e.activation(
                    out=g2dst(m)[:, tc * TCW:(tc + 1) * TCW], in_=ps[m][:, :],
                    func=AF.Silu, bias=SWQ[:, 224 + m:225 + m],
                    scale=SWQ[:, 172 + m:173 + m]), inc=(m == 7))
        g2_done = g2_ev
        self.tap("tap_g2", None) if False else None

        b.wait("sync", g2_done)
        for tok_ in s2_park_toks:
            b.wait("sync", tok_)
        tok_x1r = b.dma("sync", RD[:, :], t["x1_park"][:, :], st="park", depth=2)
        tok_s2r = b.dma("sync", RB[:, 0:8192], t["s2_park"][:, :], st="park", depth=2)
        OUTF = RC[:, 8192:16384].bitcast(F32)
        out_war = {}
        last_store = None
        for f in range(NF):
            s2src = (RE[:, f * T:(f + 1) * T] if f < 4
                     else RB[:, (f - 4) * T:(f - 3) * T])
            slot = f % 2
            if self.sq_war is not None:
                b.wait("vector", self.sq_war)
            b.wait("vector", g2_done)
            b.wait("vector", tok_x1r)
            b.wait("vector", tok_s2r)
            t1 = b.do("vector", lambda e, f=f, s2=s2src: e.tensor_tensor(
                out=self.SQ[:, :], in0=s2, in1=g2dst(f), op=ALU.mult), inc=True)
            b.wait("vector", t1)
            if out_war.get(slot) is not None:
                b.wait("vector", out_war[slot])
            t2 = b.do("vector", lambda e, f=f, slot=slot: e.tensor_tensor(
                out=OUTF[:, slot * 2048:(slot + 1) * 2048],
                in0=RD[:, f * T:(f + 1) * T], in1=self.SQ[:, :], op=ALU.add), inc=True)
            self.sq_war = t2
            b.wait("sync", t2)
            last_store = b.dma("sync", t["y_t"].ap().rearrange("(c p) n -> p c n", p=128)[:, f, :],
                               OUTF[:, slot * 2048:(slot + 1) * 2048], st="out", depth=2)
            out_war[slot] = last_store
            self.st_toks.append(last_store)

    def wdquant(self):
        b, t = self.b, self.t
        RS, SWQ = self.RS, self.SWQ
        wd_r = t["wd"].ap().rearrange("(c p) n -> c p n", p=128)
        ps7 = self.ps[7][:, :].bitcast(BF16)
        wdqh = t["wdq_hbm"].ap().rearrange("p (i h) -> p i h", h=H)
        stg = RS[:, 2816:3328]
        prev_in = {}
        prev_ttr = {}
        tcs = None
        ttr = None
        sts = None
        b.wait("gpsimd", self.tok_x1)
        for ic in range(NF):
            sl = 0
            inb = RS[:, 0:2816]
            if prev_in.get(sl) is not None:
                b.wait("gpsimd", prev_in[sl])
            if prev_ttr.get(sl) is not None:
                b.wait("gpsimd", prev_ttr[sl])
            tin = b.dma("gpsimd", inb, wd_r[ic], st="wdi", depth=2)
            b.wait("vector", tin)
            rmax = SWQ[:, 240:241]; rrec = SWQ[:, 241:242]; rw127 = SWQ[:, 242:243]
            tv = b.do("vector", lambda e, i=inb: e.reduce_max(
                out=rmax, in_=i, axis=mybir.AxisListType.X, apply_absolute_value=True), inc=True)
            b.wait("vector", tv)
            tv = b.do("vector", lambda e: e.reciprocal(out=rrec, in_=rmax), inc=True)
            b.wait("vector", tv)
            tv = b.do("vector", lambda e: e.tensor_scalar(
                out=rw127, in0=rrec, scalar1=127.0, scalar2=None, op0=ALU.mult), inc=True)
            b.do("vector", lambda e, ic=ic: e.tensor_scalar(
                out=SWQ[:, 44 + ic:45 + ic], in0=rmax, scalar1=1.0 / 127.0, scalar2=None,
                op0=ALU.mult))
            b.wait("vector", tv)
            qtok = None
            for pc in range(3):
                w = 1024 if pc < 2 else 768
                piece = inb[:, pc * 1024: pc * 1024 + w]
                deps = [x for x in (self.wq_prev_qs, qtok) if x is not None]
                qtok = self.quant_piece(piece, piece, rw127, out_signed=True,
                                        dep_toks=deps, width=w)
                self.wq_prev_qs = qtok
            b.wait("vector", qtok)
            tcs = b.do("vector", lambda e, ic=ic, i=inb: e.tensor_reduce(
                out=SWQ[:, 108 + ic:109 + ic], in_=i, axis=mybir.AxisListType.X,
                op=ALU.add), inc=True)
            prev_in[sl] = tcs
            b.wait("tensor", qtok)
            for g in range(6):
                icc0, icc1 = g * 4, min(g * 4 + 4, NIG)
                npc = icc1 - icc0
                if self.ps6_war.get(7) is not None:
                    b.wait("tensor", self.ps6_war[7])
                for icc in range(icc0, icc1):
                    ttr = b.do("tensor", lambda e, icc=icc, icc0=icc0, i=inb: e.transpose(
                        ps7[:, (icc - icc0) * 128:(icc - icc0 + 1) * 128],
                        i[:, icc * 128:(icc + 1) * 128], self.EYE[:, :]),
                        inc=(icc == icc1 - 1))
                prev_ttr[sl] = ttr
                b.wait("scalar", ttr)
                if sts is not None:
                    b.wait("scalar", sts)
                tse = b.do("scalar", lambda e, npc=npc: e.copy(
                    out=stg[:, 0:npc * 128], in_=ps7[:, 0:npc * 128]), inc=True)
                self.ps6_war[7] = tse
                b.wait("sync", tse)
                sts = b.dma("sync", wdqh[:, icc0:icc1, ic * 128:(ic + 1) * 128],
                            stg[:, 0:npc * 128].rearrange("p (i c) -> p i c", c=128),
                            st="wqo", depth=2)
                self.wq_store_toks.append(sts)
        self.wd_v_done = tcs
        self.wdq_done = [tcs, prev_ttr[0]] + self.wq_store_toks[-2:]


    def finish(self):
        b = self.b
        for s, v in list(b.cnt.items()):
            if v > 0:
                b.wait("sync", (s, v))


def build_program(debug=False, upto="full", ncores=NCORES, attn_js=None, attn_hps=None):
    nc = bass.Bass("TRN2", target_bir_lowering=False)
    mk = MK(nc, debug=debug, upto=upto)
    mk.ncores = ncores
    if attn_js is not None:
        mk.attn_js = attn_js
    if attn_hps is not None:
        mk.attn_hps = attn_hps
    mk.build()
    return nc, mk


def kernel(**inputs):
    from concourse.bass_utils import run_bass_kernel_spmd
    nc, m = build_program(debug=False, upto="full")
    in_maps = stage_inputs(inputs)
    r = run_bass_kernel_spmd(nc, in_maps, core_ids=list(range(NCORES)))
    out = np.zeros((NCORES, T, H), np.float32)
    for c in range(NCORES):
        out[c] = np.asarray(r.results[c]["y_t"], np.float32).T
    return out


# version 15
# speedup vs baseline: 1.0206x; 1.0206x over previous
import numpy as np
import ml_dtypes
from contextlib import ExitStack

import os
import concourse.bass as bass
import concourse.mybir as mybir

MKSKIP = set(os.environ.get("MKSKIP", "").split(","))

F32 = mybir.dt.float32
BF16 = mybir.dt.bfloat16
F16 = mybir.dt.float16
I16 = mybir.dt.int16
I32 = mybir.dt.int32
AF = mybir.ActivationFunctionType
ALU = mybir.AluOpType

H = 1024
T = 2048
NF = 8
NTC = 4
TCW = 512
F1 = 4096
NF1 = 32
IG = 2816
NIG = 22
NHP = 8
NB = 32
EPS = 1e-5

NCORES = 8


class Stream:

    def __init__(self, bld, queue, name, depth):
        self.b = bld
        self.q = queue
        self.depth = depth
        self.names = []
        for i in range(depth):
            nm = "%s%d" % (name, i)
            self.names.append(nm)
            bld.sems[nm] = bld.stack.enter_context(bld.nc.semaphore(nm))
            bld.cnt[nm] = 0
        self.i = 0

    def dma(self, out, in_):
        b = self.b
        k = self.i % self.depth
        nm = self.names[k]
        rounds = self.i // self.depth
        if rounds > 0:
            b.wait(self.q, (nm, 16 * rounds))
        b.ninstr += 1
        h = b.sems[nm]
        b.ops[self.q].append(
            lambda e, o=out, i_=in_, h=h: e.dma_start(out=o, in_=i_).then_inc(h, 16))
        b.cnt[nm] += 16
        self.i += 1
        return (nm, 16 * (rounds + 1))


class Bld:
    ENGS = ("sync", "gpsimd", "vector", "scalar", "tensor")
    ENG2SEM = {"vector": "pv", "scalar": "psc", "tensor": "pt", "gpsimd": "pg"}

    def __init__(self, nc, stack):
        self.nc = nc
        self.stack = stack
        self.ops = {e: [] for e in self.ENGS}
        self.sems = {}
        for s in ("pv", "psc", "pt", "pg"):
            self.sems[s] = stack.enter_context(nc.semaphore(s))
        self.cnt = {s: 0 for s in self.sems}
        self.lastw = {}
        self.ninstr = 0
        self.streams = {}

    def stream(self, queue, name, depth=2):
        if name not in self.streams:
            self.streams[name] = Stream(self, queue, name, depth)
        return self.streams[name]

    def do(self, eng, fn, inc=False):
        self.ninstr += 1
        if inc:
            s = self.ENG2SEM[eng]
            self.cnt[s] += 1
            tok = (s, self.cnt[s])
            h = self.sems[s]
            self.ops[eng].append(lambda e, fn=fn, h=h: fn(e).then_inc(h, 1))
            return tok
        self.ops[eng].append(fn)
        return None

    def dma(self, q, out, in_, st=None, depth=2):
        return self.stream(q, st or ("_d_" + q), depth).dma(out, in_)

    def wait(self, eng, tok):
        if tok is None:
            return
        s, v = tok
        key = (eng, s)
        if self.lastw.get(key, 0) >= v:
            return
        self.lastw[key] = v
        self.ninstr += 1
        h = self.sems[s]
        self.ops[eng].append(lambda e, h=h, v=v: e.wait_ge(h, v))

    def waits(self, eng, toks):
        for t in toks:
            self.wait(eng, t)

    def mark(self, eng):
        s = self.ENG2SEM[eng]
        self.cnt[s] += 1
        tok = (s, self.cnt[s])
        h = self.sems[s]
        self.ninstr += 1
        self.ops[eng].append(lambda e, h=h: e.sem_inc(h, 1))
        return tok

    def finalize(self):
        with self.nc.Block() as block:
            for eng in self.ENGS:
                lst = self.ops[eng]

                def make(lst):
                    def _(e):
                        for f in lst:
                            f(e)
                    return _
                getattr(block, eng)(make(lst))


def stage_inputs(inputs):
    f32 = np.float32
    x = np.asarray(inputs["L0_attn_hidden_states"], f32)

    def T_(a):
        return np.ascontiguousarray(np.asarray(a, f32).T)

    def col8(a):
        a = np.asarray(a, f32)
        return np.ascontiguousarray(a.reshape(-1, 128).T)

    shared = {
        "wq_t": T_(inputs["L0_attn_Wq"]),
        "wk_t": T_(inputs["L0_attn_Wk"]),
        "wv_t": T_(inputs["L0_attn_Wv"]),
        "bq": col8(inputs["L0_attn_bq"]),
        "bk": col8(inputs["L0_attn_bk"]),
        "bv": col8(inputs["L0_attn_bv"]),
        "bvrow": np.asarray(inputs["L0_attn_bv"], f32).reshape(1, H),
        "lng": col8(inputs["L0_ffn_ln_gamma"]),
        "lnb": col8(inputs["L0_ffn_ln_beta"]),
        "fc1q": np.ascontiguousarray(np.asarray(inputs["L0_ffn_fc1_q"], np.int32)),
        "fc1s": np.ascontiguousarray(np.repeat(np.asarray(inputs["L0_ffn_fc1_scales"], f32), 128, axis=0)),
        "fc1b": col8(inputs["L0_ffn_fc1_bias"]),
        "fc2q": np.ascontiguousarray(np.asarray(inputs["L0_ffn_fc2_q"], np.int32)),
        "fc2s": np.ascontiguousarray(np.repeat(np.asarray(inputs["L0_ffn_fc2_scales"], f32), 128, axis=0)),
        "fc2b": col8(inputs["L0_ffn_fc2_bias"]),
        "nw": col8(inputs["L1_attn_norm_w"]),
        "nb": col8(inputs["L1_attn_norm_b"]),
        "qkvw_t": T_(inputs["L1_attn_qkv_w"]),
        "bq1": col8(np.asarray(inputs["L1_attn_qkv_b"], f32)[0:H]),
        "bkrow": np.asarray(inputs["L1_attn_qkv_b"], f32)[H:2 * H].reshape(1, H),
        "bvrow1": np.asarray(inputs["L1_attn_qkv_b"], f32)[2 * H:3 * H].reshape(1, H),
        "projw_t": T_(inputs["L1_attn_proj_w"]),
        "pb": col8(inputs["L1_attn_proj_b"]),
        "clipx": np.asarray(inputs["L1_ffn_clip_x"], f32).reshape(1, 1),
        "clipd": np.asarray(inputs["L1_ffn_clip_down"], f32).reshape(1, 1),
        "wg": np.ascontiguousarray(np.asarray(inputs["L1_ffn_w_gate"], f32)),
        "wu": np.ascontiguousarray(np.asarray(inputs["L1_ffn_w_up"], f32)),
        "wd": np.ascontiguousarray(np.asarray(inputs["L1_ffn_w_down"], f32)),
        "eye": np.eye(128, dtype=ml_dtypes.bfloat16),
        "ones_c_bf": np.ones((128, 1), ml_dtypes.bfloat16),
        "ones_r_f": np.ones((1, 128), f32),
        "ones_r_bf": np.ones((1, 128), ml_dtypes.bfloat16),
    }
    in_maps = []
    for c in range(NCORES):
        d = dict(shared)
        d["x_t"] = np.ascontiguousarray(x[c].T)
        in_maps.append(d)
    return in_maps


def declare_dram(nc):
    t = {}

    def inp(name, shape, dtype=F32):
        t[name] = nc.dram_tensor(name, shape, dtype, kind="ExternalInput")

    inp("x_t", [H, T])
    for n in ("wq_t", "wk_t", "wv_t"):
        inp(n, [H, H])
    for n in ("bq", "bk", "bv", "lng", "lnb", "fc2b", "nw", "nb", "bq1", "pb"):
        inp(n, [128, 8])
    inp("bvrow", [1, H]); inp("bkrow", [1, H]); inp("bvrow1", [1, H])
    inp("fc1q", [H, F1], I32); inp("fc1s", [H, F1]); inp("fc1b", [128, 32])
    inp("fc2q", [F1, H], I32); inp("fc2s", [F1, H])
    inp("qkvw_t", [H, 3 * H]); inp("projw_t", [H, H])
    inp("clipx", [1, 1]); inp("clipd", [1, 1])
    inp("wg", [IG, H]); inp("wu", [IG, H]); inp("wd", [H, IG])
    inp("eye", [128, 128], BF16)
    inp("ones_c_bf", [128, 1], BF16)
    inp("ones_r_f", [1, 128])
    inp("ones_r_bf", [1, 128], BF16)

    t["xn_park"] = nc.dram_tensor("xn_park", [128, NF * T], BF16)
    t["x1_park"] = nc.dram_tensor("x1_park", [128, NF * T], BF16)
    t["xn2_park"] = nc.dram_tensor("xn2_park", [128, NF * T], BF16)
    t["s2_park"] = nc.dram_tensor("s2_park", [128, 4 * T], BF16)
    t["w2u_hbm"] = nc.dram_tensor("w2u_hbm", [128, NF1 * H], BF16)
    t["wgq_hbm"] = nc.dram_tensor("wgq_hbm", [128, NF * IG], BF16)
    t["wuq_hbm"] = nc.dram_tensor("wuq_hbm", [128, NF * IG], BF16)
    t["wdq_hbm"] = nc.dram_tensor("wdq_hbm", [128, NIG * H], BF16)
    t["h_park"] = nc.dram_tensor("h_park", [128, 2 * T], BF16)
    t["ccx_in"] = nc.dram_tensor("ccx_in", [1, 1], F32)
    t["ccx_out"] = nc.dram_tensor("ccx_out", [1, 1], F32)
    t["cch_in"] = nc.dram_tensor("cch_in", [1, 1], F32)
    t["cch_out"] = nc.dram_tensor("cch_out", [1, 1], F32)

    t["y_t"] = nc.dram_tensor("y_t", [H, T], F32, kind="ExternalOutput")
    return t


class MK:
    def __init__(self, nc, debug=False, upto="full"):
        self.ncores = NCORES
        self.attn_js = list(range(NB))
        self.attn_hps = NHP
        self.nc = nc
        self.debug = debug
        self.upto = upto
        self.t = declare_dram(nc)
        self.taps = {}
        if debug:
            for name, shape, dt in [
                ("tap_xn", [128, NF * T], BF16), ("tap_h1", [128, NF * T], BF16),
                ("tap_g", [128, NF * T], BF16), ("tap_x1", [128, NF * T], BF16),
                ("tap_xn2", [128, NF * T], BF16), ("tap_a", [128, NF * H], BF16),
                ("tap_s2", [128, 4 * T], BF16), ("tap_xq", [128, NF * T], BF16),
                ("tap_h", [128, NIG * T], BF16), ("tap_g2", [128, NF * T], BF16),
                ("tap_y1", [128, NF1 * TCW], BF16), ("tap_w1", [128, NF * F1], BF16),
                ("tap_qt", [128, T], BF16), ("tap_kt", [128, T], BF16),
                ("tap_vt", [128, 16 * 128], BF16), ("tap_st", [128, T], F32),
                ("tap_wgq", [128, NF * IG], BF16),
                ("tap_bias", [128, 112], F32), ("tap_sc", [128, 64], F32),
            ]:
                self.taps[name] = nc.dram_tensor(name, shape, dt, kind="ExternalOutput")

    def tap(self, name, src_ap, c0=0, cols=None, barrier=True):
        if not self.debug or name not in self.taps:
            return None
        for s in ("pv", "psc", "pt", "pg"):
            if self.b.cnt[s] > 0:
                self.b.wait("sync", (s, self.b.cnt[s]))
        dst = self.taps[name]
        ap = dst[:, c0:c0 + cols] if cols is not None else dst[:, :]
        tok = self.b.dma("sync", ap, src_ap, st="tap", depth=2)
        self.st_toks.append(tok)
        if barrier:
            for e in ("vector", "scalar", "tensor", "gpsimd"):
                self.b.wait(e, tok)
        return tok

    def rg(self, region, c0, c1):
        return region[:, c0:c1]

    def chunk(self, region, i, width=T):
        return region[:, i * width:(i + 1) * width]

    def build(self):
        nc = self.nc
        with ExitStack() as stack:
            b = Bld(nc, stack)
            self.b = b
            sb = lambda n, s, d=BF16: stack.enter_context(nc.sbuf_tensor(n, s, d))
            self.RA = sb("RA", [128, 16384])
            self.RB = sb("RB", [128, 16384])
            self.RC = sb("RC", [128, 16384])
            self.RD = sb("RD", [128, 16384])
            self.RE = sb("RE", [128, 16384])
            self.RS = sb("RS", [128, 6144])
            self.SQ = sb("SQ", [128, 2048])
            self.QS = sb("QS", [128, 2048], F32)
            self.ROWS = sb("ROWS", [128, 3072], F32)
            self.BIAS = sb("BIAS", [128, 112], F32)
            self.SWQ = sb("SWQ", [128, 256], F32)
            self.SC = sb("SC", [128, 64], F32)
            self.HMX = sb("HMX", [128, 96], F32)
            self.BQC = sb("BQC", [128, 8])
            self.EYE = sb("EYE", [128, 128])
            self.ONESC = sb("ONESC", [128, 1])
            self.ONESRF = sb("ONESRF", [1, 128], F32)
            self.ONESRB = sb("ONESRB", [1, 128])
            self.BVROW = sb("BVROW", [1, H])
            self.SCR = sb("SCR", [128, 32], F32)
            self.ps = [stack.enter_context(nc.psum_tensor("ps%d" % i, [128, 512], F32))
                       for i in range(8)]
            self.st_toks = []
            self.wq_sched = []
            self.ln_war = {}
            self.sq_war = None
            self.sq_sl_war = {}
            self.sq_mid_war = None
            self.rows_war = None
            self.bc_war = {}

            self.setup_consts()
            self.phase_A()
            if self.upto != "A":
                self.phase_ffn()
                if self.upto != "ffn":
                    self.phase_attn()
                    if self.upto not in ("l0",):
                        self.phase_l1()
            self.finish()
            b.finalize()
        return self

    def quant_piece(self, src_ap, dst_ap, recip_ap, out_signed, dep_toks=(), width=1024):
        b = self.b
        QS = self.QS
        assert width <= 1024
        v = QS[:, 0:512].bitcast(F16)[:, 0:width]
        qi = QS[:, 512:1024].bitcast(I16)[:, 0:width]
        d = QS[:, 1024:1536].bitcast(F16)[:, 0:width]
        q2 = QS[:, 1536:2048].bitcast(F16)[:, 0:width]
        q3 = QS[:, 0:512].bitcast(F16)[:, 0:width]
        for t in dep_toks:
            b.wait("vector", t)
        t1 = b.do("vector", lambda e: e.tensor_scalar(
            out=v, in0=src_ap, scalar1=recip_ap, scalar2=128.0,
            op0=ALU.mult, op1=ALU.add), inc=True)
        b.wait("vector", t1)
        t2 = b.do("vector", lambda e: e.tensor_copy(out=qi, in_=v), inc=True)
        b.wait("vector", t2)
        t3 = b.do("vector", lambda e: e.tensor_tensor(
            out=d, in0=v, in1=qi, op=ALU.subtract), inc=True)
        b.wait("vector", t3)
        t4 = b.do("vector", lambda e: e.scalar_tensor_tensor(
            out=q2, in0=d, scalar=0.5, in1=qi, op0=ALU.is_gt, op1=ALU.add), inc=True)
        b.wait("vector", t4)
        if out_signed:
            t5 = b.do("vector", lambda e: e.tensor_scalar(
                out=q3, in0=q2, scalar1=0.0, scalar2=255.0,
                op0=ALU.max, op1=ALU.min), inc=True)
            b.wait("scalar", t5)
            t6 = b.do("scalar", lambda e: e.activation(
                out=dst_ap, in_=q3, func=AF.Copy, bias=-128.0), inc=True)
            return t6
        t5 = b.do("vector", lambda e: e.tensor_scalar(
            out=dst_ap, in0=q2, scalar1=0.0, scalar2=255.0,
            op0=ALU.max, op1=ALU.min), inc=True)
        return t5

    def setup_consts(self):
        b, t = self.b, self.t
        self.ld_toks = {}
        for name, dst in [("eye", self.EYE[:, :]), ("ones_c_bf", self.ONESC[:, :]),
                          ("ones_r_bf", self.ONESRB[:, :])]:
            self.ld_toks[name] = b.dma("sync", dst, t[name][:, :])
        self.ld_toks["ones_r_f"] = b.dma("sync", self.ONESRF[:, :], t["ones_r_f"][:, :])
        BIAS = self.BIAS
        tok_bq = b.dma("sync", self.SCR[:, 0:8], t["bq"][:, :])
        tok_f2 = b.dma("sync", self.SCR[:, 8:16], t["fc2b"][:, :])
        for name, c in [("bk", 16), ("lng", 24), ("lnb", 32), ("pb", 48),
                        ("nw", 56), ("nb", 64), ("bv", 104)]:
            self.ld_toks[name] = b.dma("sync", BIAS[:, c:c + 8], t[name][:, :])
        self.ld_toks["fc1b"] = b.dma("sync", BIAS[:, 72:104], t["fc1b"][:, :])
        b.wait("vector", tok_bq)
        tq = b.do("vector", lambda e: e.tensor_scalar(
            out=BIAS[:, 8:16], in0=self.SCR[:, 0:8], scalar1=0.125, scalar2=None,
            op0=ALU.mult), inc=True)
        b.wait("vector", tok_f2)
        tf = b.do("vector", lambda e: e.tensor_scalar(
            out=BIAS[:, 40:48], in0=self.SCR[:, 8:16], scalar1=0.5, scalar2=None,
            op0=ALU.mult), inc=True)
        self.ld_toks["bq8"] = tq
        self.ld_toks["fc2bh"] = tf
        rb = self.ROWS[:, :].bitcast(BF16)
        self.r_bk = rb[32:33, 0:1024]
        self.r_bv1 = rb[32:33, 1024:2048]
        self.r_sk = rb[32:33, 2048:3072]
        self.r_sv = rb[32:33, 3072:4096]
        self.r_svp = rb[32:33, 4096:5120]
        self.ld_toks["bkrow"] = b.dma("gpsimd", self.r_bk, t["bkrow"][:, :])
        self.ld_toks["bvrow1"] = b.dma("gpsimd", self.r_bv1, t["bvrow1"][:, :])
        self.ld_toks["bvrow"] = b.dma("gpsimd", self.BVROW[:, :], t["bvrow"][:, :])
        self.ld_toks["bq1c"] = b.dma("gpsimd", self.BQC[:, :], t["bq1"][:, :])
        tms = b.do("vector", lambda e: e.memset(self.SC[:, :], 0.0), inc=True)
        b.wait("sync", tms)
        self.ld_toks["clip"] = b.dma("sync", self.SC[0:1, 0:1], t["clipx"][:, :])
        self.ld_toks["clipd"] = b.dma("sync", self.SC[0:1, 1:2], t["clipd"][:, :])

    def emit_ln(self, src_region, dst_region, gamma_col=None, beta_col=None,
                src_tok=None, dst_name="", interleave=None):
        b = self.b
        ps = self.ps
        R1 = self.ROWS
        last = None
        pending = None
        if self.sq_war is not None:
            b.wait("scalar", self.sq_war)
        if src_tok is not None:
            b.wait("scalar", src_tok)
        for tc in range(NTC):
            ts_ = slice(tc * TCW, (tc + 1) * TCW)
            sbank = ps[tc % 2]
            bbank_r = ps[4 + (tc % 2)]
            bbank_m = ps[6 + (tc % 2)]
            if self.ln_war.get(id(sbank)) is not None:
                b.wait("tensor", self.ln_war[id(sbank)])
            for f in range(NF):
                src = src_region[:, f * T + tc * TCW: f * T + (tc + 1) * TCW]
                if src_tok is not None:
                    b.wait("tensor", src_tok)
                b.do("tensor", lambda e, s=src, ps_=sbank, st=(f == 0), sp=(f == NF - 1): e.matmul(
                    ps_[0:1, :], self.ONESC[:, :], s, start=st, stop=sp))
            stats_done = None
            for f in range(NF):
                src = src_region[:, f * T + tc * TCW: f * T + (tc + 1) * TCW]
                so = 512 + (f % 2) * 1024
                if self.sq_sl_war.get(f % 2) is not None:
                    b.wait("scalar", self.sq_sl_war[f % 2])
                tsq = b.do("scalar", lambda e, s=src, so=so: e.activation(
                    out=self.SQ[:, so:so + TCW], in_=s, func=AF.Square), inc=True)
                b.wait("tensor", tsq)
                tmm = b.do("tensor", lambda e, ps_=sbank, so=so,
                           st=(f == 0), sp=(f == NF - 1): e.matmul(
                    ps_[32:33, :], self.ONESC[:, :], self.SQ[:, so:so + TCW],
                    start=st, stop=sp), inc=True)
                self.sq_sl_war[f % 2] = tmm
                stats_done = tmm
            b.wait("vector", stats_done)
            rstd, mr = R1[0:1, 0:512], R1[0:1, 512:1024]
            mu, msq = R1[0:1, 1024:1536], R1[0:1, 1536:2048]
            mu2, vep = R1[0:1, 2048:2560], R1[0:1, 2048:2560]
            var, rec = R1[0:1, 2560:3072], R1[0:1, 2560:3072]
            if self.rows_war is not None:
                b.wait("vector", self.rows_war)
            t_mu = b.do("vector", lambda e, ps_=sbank: e.tensor_scalar(
                out=mu, in0=ps_[0:1, :], scalar1=1.0 / H, scalar2=None, op0=ALU.mult), inc=True)
            t_ms = b.do("vector", lambda e, ps_=sbank: e.tensor_scalar(
                out=msq, in0=ps_[32:33, :], scalar1=1.0 / H, scalar2=None, op0=ALU.mult), inc=True)
            self.ln_war[id(sbank)] = t_ms
            b.wait("vector", t_mu)
            tv = b.do("vector", lambda e: e.tensor_tensor(out=mu2, in0=mu, in1=mu, op=ALU.mult), inc=True)
            b.wait("vector", tv)
            tv = b.do("vector", lambda e: e.tensor_tensor(out=var, in0=msq, in1=mu2, op=ALU.subtract), inc=True)
            b.wait("vector", tv)
            tv = b.do("vector", lambda e: e.tensor_scalar(
                out=vep, in0=var, scalar1=EPS, scalar2=None, op0=ALU.add), inc=True)
            b.wait("vector", tv)
            tv = b.do("vector", lambda e: e.reciprocal(out=rec, in_=vep), inc=True)
            b.wait("scalar", tv)
            tsc = b.do("scalar", lambda e: e.activation(out=rstd, in_=rec, func=AF.Sqrt), inc=True)
            b.wait("vector", tsc)
            trows = b.do("vector", lambda e: e.tensor_tensor(out=mr, in0=mu, in1=rstd, op=ALU.mult), inc=True)
            b.wait("tensor", tsc)
            b.wait("tensor", trows)
            if self.bc_war.get(tc % 2) is not None:
                b.wait("tensor", self.bc_war[tc % 2])
            b.do("tensor", lambda e, p_=bbank_r: e.matmul(
                p_[:, :], self.ONESRF[:, :], rstd, start=True, stop=True))
            tb = b.do("tensor", lambda e, p_=bbank_m: e.matmul(
                p_[:, :], self.ONESRF[:, :], mr, start=True, stop=True), inc=True)
            self.rows_war = tb

            def emit_apply(tc, tb, bbank_r, bbank_m):
                lastv = None
                for f in range(NF):
                    src = src_region[:, f * T + tc * TCW: f * T + (tc + 1) * TCW]
                    dst = dst_region[:, f * T + tc * TCW: f * T + (tc + 1) * TCW]
                    b.wait("vector", tb)
                    if self.sq_war is not None:
                        b.wait("vector", self.sq_war)
                    tv = b.do("vector", lambda e, s=src, p_=bbank_r: e.tensor_tensor(
                        out=self.SQ[:, 0:TCW], in0=s, in1=p_[:, :], op=ALU.mult), inc=True)
                    b.wait("vector", tv)
                    if gamma_col is None:
                        tv = b.do("vector", lambda e, d=dst, p_=bbank_m: e.tensor_tensor(
                            out=d, in0=self.SQ[:, 0:TCW], in1=p_[:, :], op=ALU.subtract), inc=True)
                        self.sq_war = tv
                    else:
                        if self.sq_mid_war is not None:
                            b.wait("vector", self.sq_mid_war)
                        tv = b.do("vector", lambda e, p_=bbank_m: e.tensor_tensor(
                            out=self.SQ[:, 1024:1024 + TCW], in0=self.SQ[:, 0:TCW],
                            in1=p_[:, :], op=ALU.subtract), inc=True)
                        self.sq_war = tv
                        b.wait("scalar", tv)
                        g = self.BIAS[:, gamma_col + f:gamma_col + f + 1]
                        bb = self.BIAS[:, beta_col + f:beta_col + f + 1]
                        tv = b.do("scalar", lambda e, d=dst, g=g, bb=bb: e.activation(
                            out=d, in_=self.SQ[:, 1024:1024 + TCW], func=AF.Identity,
                            scale=g, bias=bb), inc=True)
                        self.sq_mid_war = tv
                    lastv = tv
                return lastv

            if pending is not None:
                last = emit_apply(*pending)
                self.bc_war[pending[0] % 2] = last
            pending = (tc, tb, bbank_r, bbank_m)
            if interleave is not None:
                interleave()
        last = emit_apply(*pending)
        self.bc_war[pending[0] % 2] = last
        if gamma_col is not None:
            self.sq_war = last
        return last

    def phase_A(self):
        b, t = self.b, self.t
        xin = t["x_t"].ap().rearrange("(c p) n -> p c n", p=128)
        xa = self.RA[:, :].rearrange("p (c n) -> p c n", n=T)
        tok_x = b.dma("gpsimd", xa, xin, st="gx", depth=1)
        for k in ("ones_c_bf", "ones_r_f"):
            b.wait("tensor", self.ld_toks[k])
        b.wait("scalar", tok_x)
        b.wait("tensor", tok_x)
        b.wait("vector", tok_x)
        self.ffn_prelude_init()
        self.tok_ln1 = self.emit_ln(self.RA, self.RB,
                                    interleave=lambda: self.ffn_prelude_step(4))
        self.tap("tap_xn", self.RB[:, :])
        b.wait("vector", self.ld_toks["lng"])
        b.wait("vector", self.ld_toks["lnb"])
        self.tok_h1 = self.emit_ln(self.RB, self.RC, gamma_col=24, beta_col=32,
                                   interleave=lambda: self.ffn_prelude_step(8))
        self.tap("tap_h1", self.RC[:, :])
        b.wait("sync", self.tok_ln1)
        self.tok_xnpark = b.dma("sync", t["xn_park"][:, :], self.RB[:, :], st="park", depth=2)

    def w2_bounce_init(self):
        t = self.t
        self.w2q_r = t["fc2q"].ap().rearrange("(c p) n -> c p n", p=128)
        self.w2s_r = t["fc2s"].ap().rearrange("(c p) n -> c p n", p=128)
        self.w2_k = 0
        self.w2_slot_tok = {}
        self.w2_store_toks = []

    def w2_bounce_step(self, n):
        b, t = self.b, self.t
        RD = self.RD
        for _ in range(n):
            k = self.w2_k
            if k >= NF1:
                return
            self.w2_k += 1
            s = k % 4
            base = s * 4096
            qv = RD[:, base:base + 1024]
            sv = RD[:, base + 1024:base + 2048]
            ov = RD[:, base + 2048:base + 3072]
            if s in self.w2_slot_tok:
                b.wait("gpsimd", self.w2_slot_tok[s])
            t1 = b.dma("gpsimd", qv, self.w2q_r[k], st="w2i", depth=8)
            t2 = b.dma("gpsimd", sv, self.w2s_r[k], st="w2i", depth=8)
            b.wait("scalar", t1)
            t3 = b.do("scalar", lambda e, qv=qv, ov=ov: e.activation(
                out=ov, in_=qv, func=AF.Copy, bias=-128.0), inc=True)
            b.wait("vector", t3)
            b.wait("vector", t2)
            t4 = b.do("vector", lambda e, qv=qv, sv=sv, ov=ov: e.tensor_tensor(
                out=qv, in0=ov, in1=sv, op=ALU.mult), inc=True)
            b.wait("sync", t4)
            t5 = b.dma("sync", t["w2u_hbm"][:, k * 1024:(k + 1) * 1024], qv, st="w2o", depth=4)
            self.w2_slot_tok[s] = t5
            self.w2_store_toks.append(t5)

    def wquant_init(self):
        t = self.t
        self.wq_list = []
        wg_r = t["wg"].ap().rearrange("(c p) n -> c p n", p=128)
        wu_r = t["wu"].ap().rearrange("(c p) n -> c p n", p=128)
        for ic in range(NIG):
            self.wq_list.append(("wg", wg_r[ic], t["wgq_hbm"], ic, 0, 64))
        for ic in range(NIG):
            self.wq_list.append(("wu", wu_r[ic], t["wuq_hbm"], ic, 22, 86))
        self.wq_i = 0
        self.wq_prev_qs = None
        self.wq_slot_tok = {}
        self.wq_t_ttr = None
        self.wq_t_tcs = None
        self.wq_t_tse = None
        self.wq_t_sts = None
        self.wq_store_toks = []

    def wquant_step(self, n):
        b, t = self.b, self.t
        RS, SWQ = self.RS, self.SWQ
        for _ in range(n):
            if self.wq_i >= len(self.wq_list):
                return
            name, src, dsth, ic, scol, ccol = self.wq_list[self.wq_i]
            self.wq_i += 1
            sl = self.wq_i % 2
            inb = RS[:, sl * 1024: (sl + 1) * 1024]
            wqs = RS[:, 4096:5120]
            stg = RS[:, 5120:6144]
            if ("in", sl) in self.wq_slot_tok:
                b.wait("gpsimd", self.wq_slot_tok[("in", sl)])
            if getattr(self, "fc2_last", None) is not None:
                b.wait("gpsimd", self.fc2_last)
            tin = b.dma("gpsimd", inb, src, st="wqi", depth=2)
            b.wait("vector", tin)
            rmax = SWQ[:, 240:241]; rrec = SWQ[:, 241:242]; rw127 = SWQ[:, 242:243]
            tv = b.do("vector", lambda e, i=inb, o=rmax: e.reduce_max(
                out=o, in_=i, axis=mybir.AxisListType.X, apply_absolute_value=True), inc=True)
            b.wait("vector", tv)
            tv = b.do("vector", lambda e, o=rrec, i=rmax: e.reciprocal(out=o, in_=i), inc=True)
            b.wait("vector", tv)
            tv = b.do("vector", lambda e, o=rw127, i=rrec: e.tensor_scalar(
                out=o, in0=i, scalar1=127.0, scalar2=None, op0=ALU.mult), inc=True)
            b.do("vector", lambda e, o=SWQ[:, scol + ic:scol + ic + 1], i=rmax: e.tensor_scalar(
                out=o, in0=i, scalar1=1.0 / 127.0, scalar2=None, op0=ALU.mult))
            b.wait("vector", tv)
            deps = []
            if self.wq_prev_qs is not None:
                deps.append(self.wq_prev_qs)
            if self.wq_t_ttr is not None:
                deps.append(self.wq_t_ttr)
            if self.wq_t_tcs is not None:
                deps.append(self.wq_t_tcs)
            t6 = self.quant_piece(inb, wqs, rw127, out_signed=True, dep_toks=deps)
            self.wq_prev_qs = t6
            self.wq_slot_tok[("in", sl)] = t6
            b.wait("vector", t6)
            tcs = b.do("vector", lambda e, i=wqs, o=SWQ[:, ccol + ic:ccol + ic + 1]: e.tensor_reduce(
                out=o, in_=i, axis=mybir.AxisListType.X, op=ALU.add), inc=True)
            self.wq_t_tcs = tcs
            self.wq_v_done = tcs
            ps7 = self.ps[7][:, :].bitcast(BF16)
            if self.wq_t_tse is not None:
                b.wait("tensor", self.wq_t_tse)
            b.wait("tensor", t6)
            ttr = None
            for j in range(8):
                ttr = b.do("tensor", lambda e, j=j, i=wqs, p=ps7: e.transpose(
                    p[:, j * 128:(j + 1) * 128], i[:, j * 128:(j + 1) * 128],
                    self.EYE[:, :]), inc=(j == 7))
            self.wq_t_ttr = ttr
            b.wait("scalar", ttr)
            if self.wq_t_sts is not None:
                for _t in self.wq_t_sts:
                    b.wait("scalar", _t)
            tse = b.do("scalar", lambda e, p=ps7, s=stg: e.copy(
                out=s[:, 0:1024], in_=p[:, 0:1024]), inc=True)
            self.wq_t_tse = tse
            b.wait("sync", tse)
            dsthr = dsth.ap().rearrange("p (j i) -> p j i", i=IG)
            stl = [b.dma("sync", dsthr[:, 0:8, ic * 128:(ic + 1) * 128],
                         stg[:, 0:1024].rearrange("p (j c) -> p j c", c=128),
                         st="wqo", depth=2)]
            self.wq_store_toks += stl
            self.wq_t_sts = stl[-1:]

    def ffn_prelude_init(self):
        b, t = self.b, self.t
        self.wquant_init()
        self.pr_w2q = t["fc2q"].ap().rearrange("(c p) n -> c p n", p=128)
        self.pr_w2s = t["fc2s"].ap().rearrange("(c p) n -> c p n", p=128)
        self.pr_f1q = t["fc1q"].ap().rearrange("(c p) n -> c p n", p=128)
        self.pr_f1s = t["fc1s"].ap().rearrange("(c p) n -> c p n", p=128)
        self.pr_bsl = {}
        self.pr_wsl = {}
        self.pr_qsb = {}
        self.pr_ld = {}
        self.pr_ovw = None
        self.pr_load_i = 0
        self.pr_comp_i = 0
        self.w1ptok = {}
        self.ffn_prelude_step(0)

    def _pr_load(self, p):
        b, t = self.b, self.t
        RD, RS = self.RD, self.RS
        if p < NF1:
            s = p % 4
            qv = RD[:, s * 4096: s * 4096 + 1024]
            sv = RD[:, s * 4096 + 1024: s * 4096 + 2048]
            if self.pr_bsl.get(s) is not None:
                b.wait("gpsimd", self.pr_bsl[s])
            t1 = b.dma("gpsimd", qv, self.pr_w2q[p], st="w2i", depth=8)
            t2 = b.dma("gpsimd", sv, self.pr_w2s[p], st="w2i", depth=8)
        else:
            i = p - NF1
            sl = i % 2
            hh, k = i // 8, i % 8
            inb = RS[:, sl * 2048:(sl + 1) * 2048]
            scb = self.QS[:, :].bitcast(BF16)[:, sl * 2048:(sl + 1) * 2048]
            if self.pr_wsl.get(sl) is not None:
                b.wait("gpsimd", self.pr_wsl[sl])
            t1 = b.dma("gpsimd", inb, self.pr_f1q[k][:, hh * 2048:(hh + 1) * 2048],
                       st="w1i", depth=4)
            if self.pr_qsb.get(sl) is not None:
                b.wait("gpsimd", self.pr_qsb[sl])
            t2 = b.dma("gpsimd", scb, self.pr_f1s[k][:, hh * 2048:(hh + 1) * 2048],
                       st="w1i", depth=4)
            qv, sv = inb, scb
        self.pr_ld[p] = (qv, sv, t1, t2)

    def _pr_comp(self, p):
        b, t = self.b, self.t
        RD, RE, RS = self.RD, self.RE, self.RS
        qv, sv, t1, t2 = self.pr_ld.pop(p)
        if p < NF1:
            s = p % 4
            ov = RD[:, s * 4096 + 2048: s * 4096 + 3072]
            b.wait("scalar", t1)
            t3 = b.do("scalar", lambda e, qv=qv, ov=ov: e.activation(
                out=ov, in_=qv, func=AF.Copy, bias=-128.0), inc=True)
            b.wait("vector", t3)
            b.wait("vector", t2)
            t4 = b.do("vector", lambda e, qv=qv, sv=sv, ov=ov: e.tensor_tensor(
                out=qv, in0=ov, in1=sv, op=ALU.mult), inc=True)
            b.wait("gpsimd", t4)
            t5 = b.dma("gpsimd", t["w2u_hbm"][:, p * 1024:(p + 1) * 1024], qv,
                       st="w2o", depth=4)
            self.pr_bsl[s] = t5
            self.w2_all_done = t5
        else:
            i = p - NF1
            sl = i % 2
            hh, k = i // 8, i % 8
            OV2 = RS[:, 4096:6144]
            reg = RD if k < 4 else RE
            dst = reg[:, (k % 4) * 4096 + hh * 2048: (k % 4) * 4096 + (hh + 1) * 2048]
            if k < 4:
                b.wait("vector", self.pr_bsl[k % 4])
            b.wait("scalar", t1)
            if self.pr_ovw is not None:
                b.wait("scalar", self.pr_ovw)
            t3 = b.do("scalar", lambda e, qv=qv, OV2=OV2: e.activation(
                out=OV2, in_=qv, func=AF.Copy, bias=-128.0), inc=True)
            b.wait("vector", t3)
            b.wait("vector", t2)
            t4 = b.do("vector", lambda e, dst=dst, sv=sv, OV2=OV2: e.tensor_tensor(
                out=dst, in0=OV2, in1=sv, op=ALU.mult), inc=True)
            self.pr_wsl[sl] = t3
            self.pr_qsb[sl] = t4
            self.pr_ovw = t4
            self.w1ptok[(k, hh)] = t4
            self.pr_last = t4

    def ffn_prelude_step(self, n):
        NP = NF1 + 16
        for _ in range(n + (1 if n == 0 else 0)):
            while (self.pr_load_i < NP and
                   self.pr_load_i < self.pr_comp_i + (4 if self.pr_load_i < NF1 else 2)):
                self._pr_load(self.pr_load_i)
                self.pr_load_i += 1
            if n == 0 or self.pr_comp_i >= NP:
                return
            self._pr_comp(self.pr_comp_i)
            self.pr_comp_i += 1

    def phase_ffn(self):
        b, t = self.b, self.t
        RA, RB, RC, RD, RE, RS = self.RA, self.RB, self.RC, self.RD, self.RE, self.RS
        self.ffn_prelude_step(NF1 + 16)
        last_td = self.pr_last
        for sl in (0, 1):
            self.wq_slot_tok[("in", sl)] = self.pr_wsl[1]
        self.wq_prev_qs = last_td
        self.w2fc2_guard = last_td
        self.tap("tap_w1", RD[:, 0:16384], c0=0, cols=16384)
        self.tap("tap_w1", RE[:, 0:16384], c0=16384, cols=16384)

        def w1c(k):
            reg = RD if k < 4 else RE
            return reg[:, (k % 4) * 4096:(k % 4) * 4096 + 4096]
        b.wait("tensor", self.tok_h1)
        fc2_evac_tok = None
        fc1_mm_done = None
        w2sl_tok = {}
        ev = None
        evmg = {}
        for tc in range(NTC):
            for mg in range(16):
                b2 = (mg % 2) * 2
                if evmg.get(mg % 2) is not None:
                    b.wait("tensor", evmg[mg % 2])
                for k in range(8):
                    b.wait("tensor", self.w1ptok[(k, mg // 8)])
                    for mi in range(2):
                        m = mg * 2 + mi
                        if fc2_evac_tok is not None:
                            b.wait("tensor", fc2_evac_tok)
                        fc1_mm_done = b.do("tensor", lambda e, k=k, m=m, mi=mi, tc=tc, b2=b2,
                                           st=(k == 0), sp=(k == 7): e.matmul(
                            self.ps[b2 + mi][:, :], w1c(k)[:, m * 128:(m + 1) * 128],
                            self.RC[:, k * T + tc * TCW: k * T + (tc + 1) * TCW],
                            start=st, stop=sp),
                            inc=(k == 7 and mi == 1))
                b.wait("scalar", self.ld_toks["fc1b"])
                b.wait("scalar", fc1_mm_done)
                for mi in range(2):
                    m = mg * 2 + mi
                    ev = b.do("scalar", lambda e, mi=mi, m=m, b2=b2: e.activation(
                        out=RA[:, m * TCW:(m + 1) * TCW], in_=self.ps[b2 + mi][:, :],
                        func=AF.Silu, bias=self.BIAS[:, 72 + m:73 + m]), inc=(mi == 1))
                evmg[mg % 2] = ev
                if mg % 2 == 0:
                    self.wquant_step(1 if (tc * 16 + mg) % 4 else 2)
            fc2_evac_tok = ev
            if tc == 0:
                self.tap("tap_y1", RA[:, 0:NF1 * TCW])
            fc2_mm_done = None
            if getattr(self, "ffn_ev2", None) is not None:
                b.wait("tensor", self.ffn_ev2)
            if self.wq_t_tse is not None:
                b.wait("tensor", self.wq_t_tse)
            for tok_ in (self.wq_slot_tok.get(("in", 0)), self.wq_slot_tok.get(("in", 1)),
                         self.wq_prev_qs):
                b.wait("sync", tok_)
            b.wait("sync", self.w2_all_done)
            for kk in range(NF1 // 2):
                sl = kk % 2
                wv = RS[:, sl * 2048: (sl + 1) * 2048]
                b.wait("sync", self.w2fc2_guard)
                if sl in w2sl_tok:
                    b.wait("sync", w2sl_tok[sl])
                tdma = b.dma("sync", wv, t["w2u_hbm"][:, kk * 2048:(kk + 1) * 2048],
                             st="w2l", depth=2)
                b.wait("tensor", tdma)
                b.wait("tensor", ev)
                for k2 in range(2):
                    k = kk * 2 + k2
                    for m in range(8):
                        fc2_mm_done = b.do("tensor", lambda e, k=k, k2=k2, m=m, wv=wv,
                                           st=(k == 0), sp=(k == NF1 - 1): e.matmul(
                            self.ps[m][:, :], wv[:, k2 * 1024 + m * 128: k2 * 1024 + (m + 1) * 128],
                            RA[:, k * TCW:(k + 1) * TCW], start=st, stop=sp),
                            inc=(m == 7 and k2 == 1))
                w2sl_tok[sl] = fc2_mm_done
            b.wait("vector", self.tok_xnpark)
            b.wait("vector", fc2_mm_done)
            ev2 = None
            ev2half = None
            for m in range(8):
                ev2 = b.do("vector", lambda e, m=m, tc=tc: e.scalar_tensor_tensor(
                    out=RB[:, m * T + tc * TCW: m * T + (tc + 1) * TCW],
                    in0=self.ps[m][:, :], scalar=0.5,
                    in1=self.BIAS[:, 40 + m:41 + m].broadcast_to((128, TCW)),
                    op0=ALU.mult, op1=ALU.add), inc=(m == 3 or m == 7))
                if m == 3:
                    ev2half = ev2
            fc2_evac_tok = ev2half
            self.fc2_last = fc2_mm_done
            self.ffn_ev2 = ev2
        self.wquant_step(len(self.wq_list))
        self.tap("tap_wgq", None) if False else None
        self.fc1_done = fc1_mm_done
        b.wait("sync", fc1_mm_done)
        b.wait("sync", self.tok_xnpark)
        tok_xn = b.dma("sync", RD[:, :], t["xn_park"][:, :], st="park", depth=2)
        self.tok_xn_restore = tok_xn
        gtok = None
        for f in range(NF):
            b.wait("vector", tok_xn)
            b.wait("vector", self.ffn_ev2)
            if self.sq_war is not None:
                b.wait("vector", self.sq_war)
            t1 = b.do("vector", lambda e, f=f: e.tensor_tensor(
                out=self.SQ[:, :], in0=RB[:, f * T:(f + 1) * T],
                in1=RD[:, f * T:(f + 1) * T], op=ALU.add), inc=True)
            b.wait("scalar", t1)
            gtok = b.do("scalar", lambda e, f=f: e.activation(
                out=RB[:, f * T:(f + 1) * T], in_=self.SQ[:, :], func=AF.Silu), inc=True)
            self.sq_war = gtok
        self.tok_g = gtok
        self.tap("tap_g", RB[:, :])

    def phase_attn(self):
        b, t = self.b, self.t
        RA, RB, RC, RD, RE, RS = self.RA, self.RB, self.RC, self.RD, self.RE, self.RS
        SC = self.SC
        ps = self.ps
        xin = t["x_t"].ap().rearrange("(c p) n -> p c n", p=128)
        xe = RE[:, :].rearrange("p (c n) -> p c n", n=T)
        b.wait("gpsimd", self.fc1_done)
        tok_xe = b.dma("gpsimd", xe, xin, st="gx", depth=1)
        tmem = b.do("vector", lambda e: e.memset(RC[:, 6144:10240], 0.0), inc=True)
        QT = RC[:, 0:2048]
        KT = RC[:, 2048:4096]
        VT = RC[:, 4096:6144]
        QBD = RC[:, 6144:10240]
        AWSP = RC[:, 12288:14336]
        AW_E = RA[:, 4096:6144]
        AW_N = RA[:, 6144:8192]
        SCRF = RA[:, 0:4096].bitcast(F32)

        def aws(s):
            return RC[:, 10240 + s * 512: 10240 + (s + 1) * 512]

        def awt(s):
            return RC[:, 14336 + s * 768: 14336 + (s + 1) * 768]

        ps6b = ps[6][:, :].bitcast(BF16)
        ps7b = ps[7][:, :].bitcast(BF16)
        self.ps6_war = {6: self.wq_t_tse, 7: self.wq_t_tse}
        self.mid_aw_war = {}
        self.mid_awn_war = {}
        self.awt_ctx_war = {}
        self.sp_tn_war = None
        self.sp_ttr_war = None
        self.scrf_war = None
        hw_war = {}
        started = {}
        xupd_tok = None
        sp_exp_tok = {}
        mid_exp_tok = {}
        awsp_war = None
        ctx_done = None
        vt_slot_tok = {}
        qbd_tok = None
        b.wait("tensor", self.ld_toks["eye"])
        b.wait("tensor", self.ld_toks["ones_r_bf"])
        b.wait("tensor", self.ld_toks["bvrow"])
        b.wait("tensor", self.tok_xn_restore)
        b.wait("tensor", self.ffn_ev2)
        b.wait("tensor", self.tok_g)
        self.tap("tap_xn2", RD[:, :])
        self.tap("tap_bias", self.BIAS[:, 8:104], cols=96)
        if self.wq_t_tse is not None:
            b.wait("tensor", self.wq_t_tse)
        for tok_ in ((self.wq_t_sts or []) + [self.wq_t_tcs, self.wq_prev_qs,
                                              self.wq_t_ttr, self.fc2_last]):
            b.wait("gpsimd", tok_)

        def ctx_mm(bank, cols, kb, h, sc_i, awt_ap, st, sp, inc=False):
            vsrc = VT if (kb % 2) == 0 else RA[:, 8192:10240]
            lhsT = vsrc[0:64, (kb // 2) * 128 + h * 64: (kb // 2) * 128 + (h + 1) * 64]
            rhs = awt_ap[0:64, sc_i * 128 + h * 64: sc_i * 128 + (h + 1) * 64]
            return b.do("tensor", lambda e, o=ps[bank][h * 64:(h + 1) * 64, cols:cols + 64],
                        l=lhsT, r=rhs, st=st, sp=sp: e.matmul(
                o, l, r, start=st, stop=sp), inc=inc)

        for hp in range(self.attn_hps):
            sl3 = (hp % 2) * 3072
            wsl = {}
            for wi, wname in enumerate(("wq_t", "wk_t", "wv_t")):
                dst = RS[:, sl3 + wi * 1024: sl3 + (wi + 1) * 1024]
                if (hp % 2, wi) in hw_war:
                    b.wait("gpsimd", hw_war[(hp % 2, wi)])
                srcw = t[wname].ap()[:, hp * 128:(hp + 1) * 128].rearrange(
                    "(c p) n -> p c n", p=128)
                wsl[wname] = (dst, b.dma(
                    "gpsimd", dst.rearrange("p (c n) -> p c n", n=128), srcw,
                    st="hw", depth=3))
            qev = kev = vfev = None
            VFM = RA[:, 4096:6144]
            for which, (wname, bank, biascol, scale, dreg) in enumerate(
                    [("wq_t", 4, 8, 0.125, QT), ("wk_t", 5, 16, 1.0, KT),
                     ("wv_t", 4, 104, 1.0, None)]):
                wdst, wtok = wsl[wname]
                b.wait("tensor", wtok)
                ev_prev = None
                for tc in range(NTC):
                    if ctx_done is not None:
                        b.wait("tensor", ctx_done)
                    if which == 2 and qev is not None:
                        b.wait("tensor", qev)
                    if which == 2 and self.sp_tn_war is not None:
                        b.wait("vector", self.sp_tn_war)
                    if ev_prev is not None:
                        b.wait("tensor", ev_prev)
                    mm = None
                    for k in range(NF):
                        mm = b.do("tensor", lambda e, k=k, tc=tc, w=wdst, bank=bank,
                                  st=(k == 0), sp=(k == NF - 1): e.matmul(
                            ps[bank][:, :], w[:, k * 128:(k + 1) * 128],
                            RD[:, k * T + tc * TCW: k * T + (tc + 1) * TCW],
                            start=st, stop=sp), inc=(k == NF - 1))
                    b.wait("vector", mm)
                    dst = (dreg if dreg is not None else VFM)
                    ev_prev = b.do("vector", lambda e, tc=tc, bank=bank, bc=biascol, sc=scale, hp=hp,
                                   d=dst[:, tc * TCW:(tc + 1) * TCW]:
                                   e.scalar_tensor_tensor(
                        out=d, in0=ps[bank][:, :], scalar=sc,
                        in1=self.BIAS[:, bc + hp:bc + hp + 1].broadcast_to((128, TCW)),
                        op0=ALU.mult, op1=ALU.add), inc=True)
                if which == 0:
                    qev = ev_prev
                elif which == 1:
                    kev = ev_prev
                else:
                    vfev = ev_prev
            vev = None
            for half in range(2):
                tb_ = 6 + half
                psb_ = ps6b if half == 0 else ps7b
                if self.ps6_war.get(tb_) is not None:
                    b.wait("tensor", self.ps6_war[tb_])
                b.wait("tensor", vfev)
                ttrv = None
                for cc in range(8):
                    ci = half * 8 + cc
                    ttrv = b.do("tensor", lambda e, ci=ci, cc=cc, psb_=psb_: e.transpose(
                        psb_[:, cc * 128:(cc + 1) * 128],
                        VFM[:, ci * 128:(ci + 1) * 128], self.EYE[:, :]),
                        inc=(cc == 7))
                b.wait("scalar", ttrv)
                if ctx_done is not None:
                    b.wait("scalar", ctx_done)
                vev = b.do("scalar", lambda e, half=half, psb_=psb_: e.copy(
                    out=VT[:, half * 1024:(half + 1) * 1024],
                    in_=psb_[:, 0:1024]), inc=True)
                self.ps6_war[tb_] = vev
            self.sp_tn_war = ttrv
            hw_war[(hp % 2, 0)] = qev
            hw_war[(hp % 2, 1)] = kev
            hw_war[(hp % 2, 2)] = vev
            V2 = RA[0:64, 8192:10240]
            if ctx_done is not None:
                b.wait("sync", ctx_done)
            b.wait("sync", vev)
            tok_v2 = b.dma("sync", V2, VT[64:128, :], st="v2", depth=1)
            b.wait("scalar", tmem)
            b.wait("scalar", qev)
            if ctx_done is not None:
                b.wait("scalar", ctx_done)
            qbdo = QBD.rearrange("p (j s) -> p j s", s=128)
            qti = QT.rearrange("p (j s) -> p j s", s=64)
            b.do("scalar", lambda e: e.copy(out=qbdo[0:64, :, 0:64], in_=qti[0:64, :, :]))
            qbd_tok = b.do("scalar", lambda e: e.copy(
                out=qbdo[64:128, :, 64:128], in_=qti[64:128, :, :]), inc=True)
            for bank in range(4):
                started[bank] = False
            if xupd_tok is not None:
                b.wait("tensor", xupd_tok)
            for j in [jj for jj in self.attn_js if jj in (0, NB - 1)]:
                bank = j // 8
                cols = (j % 8) * 64
                if j in (0, NB - 1):
                    cp_tok = None
                    for c4 in range(4):
                        sbank = 4 + (c4 % 2)
                        if sp_exp_tok.get(sbank) is not None:
                            b.wait("tensor", sp_exp_tok[sbank])
                        if mid_exp_tok.get(sbank) is not None:
                            b.wait("tensor", mid_exp_tok[sbank])
                        b.wait("tensor", qbd_tok)
                        b.wait("tensor", kev)
                        mm = b.do("tensor", lambda e, c4=c4, sbank=sbank, j=j: e.matmul(
                            ps[sbank][:, :], QBD[:, j * 128:(j + 1) * 128],
                            KT[:, c4 * TCW:(c4 + 1) * TCW], start=True, stop=True), inc=True)
                        b.wait("scalar", mm)
                        if self.scrf_war is not None:
                            b.wait("scalar", self.scrf_war)
                        cp_tok = b.do("scalar", lambda e, c4=c4, sbank=sbank: e.copy(
                            out=SCRF[:, c4 * TCW:(c4 + 1) * TCW], in_=ps[sbank][:, :]),
                            inc=True)
                        sp_exp_tok[sbank] = cp_tok
                    b.wait("vector", cp_tok)
                    tv = b.do("vector", lambda e: e.reduce_max(
                        out=SC[:, 14:15], in_=SCRF[:, :], axis=mybir.AxisListType.X,
                        negate=True), inc=True)
                    self.scrf_war = tv
                    b.wait("scalar", tv)
                    if self.sp_tn_war is not None:
                        b.wait("scalar", self.sp_tn_war)
                    te = b.do("scalar", lambda e: e.activation(
                        out=AW_E[:, :], in_=SCRF[:, :], func=AF.Exp,
                        bias=SC[:, 14:15], scale=1.0, accum_out=SC[:, 15:16]), inc=True)
                    b.wait("vector", te)
                    tr = b.do("vector", lambda e: e.reciprocal(
                        out=SC[:, 16:17], in_=SC[:, 15:16]), inc=True)
                    b.wait("vector", tr)
                    if self.sp_ttr_war is not None:
                        b.wait("vector", self.sp_ttr_war)
                    tn = b.do("vector", lambda e: e.tensor_scalar(
                        out=AW_N[:, :], in0=AW_E[:, :], scalar1=SC[:, 16:17],
                        scalar2=None, op0=ALU.mult), inc=True)
                    self.sp_tn_war = tn
                    evtok = {}
                    ttr = None
                    for half in range(2):
                        tbank = 6 + half
                        psb = ps6b if half == 0 else ps7b
                        if self.ps6_war.get(tbank) is not None:
                            b.wait("tensor", self.ps6_war[tbank])
                        b.wait("tensor", tn)
                        for cc in range(8):
                            ci = half * 8 + cc
                            ttr = b.do("tensor", lambda e, ci=ci, cc=cc, psb=psb: e.transpose(
                                psb[:, cc * 128:(cc + 1) * 128],
                                AW_N[:, ci * 128:(ci + 1) * 128], self.EYE[:, :]),
                                inc=(cc == 7))
                        b.wait("scalar", ttr)
                        if awsp_war is not None:
                            b.wait("scalar", awsp_war)
                        tev = b.do("scalar", lambda e, half=half, psb=psb: e.copy(
                            out=AWSP[:, half * 1024:(half + 1) * 1024],
                            in_=psb[:, 0:1024]), inc=True)
                        evtok[half] = tev
                        self.ps6_war[tbank] = tev
                    self.sp_ttr_war = ttr
                    awsp_war = None
                    b.wait("tensor", vev)
                    for sl8 in evtok.values():
                        b.wait("tensor", sl8)
                    mmc = None
                    for h in range(2):
                        for ci in range(16):
                            mmc = b.do("tensor", lambda e, ci=ci, h=h, st=(ci == 0), bank=bank,
                                       cols=cols: e.matmul(
                                ps[bank][h * 64:(h + 1) * 64, cols:cols + 64],
                                VT[:, ci * 128 + h * 64: ci * 128 + (h + 1) * 64],
                                AWSP[:, ci * 128 + h * 64: ci * 128 + (h + 1) * 64],
                                start=st, stop=(ci == 15)), inc=(ci == 15 and h == 1))
                    awsp_war = mmc
                    ctx_done = mmc
                else:
                    pass
            if True:
                def mid_meta(j):
                    if j == 1:
                        return dict(kbs=[0, 1, 2, 31],
                                    spans=[(KT[:, 0:192], 0, 192), (KT[:, 1984:2048], 192, 64)],
                                    width=256, dummy0=None)
                    if j == NB - 2:
                        return dict(kbs=[0, 29, 30, 31],
                                    spans=[(KT[:, 0:64], 0, 64), (KT[:, 1856:2048], 64, 192)],
                                    width=256, dummy0=None)
                    if j % 2 == 0:
                        return dict(kbs=[0, j - 1, j, j + 1, None, 31],
                                    spans=[(KT[:, 0:64], 0, 64),
                                           (KT[:, (j - 1) * 64:(j + 2) * 64], 64, 192),
                                           (KT[:, 1984:2048], 320, 64)],
                                    width=384, dummy0=256)
                    return dict(kbs=[j - 1, j, j + 1, 31, 0, None],
                                spans=[(KT[:, (j - 1) * 64:(j + 2) * 64], 0, 192),
                                       (KT[:, 1984:2048], 192, 64),
                                       (KT[:, 0:64], 256, 64)],
                                width=384, dummy0=320)

                def em_scores(j, meta):
                    sbank = 4 + (j % 2)
                    if mid_exp_tok.get(sbank) is not None:
                        b.wait("tensor", mid_exp_tok[sbank])
                    if sp_exp_tok.get(sbank) is not None:
                        b.wait("tensor", sp_exp_tok[sbank])
                    b.wait("tensor", qbd_tok)
                    b.wait("tensor", kev)
                    msv = None
                    if meta["dummy0"] is not None:
                        if mid_exp_tok.get((sbank, "ms")) is not None:
                            b.wait("vector", mid_exp_tok[(sbank, "ms")])
                        msv = b.do("vector", lambda e, sbank=sbank, d0=meta["dummy0"]: e.memset(
                            ps[sbank][:, d0:d0 + 64], -30000.0), inc=True)
                        b.wait("tensor", msv)
                    mm = None
                    for _i, (rhs, c0, w) in enumerate(meta["spans"]):
                        mm = b.do("tensor", lambda e, rhs=rhs, c0=c0, w=w, j=j,
                                  sbank=sbank: e.matmul(
                            ps[sbank][:, c0:c0 + w], QBD[:, j * 128:(j + 1) * 128],
                            rhs, start=True, stop=True), inc=(_i == len(meta["spans"]) - 1))
                    meta["mm"] = mm
                    meta["msv"] = msv

                def em_max(j, meta):
                    sl = j % 2
                    sbank = 4 + sl
                    b.wait("vector", meta["mm"])
                    tv_ = b.do("vector", lambda e, sbank=sbank, w=meta["width"], sl=sl: e.reduce_max(
                        out=SC[:, 8 + sl:9 + sl], in_=self.ps[sbank][:, 0:w],
                        axis=mybir.AxisListType.X, negate=True), inc=True)
                    meta["max"] = tv_

                def em_exp(j, meta):
                    sl = j % 2
                    sbank = 4 + sl
                    b.wait("scalar", meta["max"])
                    if self.mid_aw_war.get(sl) is not None:
                        b.wait("scalar", self.mid_aw_war[sl])
                    te_ = b.do("scalar", lambda e, sbank=sbank, w=meta["width"], sl=sl: e.activation(
                        out=aws(sl)[:, 0:w], in_=self.ps[sbank][:, 0:w], func=AF.Exp,
                        bias=SC[:, 8 + sl:9 + sl], scale=1.0,
                        accum_out=SC[:, 10 + sl:11 + sl]), inc=True)
                    meta["exp"] = te_
                    mid_exp_tok[sbank] = te_
                    mid_exp_tok[(sbank, "ms")] = te_

                def em_norm(j, meta):
                    sl = j % 2
                    b.wait("vector", meta["exp"])
                    tr_ = b.do("vector", lambda e, sl=sl: e.reciprocal(
                        out=SC[:, 12 + sl:13 + sl], in_=SC[:, 10 + sl:11 + sl]), inc=True)
                    b.wait("vector", tr_)
                    if self.mid_awn_war.get(sl) is not None:
                        b.wait("vector", self.mid_awn_war[sl])
                    tn_ = b.do("vector", lambda e, sl=sl, w=meta["width"]: e.tensor_scalar(
                        out=aws(2 + sl)[:, 0:w], in0=aws(sl)[:, 0:w],
                        scalar1=SC[:, 12 + sl:13 + sl], scalar2=None, op0=ALU.mult), inc=True)
                    self.mid_aw_war[sl] = tn_
                    meta["norm"] = tn_

                def em_transp(j, meta):
                    sl = j % 2
                    bank = 6 + sl
                    psb = ps6b if sl == 0 else ps7b
                    npieces = meta["width"] // 64
                    if self.ps6_war.get(bank) is not None:
                        b.wait("tensor", self.ps6_war[bank])
                    b.wait("tensor", meta["norm"])
                    ttr_ = None
                    for c in range(npieces):
                        ttr_ = b.do("tensor", lambda e, c=c, psb=psb, sl=sl: e.transpose(
                            psb[0:64, c * 128: (c + 1) * 128],
                            aws(2 + sl)[:, c * 64:(c + 1) * 64], self.EYE[:, :]),
                            inc=(c == npieces - 1))
                    b.wait("scalar", ttr_)
                    if self.awt_ctx_war.get(sl) is not None:
                        b.wait("scalar", self.awt_ctx_war[sl])
                    evl_ = b.do("scalar", lambda e, psb=psb, sl=sl,
                                w=npieces * 128: e.copy(
                        out=awt(sl)[0:64, 0:w], in_=psb[0:64, 0:w]), inc=True)
                    self.ps6_war[bank] = evl_
                    self.mid_awn_war[sl] = evl_
                    meta["evl"] = evl_

                def em_ctx(j, meta):
                    bank = j // 8
                    cols = (j % 8) * 64
                    b.wait("tensor", meta["evl"])
                    b.wait("tensor", vev)
                    b.wait("tensor", tok_v2)
                    valid = [(i_, kb) for i_, kb in enumerate(meta["kbs"]) if kb is not None]
                    mmc_ = None
                    for h in range(2):
                        for idx, (sc_i, kb) in enumerate(valid):
                            mmc_ = ctx_mm(bank, cols, kb, h, sc_i, awt(j % 2),
                                          st=(idx == 0), sp=(idx == len(valid) - 1),
                                          inc=(idx == len(valid) - 1 and h == 1))
                    self.awt_ctx_war[j % 2] = mmc_
                    return mmc_

                stages = [em_scores, em_max, em_exp, em_norm, em_transp]
                groups = [[1], [2, 3], [4, 5], [6, 7], [8, 9], [10, 11], [12, 13],
                          [14, 15], [16, 17], [18, 19], [20, 21], [22, 23], [24, 25],
                          [26, 27], [28, 29], [30]]
                groups = [[j for j in g if j in self.attn_js] for g in groups]
                groups = [g for g in groups if g]
                for grp in groups:
                    metas = {j: mid_meta(j) for j in grp}
                    for stg in stages:
                        for j in grp:
                            stg(j, metas[j])
                    for j in grp:
                        ctx_done = em_ctx(j, metas[j])
            if hp == 0:
                self.tap("tap_qt", QT)
                self.tap("tap_kt", KT)
                self.tap("tap_vt", VT)
            b.wait("vector", ctx_done)
            b.wait("vector", tok_xe)
            b.wait("vector", self.tok_g)
            for q4 in range(4):
                if self.sq_war is not None:
                    b.wait("vector", self.sq_war)
                t1 = b.do("vector", lambda e, q4=q4, hp=hp: e.tensor_tensor(
                    out=self.SQ[:, 0:TCW], in0=ps[q4][:, :],
                    in1=RB[:, hp * T + q4 * TCW: hp * T + (q4 + 1) * TCW],
                    op=ALU.mult), inc=True)
                b.wait("vector", t1)
                xupd_tok = b.do("vector", lambda e, q4=q4, hp=hp: e.tensor_tensor(
                    out=RE[:, hp * T + q4 * TCW: hp * T + (q4 + 1) * TCW],
                    in0=RE[:, hp * T + q4 * TCW: hp * T + (q4 + 1) * TCW],
                    in1=self.SQ[:, 0:TCW], op=ALU.add), inc=True)
                self.sq_war = xupd_tok
        self.tok_x1 = xupd_tok
        self.tap("tap_x1", RE[:, :])

    def softmax_transpose(self, sbank, width, j, aws, awt, ps6b, nchunks,
                          extra_dep=None):
        b = self.b
        SC = self.SC
        sl = j % 2
        nm = SC[:, 8 + sl:9 + sl]
        rs = SC[:, 10 + sl:11 + sl]
        rc = SC[:, 12 + sl:13 + sl]
        if extra_dep is not None:
            b.wait("vector", extra_dep)
        tv = b.do("vector", lambda e: e.reduce_max(
            out=nm, in_=self.ps[sbank][:, 0:width],
            axis=mybir.AxisListType.X, negate=True), inc=True)
        b.wait("scalar", tv)
        if self.mid_aw_war.get(sl) is not None:
            b.wait("scalar", self.mid_aw_war[sl])
        te = b.do("scalar", lambda e: e.activation(
            out=aws(sl)[:, 0:width], in_=self.ps[sbank][:, 0:width], func=AF.Exp,
            bias=nm, scale=1.0, accum_out=rs), inc=True)
        b.wait("vector", te)
        tr = b.do("vector", lambda e: e.reciprocal(out=rc, in_=rs), inc=True)
        b.wait("vector", tr)
        if self.mid_awn_war.get(sl) is not None:
            b.wait("vector", self.mid_awn_war[sl])
        tn = b.do("vector", lambda e: e.tensor_scalar(
            out=aws(2 + sl)[:, 0:width], in0=aws(sl)[:, 0:width], scalar1=rc,
            scalar2=None, op0=ALU.mult), inc=True)
        self.mid_aw_war[sl] = tn
        ttr = None
        npieces = width // 64
        bank = 6 + sl
        psb = self.ps[bank][:, :].bitcast(BF16)
        if self.ps6_war.get(bank) is not None:
            b.wait("tensor", self.ps6_war[bank])
        b.wait("tensor", tn)
        for c in range(npieces):
            ttr = b.do("tensor", lambda e, c=c, psb=psb: e.transpose(
                psb[0:64, c * 128: (c + 1) * 128],
                aws(2 + sl)[:, c * 64:(c + 1) * 64], self.EYE[:, :]),
                inc=(c == npieces - 1))
        b.wait("scalar", ttr)
        if self.awt_ctx_war.get(sl) is not None:
            b.wait("scalar", self.awt_ctx_war[sl])
        evl = b.do("scalar", lambda e, psb=psb, w=npieces * 128: e.copy(
            out=awt(sl)[0:64, 0:w], in_=psb[0:64, 0:w]), inc=True)
        self.ps6_war[bank] = evl
        self.mid_awn_war[sl] = evl
        return te, evl

    def phase_l1(self):
        b, t = self.b, self.t
        RA, RB, RC, RD, RE, RS = self.RA, self.RB, self.RC, self.RD, self.RE, self.RS
        SC, SWQ, HMX, ROWS = self.SC, self.SWQ, self.HMX, self.ROWS
        ps = self.ps
        ps6b = ps[6][:, :].bitcast(BF16)

        b.wait("tensor", self.tok_x1)
        b.wait("scalar", self.tok_x1)
        tok_ln2 = self.emit_ln(RE, RD, src_tok=self.tok_x1)
        self.tap("tap_xn2", RD[:, :])

        b.wait("vector", tok_ln2)
        tv = None
        for f in range(NF):
            tv = b.do("vector", lambda e, f=f: e.reduce_max(
                out=HMX[:, f:f + 1], in_=RD[:, f * T:(f + 1) * T],
                axis=mybir.AxisListType.X, apply_absolute_value=True), inc=(f == NF - 1))
        b.wait("vector", tv)
        tv = b.do("vector", lambda e: e.reduce_max(
            out=HMX[:, 8:9], in_=HMX[:, 0:8], axis=mybir.AxisListType.X), inc=True)
        b.wait("vector", tv)
        tv = b.do("vector", lambda e: e.tensor_copy(
            out=self.SQ[:, 0:1], in_=HMX[:, 8:9]), inc=True)
        b.wait("tensor", tv)
        if self.ps6_war.get(6) is not None:
            b.wait("tensor", self.ps6_war[6])
        ttr = b.do("tensor", lambda e: e.transpose(
            ps6b[0:1, 0:128], self.SQ[:, 0:1], self.EYE[:, :]), inc=True)
        b.wait("scalar", ttr)
        tse = b.do("scalar", lambda e: e.copy(
            out=ROWS[96:97, 1024:1152], in_=ps6b[0:1, 0:128]), inc=True)
        self.ps6_war[6] = tse
        b.wait("vector", tse)
        tv = b.do("vector", lambda e: e.reduce_max(
            out=ROWS[96:97, 1152:1153], in_=ROWS[96:97, 1024:1152],
            axis=mybir.AxisListType.X), inc=True)
        b.wait("sync", tv)
        tcc = b.dma("sync", t["ccx_in"][:, :], ROWS[96:97, 1152:1153], st="cc", depth=1)
        b.wait("gpsimd", tcc)
        ar1 = b.do("gpsimd", lambda e: e.collective_compute(
            "AllReduce", ALU.max, replica_groups=[list(range(self.ncores))],
            ins=[t["ccx_in"].ap().opt()], outs=[t["ccx_out"].ap().opt()]), inc=True)
        b.wait("gpsimd", ar1)
        tgx = b.dma("gpsimd", SC[0:1, 2:3], t["ccx_out"][:, :], st="ccb", depth=1)

        b.wait("vector", self.ld_toks["nw"])
        b.wait("vector", self.ld_toks["nb"])
        b.wait("tensor", tse)
        tok_h2 = self.emit_ln(RD, RB, gamma_col=56, beta_col=64, src_tok=tok_ln2)

        b.wait("sync", tok_ln2)
        tok_x1park = b.dma("sync", t["x1_park"][:, :], RE[:, :], st="park", depth=2)
        b.wait("gpsimd", tok_x1park)
        kvw = RE
        tok_kvw = b.dma("gpsimd", RE[:, :].rearrange("p (c n) -> p c n", n=2048),
                        t["qkvw_t"].ap()[:, H:3 * H].rearrange("(c p) n -> p c n", p=128),
                        st="gw", depth=2)

        b.wait("tensor", tok_kvw)
        b.wait("tensor", tok_h2)
        kv_ev = {}
        for st_ in range(16):
            g4 = (st_ % 2) * 4
            evp = kv_ev.get(st_ % 2)
            if evp is not None:
                b.wait("tensor", evp)
            mm = None
            for k in range(NF):
                for part in range(4):
                    mm = b.do("tensor", lambda e, k=k, st_=st_, part=part, g4=g4,
                              st=(k == 0), sp=(k == NF - 1): e.matmul(
                        ps[g4 + part][:, :],
                        RB[:, k * T + st_ * 128: k * T + (st_ + 1) * 128],
                        kvw[:, k * 2048 + part * 512: k * 2048 + (part + 1) * 512],
                        start=st, stop=sp), inc=(k == NF - 1 and part == 3))
            b.wait("scalar", mm)
            ev = None
            for part in range(4):
                dstreg = RA if part < 2 else RC
                ev = b.do("scalar", lambda e, st_=st_, part=part, g4=g4, d=dstreg: e.copy(
                    out=d[:, st_ * 1024 + (part % 2) * 512: st_ * 1024 + (part % 2 + 1) * 512],
                    in_=ps[g4 + part][:, :]), inc=(part == 3))
            kv_ev[st_ % 2] = ev
        kv_done = ev

        b.wait("tensor", kv_done)
        mm = None
        for st_ in range(16):
            for ii, (reg, bank, row) in enumerate([(RA, 0, 0), (RA, 1, 0), (RC, 0, 32), (RC, 1, 32)]):
                lo = (ii % 2)
                mm = b.do("tensor", lambda e, st_=st_, reg=reg, bank=bank, row=row, lo=lo,
                          st=(st_ == 0), sp=(st_ == 15): e.matmul(
                    ps[bank][row:row + 1, :], self.ONESC[:, :],
                    reg[:, st_ * 1024 + lo * 512: st_ * 1024 + (lo + 1) * 512],
                    start=st, stop=sp), inc=(st_ == 15 and ii == 3))
        b.wait("scalar", mm)
        ev = None
        for ii, (bank, row, dst) in enumerate([(0, 0, self.r_sk[0:1, 0:512]),
                                               (1, 0, self.r_sk[0:1, 512:1024]),
                                               (0, 32, self.r_sv[0:1, 0:512]),
                                               (1, 32, self.r_sv[0:1, 512:1024])]):
            ev = b.do("scalar", lambda e, bank=bank, row=row, dst=dst: e.copy(
                out=dst, in_=ps[bank][row:row + 1, :]), inc=(ii == 3))
        b.wait("vector", ev)
        b.wait("vector", self.ld_toks["bvrow1"])
        tsvp = b.do("vector", lambda e: e.scalar_tensor_tensor(
            out=self.r_svp, in0=self.r_bv1, scalar=2048.0, in1=self.r_sv,
            op0=ALU.mult, op1=ALU.add), inc=True)

        b.wait("tensor", tsvp)
        b.wait("tensor", self.ld_toks["bkrow"])
        a_ev = None
        for c1 in range(8):
            if a_ev is not None:
                b.wait("tensor", a_ev)
            mm = None
            for st_ in range(16):
                for lo in range(2):
                    mm = b.do("tensor", lambda e, c1=c1, st_=st_, lo=lo,
                              st=(st_ == 0): e.matmul(
                        ps[2 + lo][:, :],
                        RA[:, st_ * 1024 + c1 * 64 * 2: st_ * 1024 + (c1 + 1) * 128],
                        RC[:, st_ * 1024 + lo * 512: st_ * 1024 + (lo + 1) * 512],
                        start=st, stop=False))
            for lo in range(2):
                b.do("tensor", lambda e, c1=c1, lo=lo: e.matmul(
                    ps[2 + lo][:, :], self.r_bk[0:1, c1 * 128:(c1 + 1) * 128],
                    self.r_svp[0:1, lo * 512:(lo + 1) * 512], start=False, stop=False))
                mm = b.do("tensor", lambda e, c1=c1, lo=lo: e.matmul(
                    ps[2 + lo][:, :], self.r_sk[0:1, c1 * 128:(c1 + 1) * 128],
                    self.r_bv1[0:1, lo * 512:(lo + 1) * 512], start=False, stop=True),
                    inc=(lo == 1))
            b.wait("scalar", mm)
            for lo in range(2):
                a_ev = b.do("scalar", lambda e, c1=c1, lo=lo: e.activation(
                    out=RE[:, c1 * 1024 + lo * 512: c1 * 1024 + (lo + 1) * 512],
                    in_=ps[2 + lo][:, :], func=AF.Copy, scale=1.0 / 32.0, bias=0.0),
                    inc=(lo == 1))
        a_done = a_ev
        self.tap("tap_a", RE[:, 0:8192])

        b.wait("tensor", a_done)
        b.wait("tensor", self.ld_toks["bq1c"])
        mm = None
        for c2c in range(8):
            for c1 in range(8):
                mm = b.do("tensor", lambda e, c2c=c2c, c1=c1, st=(c1 == 0),
                          sp=(c1 == 7): e.matmul(
                    ps[4][:, c2c:c2c + 1],
                    RE[:, c1 * 1024 + c2c * 128: c1 * 1024 + (c2c + 1) * 128],
                    self.BQC[:, c1:c1 + 1], start=st, stop=sp),
                    inc=(c2c == 7 and c1 == 7))
        b.wait("scalar", mm)
        tbqa = b.do("scalar", lambda e: e.copy(out=SC[:, 24:32], in_=ps[4][:, 0:8]), inc=True)

        b.wait("gpsimd", a_done)
        tok_qw = b.dma("gpsimd", RA[:, 0:8192].rearrange("p (c n) -> p c n", n=1024),
                       t["qkvw_t"].ap()[:, 0:H].rearrange("(c p) n -> p c n", p=128),
                       st="gw", depth=2)
        b.wait("tensor", tok_qw)
        q_ev = {}
        for m in range(8):
            for tc in range(NTC):
                bank = (m * NTC + tc) % 8
                evp = q_ev.get(bank)
                if evp is not None:
                    b.wait("tensor", evp)
                if bank == 4:
                    b.wait("tensor", tbqa)
                mm = None
                for k in range(NF):
                    mm = b.do("tensor", lambda e, m=m, tc=tc, k=k, bank=bank,
                              st=(k == 0), sp=(k == NF - 1): e.matmul(
                        ps[bank][:, :], RA[:, k * 1024 + m * 128: k * 1024 + (m + 1) * 128],
                        RB[:, k * T + tc * TCW: k * T + (tc + 1) * TCW],
                        start=st, stop=sp), inc=(k == NF - 1))
                b.wait("scalar", mm)
                ev = b.do("scalar", lambda e, m=m, tc=tc, bank=bank: e.copy(
                    out=RC[:, m * T + tc * TCW: m * T + (tc + 1) * TCW],
                    in_=ps[bank][:, :]), inc=True)
                q_ev[bank] = ev
        q_done = ev

        b.wait("vector", tgx)
        b.wait("vector", self.ld_toks["clip"])
        tv = b.do("vector", lambda e: e.tensor_copy(out=SC[0:1, 3:4], in_=SC[0:1, 0:1]), inc=True)
        b.wait("scalar", tv)
        tsg = b.do("scalar", lambda e: e.activation(
            out=SC[0:1, 4:5], in_=SC[0:1, 3:4], func=AF.Sigmoid), inc=True)
        b.wait("vector", tsg)
        tv = b.do("vector", lambda e: e.tensor_tensor(
            out=SC[0:1, 5:6], in0=SC[0:1, 4:5], in1=SC[0:1, 2:3], op=ALU.mult), inc=True)
        b.wait("vector", tv)
        tv = b.do("vector", lambda e: e.tensor_scalar(
            out=SC[0:1, 6:7], in0=SC[0:1, 5:6], scalar1=1.0 / 127.0, scalar2=None,
            op0=ALU.mult), inc=True)
        b.wait("vector", tv)
        tsx = b.do("vector", lambda e: e.reciprocal(out=SC[0:1, 7:8], in_=SC[0:1, 6:7]), inc=True)
        b.wait("tensor", tsx)
        b.wait("tensor", q_done)
        mmb = b.do("tensor", lambda e: e.matmul(
            ps[5][:, 0:1], self.ONESRF[:, :], SC[0:1, 6:7], start=True, stop=True))
        mmb = b.do("tensor", lambda e: e.matmul(
            ps[5][:, 1:2], self.ONESRF[:, :], SC[0:1, 7:8], start=True, stop=True), inc=True)
        b.wait("scalar", mmb)
        tsxb = b.do("scalar", lambda e: e.copy(out=SC[:, 6:8], in_=ps[5][:, 0:2]), inc=True)
        xq_tok = None
        b.wait("vector", tsxb)
        b.wait("vector", q_done)
        for f in range(NF):
            for half in range(2):
                src_ap = RD[:, f * T + half * 1024: f * T + (half + 1) * 1024]
                dst_ap = RA[:, f * T + half * 1024: f * T + (half + 1) * 1024]
                xq_tok = self.quant_piece(src_ap, dst_ap, SC[:, 7:8], out_signed=False,
                                          dep_toks=[self.wq_prev_qs] if self.wq_prev_qs else [])
                self.wq_prev_qs = xq_tok
        self.tap("tap_xq", RA[:, :])
        b.wait("sync", xq_tok)
        tok_xn2park = b.dma("sync", t["xn2_park"][:, :], RD[:, :], st="park", depth=2)

        b.wait("vector", self.wq_v_done)
        pre = None
        for i in range(NIG):
            for (scol, ccol, svcol, bvcol) in [(0, 64, 128, 180), (22, 86, 150, 202)]:
                tv1 = b.do("vector", lambda e, i=i, scol=scol, svcol=svcol: e.tensor_tensor(
                    out=SWQ[:, svcol + i:svcol + i + 1], in0=SWQ[:, scol + i:scol + i + 1],
                    in1=SC[:, 6:7], op=ALU.mult), inc=True)
                b.wait("vector", tv1)
                tv2 = b.do("vector", lambda e, i=i, ccol=ccol, svcol=svcol: e.tensor_tensor(
                    out=SWQ[:, 240 + 0:241], in0=SWQ[:, ccol + i:ccol + i + 1],
                    in1=SWQ[:, svcol + i:svcol + i + 1], op=ALU.mult), inc=True)
                b.wait("vector", tv2)
                pre = b.do("vector", lambda e, i=i, bvcol=bvcol: e.tensor_scalar(
                    out=SWQ[:, bvcol + i:bvcol + i + 1], in0=SWQ[:, 240:241],
                    scalar1=-128.0, scalar2=None, op0=ALU.mult), inc=True)

        def hdst(i):
            if i < 8:
                return RD[:, i * T:(i + 1) * T]
            if i < 16:
                return RB[:, (i - 8) * T:(i - 7) * T]
            if i < 20:
                return RE[:, 8192 + (i - 16) * T: 8192 + (i - 15) * T]
            return None

        gl_war = {}
        gu_bank_free = {}
        hm_i = 0
        h_ev = None
        hstage_tok = None
        hpark_toks = []
        b.wait("vector", tok_xn2park)
        for i in range(NIG):
            sl = i % 2
            wgsl = RS[:, sl * 2048: sl * 2048 + 1024]
            wusl = RS[:, sl * 2048 + 1024: sl * 2048 + 2048]
            if sl in gl_war:
                b.wait("sync", gl_war[sl])
            tg = b.dma("sync", wgsl[:, :].rearrange("p (j c) -> p j c", c=128),
                       t["wgq_hbm"].ap().rearrange("p (j c) -> p j c", c=IG)[:, :, i * 128:(i + 1) * 128],
                       st="wgl", depth=2)
            tu = b.dma("sync", wusl[:, :].rearrange("p (j c) -> p j c", c=128),
                       t["wuq_hbm"].ap().rearrange("p (j c) -> p j c", c=IG)[:, :, i * 128:(i + 1) * 128],
                       st="wgl", depth=2)
            b.wait("tensor", tg)
            b.wait("tensor", tu)
            b.wait("tensor", xq_tok)
            mm = None
            for tc in range(NTC):
                if gu_bank_free.get(tc) is not None:
                    b.wait("tensor", gu_bank_free[tc])
                if gu_bank_free.get(4 + tc) is not None:
                    b.wait("tensor", gu_bank_free[4 + tc])
                for j in range(NF):
                    mm = b.do("tensor", lambda e, i=i, tc=tc, j=j, w=wgsl,
                              st=(j == 0), sp=(j == NF - 1): e.matmul(
                        ps[tc][:, :], w[:, j * 128:(j + 1) * 128],
                        RA[:, j * T + tc * TCW: j * T + (tc + 1) * TCW],
                        start=st, stop=sp))
                    mm = b.do("tensor", lambda e, i=i, tc=tc, j=j, w=wusl,
                              st=(j == 0), sp=(j == NF - 1): e.matmul(
                        ps[4 + tc][:, :], w[:, j * 128:(j + 1) * 128],
                        RA[:, j * T + tc * TCW: j * T + (tc + 1) * TCW],
                        start=st, stop=sp), inc=(tc == NTC - 1 and j == NF - 1))
            gl_war[sl] = mm
            b.wait("scalar", mm)
            b.wait("vector", mm)
            b.wait("scalar", pre)
            hd = hdst(i)
            for tc in range(NTC):
                if self.sq_war is not None:
                    b.wait("scalar", self.sq_war)
                tsg_ = b.do("scalar", lambda e, i=i, tc=tc: e.activation(
                    out=self.SQ[:, 0:TCW], in_=ps[tc][:, :], func=AF.Silu,
                    bias=SWQ[:, 180 + i:181 + i], scale=SWQ[:, 128 + i:129 + i]), inc=True)
                gu_bank_free[tc] = tsg_
                b.wait("vector", tsg_)
                tup = b.do("vector", lambda e, i=i, tc=tc: e.scalar_tensor_tensor(
                    out=self.SQ[:, 512:1024], in0=ps[4 + tc][:, :],
                    scalar=SWQ[:, 150 + i:151 + i],
                    in1=SWQ[:, 202 + i:203 + i].broadcast_to((128, TCW)),
                    op0=ALU.mult, op1=ALU.add), inc=True)
                gu_bank_free[4 + tc] = tup
                b.wait("vector", tup)
                if hd is not None:
                    hdest = hd[:, tc * TCW:(tc + 1) * TCW]
                else:
                    hdest = RS[:, 4096 + (tc % 2) * 512: 4096 + (tc % 2 + 1) * 512]
                    if hstage_tok is not None:
                        b.wait("vector", hstage_tok)
                th = b.do("vector", lambda e, d=hdest: e.tensor_tensor(
                    out=d, in0=self.SQ[:, 0:TCW], in1=self.SQ[:, 512:1024],
                    op=ALU.mult), inc=True)
                self.sq_war = th
                h_ev = th
                b.wait("vector", th)
                tvm = b.do("vector", lambda e, d=hdest, c=hm_i: e.reduce_max(
                    out=HMX[:, c:c + 1], in_=d, axis=mybir.AxisListType.X,
                    apply_absolute_value=True), inc=True)
                hm_i += 1
                h_ev = tvm
                if hd is None:
                    b.wait("sync", tvm)
                    hstage_tok = b.dma("sync", t["h_park"][:, (i - 20) * T + tc * TCW:
                                                           (i - 20) * T + (tc + 1) * TCW],
                                       hdest, st="hpk", depth=2)
                    hpark_toks.append(hstage_tok)
                    hstage_tok_v = hstage_tok
        assert hm_i == 88, hm_i

        b.wait("vector", h_ev)
        tv = b.do("vector", lambda e: e.reduce_max(
            out=HMX[:, 88:89], in_=HMX[:, 0:88], axis=mybir.AxisListType.X), inc=True)
        b.wait("vector", tv)
        tv = b.do("vector", lambda e: e.tensor_copy(out=self.SQ[:, 0:1], in_=HMX[:, 88:89]), inc=True)
        b.wait("tensor", tv)
        if self.ps6_war.get(6) is not None:
            b.wait("tensor", self.ps6_war[6])
        ttr = b.do("tensor", lambda e: e.transpose(
            ps6b[0:1, 0:128], self.SQ[:, 0:1], self.EYE[:, :]), inc=True)
        b.wait("scalar", ttr)
        b.wait("scalar", tcc)
        tse = b.do("scalar", lambda e: e.copy(
            out=ROWS[96:97, 1024:1152], in_=ps6b[0:1, 0:128]), inc=True)
        self.ps6_war[6] = tse
        b.wait("vector", tse)
        tv = b.do("vector", lambda e: e.reduce_max(
            out=ROWS[96:97, 1153:1154], in_=ROWS[96:97, 1024:1152],
            axis=mybir.AxisListType.X), inc=True)
        b.wait("sync", tv)
        tcc2 = b.dma("sync", t["cch_in"][:, :], ROWS[96:97, 1153:1154], st="cc", depth=1)
        b.wait("gpsimd", tcc2)
        ar2 = b.do("gpsimd", lambda e: e.collective_compute(
            "AllReduce", ALU.max, replica_groups=[list(range(self.ncores))],
            ins=[t["cch_in"].ap().opt()], outs=[t["cch_out"].ap().opt()]), inc=True)
        b.wait("gpsimd", ar2)
        tgx2 = b.dma("gpsimd", SC[0:1, 17:18], t["cch_out"][:, :], st="ccb", depth=1)

        self.ps6_war = {6: h_ev, 7: h_ev}
        self.wdquant()

        o_ev = {}
        ev = None
        for c2 in range(8):
            for tc in range(NTC):
                bank = (c2 * NTC + tc) % 8
                evp = o_ev.get(bank)
                if evp is not None:
                    b.wait("tensor", evp)
                b.wait("tensor", h_ev)
                for bb_ in (6, 7):
                    if self.ps6_war.get(bb_) is not None:
                        b.wait("tensor", self.ps6_war[bb_])
                mm = None
                for c1 in range(8):
                    mm = b.do("tensor", lambda e, c2=c2, tc=tc, c1=c1, bank=bank,
                              st=(c1 == 0), sp=(c1 == 7): e.matmul(
                        ps[bank][:, :],
                        RE[:, c1 * 1024 + c2 * 128: c1 * 1024 + (c2 + 1) * 128],
                        RC[:, c1 * T + tc * TCW: c1 * T + (tc + 1) * TCW],
                        start=st, stop=sp), inc=(c1 == 7))
                b.wait("vector", mm)
                b.wait("vector", tbqa)
                ev = b.do("vector", lambda e, c2=c2, tc=tc, bank=bank: e.scalar_tensor_tensor(
                    out=RA[:, c2 * T + tc * TCW: c2 * T + (tc + 1) * TCW],
                    in0=ps[bank][:, :], scalar=1.0,
                    in1=SC[:, 24 + c2:25 + c2].broadcast_to((128, TCW)),
                    op0=ALU.mult, op1=ALU.add), inc=True)
                o_ev[bank] = ev
        o_done = ev

        b.wait("vector", tgx2)
        b.wait("vector", self.ld_toks["clipd"])
        tv = b.do("vector", lambda e: e.tensor_copy(out=SC[0:1, 18:19], in_=SC[0:1, 1:2]), inc=True)
        b.wait("scalar", tv)
        tsg2 = b.do("scalar", lambda e: e.activation(
            out=SC[0:1, 19:20], in_=SC[0:1, 18:19], func=AF.Sigmoid), inc=True)
        b.wait("vector", tsg2)
        tv = b.do("vector", lambda e: e.tensor_tensor(
            out=SC[0:1, 20:21], in0=SC[0:1, 19:20], in1=SC[0:1, 17:18], op=ALU.mult), inc=True)
        b.wait("vector", tv)
        tv = b.do("vector", lambda e: e.tensor_scalar(
            out=SC[0:1, 21:22], in0=SC[0:1, 20:21], scalar1=1.0 / 127.0, scalar2=None,
            op0=ALU.mult), inc=True)
        b.wait("vector", tv)
        tshr = b.do("vector", lambda e: e.reciprocal(out=SC[0:1, 22:23], in_=SC[0:1, 21:22]), inc=True)
        b.wait("tensor", tshr)
        b.wait("tensor", o_done)
        mmb = b.do("tensor", lambda e: e.matmul(
            ps[5][:, 2:3], self.ONESRF[:, :], SC[0:1, 21:22], start=True, stop=True))
        mmb = b.do("tensor", lambda e: e.matmul(
            ps[5][:, 3:4], self.ONESRF[:, :], SC[0:1, 22:23], start=True, stop=True), inc=True)
        b.wait("scalar", mmb)
        tshb = b.do("scalar", lambda e: e.copy(out=SC[:, 21:23], in_=ps[5][:, 2:4]), inc=True)
        b.wait("vector", tshb)
        b.wait("vector", self.wd_v_done)
        pre2 = None
        for m in range(8):
            tv1 = b.do("vector", lambda e, m=m: e.tensor_tensor(
                out=SWQ[:, 172 + m:173 + m], in0=SWQ[:, 44 + m:45 + m],
                in1=SC[:, 21:22], op=ALU.mult), inc=True)
            b.wait("vector", tv1)
            tv2 = b.do("vector", lambda e, m=m: e.tensor_tensor(
                out=SWQ[:, 240:241], in0=SWQ[:, 108 + m:109 + m],
                in1=SWQ[:, 172 + m:173 + m], op=ALU.mult), inc=True)
            b.wait("vector", tv2)
            pre2 = b.do("vector", lambda e, m=m: e.tensor_scalar(
                out=SWQ[:, 224 + m:225 + m], in0=SWQ[:, 240:241],
                scalar1=-128.0, scalar2=None, op0=ALU.mult), inc=True)
        b.wait("sync", o_done)
        for tok_ in hpark_toks:
            b.wait("sync", tok_)
        tok_hr = b.dma("sync", RC[:, 0:4096], t["h_park"][:, :], st="park", depth=2)

        def hq_src(i):
            if i < 20:
                return hdst(i)
            return RC[:, (i - 20) * T:(i - 19) * T]

        self.tap("tap_h", self.RD[:, :], c0=0, cols=16384)
        self.tap("tap_h", self.RB[:, :], c0=16384, cols=16384)
        self.tap("tap_h", self.RE[:, 8192:16384], c0=32768, cols=8192)
        self.tap("tap_sc", SC[:, 0:32], cols=32)
        self._hq_list = [(i, half) for i in range(NIG) for half in range(2)]
        self._hq_i = 0
        self._hq_tok = None
        _hq_first = [True]

        def hq_step(n):
            for _ in range(n):
                if self._hq_i >= len(self._hq_list):
                    return
                i, half = self._hq_list[self._hq_i]
                self._hq_i += 1
                if _hq_first[0]:
                    b.wait("vector", tok_hr)
                    b.wait("vector", pre2)
                    _hq_first[0] = False
                src_ap = hq_src(i)[:, half * 1024:(half + 1) * 1024]
                tk = self.quant_piece(src_ap, src_ap, SC[:, 22:23], out_signed=False,
                                      dep_toks=[self.wq_prev_qs] if self.wq_prev_qs else [])
                self.wq_prev_qs = tk
                self._hq_tok = tk
        self.hq_step = hq_step

        b.wait("tensor", tshb)
        b.wait("vector", self.ld_toks["pb"])
        for tok_ in self.wdq_done:
            b.wait("gpsimd", tok_)
            b.wait("sync", tok_)
        s2_ev = None
        pj_war = {}
        xr_war = {}
        s2st_war = None
        s2_park_toks = []
        for tc in range(NTC):
            for c2 in range(8):
                sl = c2 % 2
                pj = RS[:, sl * 1024: (sl + 1) * 1024]
                if (tc, c2) != (0, 0):
                    if pj_war.get(sl) is not None:
                        b.wait("gpsimd", pj_war[sl])
                tpj = b.dma("gpsimd", pj,
                            t["projw_t"].ap()[:, :].rearrange("(c p) n -> c p n", p=128)[c2],
                            st="pjl", depth=2)
                b.wait("tensor", tpj)
                b.wait("tensor", o_done)
                if s2_ev is not None and c2 == 0:
                    b.wait("tensor", s2_ev)
                mm = None
                for m in range(8):
                    mm = b.do("tensor", lambda e, tc=tc, c2=c2, m=m, pj=pj,
                              st=(c2 == 0), sp=(c2 == 7): e.matmul(
                        ps[m][:, :], pj[:, m * 128:(m + 1) * 128],
                        RA[:, c2 * T + tc * TCW: c2 * T + (tc + 1) * TCW],
                        start=st, stop=sp), inc=(m == 7))
                pj_war[sl] = mm
                hq_step(2)
            b.wait("vector", mm)
            txrs = {}
            for m in range(8):
                xsl = RS[:, 2048 + (m % 4) * 512: 2048 + (m % 4 + 1) * 512]
                if xr_war.get(m % 4) is not None:
                    b.wait("sync", xr_war[m % 4])
                txrs[m] = (xsl, b.dma("sync", xsl,
                           t["xn2_park"][:, m * T + tc * TCW: m * T + (tc + 1) * TCW],
                           st="xn2r", depth=4))
                if m % 4 != 3 and m != 7:
                    continue
                for m2 in range(m - (m % 4), m + 1):
                    xsl2, txr2 = txrs[m2]
                    b.wait("vector", txr2)
                    if m2 < 4:
                        dst = RE[:, m2 * T + tc * TCW: m2 * T + (tc + 1) * TCW]
                    else:
                        dst = RS[:, 4096:4608]
                        if s2st_war is not None:
                            b.wait("vector", s2st_war)
                    s2_ev = b.do("vector", lambda e, m2=m2, tc=tc, d=dst, x=xsl2: e.scalar_tensor_tensor(
                        out=d, in0=ps[m2][:, :], scalar=self.BIAS[:, 48 + m2:49 + m2],
                        in1=x, op0=ALU.add, op1=ALU.add), inc=True)
                    xr_war[m2 % 4] = s2_ev
                    if m2 >= 4:
                        b.wait("sync", s2_ev)
                        tpk = b.dma("sync", t["s2_park"][:, (m2 - 4) * T + tc * TCW:
                                                         (m2 - 4) * T + (tc + 1) * TCW],
                                    dst, st="s2p", depth=2)
                        s2_park_toks.append(tpk)
                        s2st_war = tpk
        s2_done = s2_ev
        self.tap("tap_s2", RE[:, 0:8192])
        hq_step(len(self._hq_list))
        hq_tok = self._hq_tok

        def g2dst(m):
            if m < 6:
                return RA[:, 4096 + m * T: 4096 + (m + 1) * T]
            return RC[:, 4096 + (m - 6) * T: 4096 + (m - 5) * T]

        dn_war = {}
        g2_ev = None
        ics = [(a, min(a + 2, NIG)) for a in range(0, NIG, 2)]
        b.wait("sync", pj_war[0])
        b.wait("sync", pj_war[1])
        for tok_ in xr_war.values():
            b.wait("sync", tok_)
        for tc in range(NTC):
            for (ic0, ic1) in ics:
                sl = (ic0 // 2) % 2
                wd = RS[:, sl * 2048: (sl + 1) * 2048]
                if dn_war.get(sl) is not None:
                    b.wait("sync", dn_war[sl])
                twd = b.dma("sync", wd[:, 0:(ic1 - ic0) * 1024],
                            t["wdq_hbm"][:, ic0 * H:ic1 * H], st="wdl", depth=2)
                b.wait("tensor", twd)
                b.wait("tensor", hq_tok)
                if g2_ev is not None and ic0 == 0:
                    b.wait("tensor", g2_ev)
                mm = None
                for ic in range(ic0, ic1):
                    for m in range(8):
                        mm = b.do("tensor", lambda e, tc=tc, ic=ic, ic0=ic0, m=m, wd=wd,
                                  st=(ic == 0), sp=(ic == NIG - 1): e.matmul(
                            ps[m][:, :], wd[:, (ic - ic0) * 1024 + m * 128:
                                            (ic - ic0) * 1024 + (m + 1) * 128],
                            hq_src(ic)[:, tc * TCW:(tc + 1) * TCW],
                            start=st, stop=sp), inc=(m == 7 and ic == ic1 - 1))
                dn_war[sl] = mm
            b.wait("scalar", mm)
            b.wait("scalar", pre2)
            for m in range(8):
                g2_ev = b.do("scalar", lambda e, tc=tc, m=m: e.activation(
                    out=g2dst(m)[:, tc * TCW:(tc + 1) * TCW], in_=ps[m][:, :],
                    func=AF.Silu, bias=SWQ[:, 224 + m:225 + m],
                    scale=SWQ[:, 172 + m:173 + m]), inc=(m == 7))
        g2_done = g2_ev
        self.tap("tap_g2", None) if False else None

        b.wait("sync", g2_done)
        for tok_ in s2_park_toks:
            b.wait("sync", tok_)
        tok_x1r = b.dma("sync", RD[:, :], t["x1_park"][:, :], st="park", depth=2)
        tok_s2r = b.dma("sync", RB[:, 0:8192], t["s2_park"][:, :], st="park", depth=2)
        OUTF = RC[:, 8192:16384].bitcast(F32)
        out_war = {}
        last_store = None
        for f in range(NF):
            s2src = (RE[:, f * T:(f + 1) * T] if f < 4
                     else RB[:, (f - 4) * T:(f - 3) * T])
            slot = f % 2
            if self.sq_war is not None:
                b.wait("vector", self.sq_war)
            b.wait("vector", g2_done)
            b.wait("vector", tok_x1r)
            b.wait("vector", tok_s2r)
            t1 = b.do("vector", lambda e, f=f, s2=s2src: e.tensor_tensor(
                out=self.SQ[:, :], in0=s2, in1=g2dst(f), op=ALU.mult), inc=True)
            b.wait("vector", t1)
            if out_war.get(slot) is not None:
                b.wait("vector", out_war[slot])
            t2 = b.do("vector", lambda e, f=f, slot=slot: e.tensor_tensor(
                out=OUTF[:, slot * 2048:(slot + 1) * 2048],
                in0=RD[:, f * T:(f + 1) * T], in1=self.SQ[:, :], op=ALU.add), inc=True)
            self.sq_war = t2
            b.wait("sync", t2)
            last_store = b.dma("sync", t["y_t"].ap().rearrange("(c p) n -> p c n", p=128)[:, f, :],
                               OUTF[:, slot * 2048:(slot + 1) * 2048], st="out", depth=2)
            out_war[slot] = last_store
            self.st_toks.append(last_store)

    def wdquant(self):
        b, t = self.b, self.t
        RS, SWQ = self.RS, self.SWQ
        wd_r = t["wd"].ap().rearrange("(c p) n -> c p n", p=128)
        ps7 = self.ps[7][:, :].bitcast(BF16)
        wdqh = t["wdq_hbm"].ap().rearrange("p (i h) -> p i h", h=H)
        stg = RS[:, 2816:3328]
        prev_in = {}
        prev_ttr = {}
        tcs = None
        ttr = None
        sts = None
        b.wait("gpsimd", self.tok_x1)
        for ic in range(NF):
            sl = 0
            inb = RS[:, 0:2816]
            if prev_in.get(sl) is not None:
                b.wait("gpsimd", prev_in[sl])
            if prev_ttr.get(sl) is not None:
                b.wait("gpsimd", prev_ttr[sl])
            tin = b.dma("gpsimd", inb, wd_r[ic], st="wdi", depth=2)
            b.wait("vector", tin)
            rmax = SWQ[:, 240:241]; rrec = SWQ[:, 241:242]; rw127 = SWQ[:, 242:243]
            tv = b.do("vector", lambda e, i=inb: e.reduce_max(
                out=rmax, in_=i, axis=mybir.AxisListType.X, apply_absolute_value=True), inc=True)
            b.wait("vector", tv)
            tv = b.do("vector", lambda e: e.reciprocal(out=rrec, in_=rmax), inc=True)
            b.wait("vector", tv)
            tv = b.do("vector", lambda e: e.tensor_scalar(
                out=rw127, in0=rrec, scalar1=127.0, scalar2=None, op0=ALU.mult), inc=True)
            b.do("vector", lambda e, ic=ic: e.tensor_scalar(
                out=SWQ[:, 44 + ic:45 + ic], in0=rmax, scalar1=1.0 / 127.0, scalar2=None,
                op0=ALU.mult))
            b.wait("vector", tv)
            qtok = None
            for pc in range(3):
                w = 1024 if pc < 2 else 768
                piece = inb[:, pc * 1024: pc * 1024 + w]
                deps = [x for x in (self.wq_prev_qs, qtok) if x is not None]
                qtok = self.quant_piece(piece, piece, rw127, out_signed=True,
                                        dep_toks=deps, width=w)
                self.wq_prev_qs = qtok
            b.wait("vector", qtok)
            tcs = b.do("vector", lambda e, ic=ic, i=inb: e.tensor_reduce(
                out=SWQ[:, 108 + ic:109 + ic], in_=i, axis=mybir.AxisListType.X,
                op=ALU.add), inc=True)
            prev_in[sl] = tcs
            b.wait("tensor", qtok)
            for g in range(6):
                icc0, icc1 = g * 4, min(g * 4 + 4, NIG)
                npc = icc1 - icc0
                if self.ps6_war.get(7) is not None:
                    b.wait("tensor", self.ps6_war[7])
                for icc in range(icc0, icc1):
                    ttr = b.do("tensor", lambda e, icc=icc, icc0=icc0, i=inb: e.transpose(
                        ps7[:, (icc - icc0) * 128:(icc - icc0 + 1) * 128],
                        i[:, icc * 128:(icc + 1) * 128], self.EYE[:, :]),
                        inc=(icc == icc1 - 1))
                prev_ttr[sl] = ttr
                b.wait("scalar", ttr)
                if sts is not None:
                    b.wait("scalar", sts)
                tse = b.do("scalar", lambda e, npc=npc: e.copy(
                    out=stg[:, 0:npc * 128], in_=ps7[:, 0:npc * 128]), inc=True)
                self.ps6_war[7] = tse
                b.wait("sync", tse)
                sts = b.dma("sync", wdqh[:, icc0:icc1, ic * 128:(ic + 1) * 128],
                            stg[:, 0:npc * 128].rearrange("p (i c) -> p i c", c=128),
                            st="wqo", depth=2)
                self.wq_store_toks.append(sts)
        self.wd_v_done = tcs
        self.wdq_done = [tcs, prev_ttr[0]] + self.wq_store_toks[-2:]


    def finish(self):
        b = self.b
        for s, v in list(b.cnt.items()):
            if v > 0:
                b.wait("sync", (s, v))


def build_program(debug=False, upto="full", ncores=NCORES, attn_js=None, attn_hps=None):
    nc = bass.Bass("TRN2", target_bir_lowering=False)
    mk = MK(nc, debug=debug, upto=upto)
    mk.ncores = ncores
    if attn_js is not None:
        mk.attn_js = attn_js
    if attn_hps is not None:
        mk.attn_hps = attn_hps
    mk.build()
    return nc, mk


def kernel(**inputs):
    from concourse.bass_utils import run_bass_kernel_spmd
    nc, m = build_program(debug=False, upto="full")
    in_maps = stage_inputs(inputs)
    r = run_bass_kernel_spmd(nc, in_maps, core_ids=list(range(NCORES)))
    out = np.zeros((NCORES, T, H), np.float32)
    for c in range(NCORES):
        out[c] = np.asarray(r.results[c]["y_t"], np.float32).T
    return out
